# version 24
# baseline (speedup 1.0000x reference)
import numpy as np
import concourse.bass as bass
import concourse.mybir as mybir
from concourse.bass_utils import run_bass_kernel_spmd

F32 = mybir.dt.float32
BF16 = mybir.dt.bfloat16
AF = mybir.ActivationFunctionType
ALU = mybir.AluOpType
AX = mybir.AxisListType.X

ENGINES = ("sync", "gpsimd", "tensor", "vector", "scalar")

TOK = 4096
GS = 512
NG = TOK // GS
D = 1024
FC = D // 128
EPS_LN = 1e-5
EPS_ATT = 1e-6


class Em:

    def __init__(self):
        self.streams = {e: [] for e in ENGINES}
        self.cnt = {}
        self.sems = {}
        self._lastwait = {}
        self.marks = {}

    def sem(self, name):
        self.cnt.setdefault(name, 0)
        return name

    def emit(self, engine, op, *args, inc=None, **kwargs):
        if inc is not None:
            self.cnt[inc[0]] += inc[1]

        def run(eng, op=op, args=args, kwargs=kwargs, inc=inc, em=self):
            r = getattr(eng, op)(*args, **kwargs)
            if inc is not None:
                r.then_inc(em.sems[inc[0]], inc[1])
        self.streams[engine].append(run)

    def wait(self, engine, sem, val=None):
        v = self.cnt.get(sem, 0) if val is None else val
        if v <= 0:
            return
        key = (engine, sem)
        if self._lastwait.get(key, -1) >= v:
            return
        self._lastwait[key] = v

        def run(eng, em=self, s=sem, v=v):
            eng.wait_ge(em.sems[s], v)
        self.streams[engine].append(run)

    def finalize(self, nc):
        import contextlib
        with contextlib.ExitStack() as st:
            for name in self.cnt:
                self.sems[name] = st.enter_context(nc.semaphore(name))
            with nc.Block() as block:
                for engname in ENGINES:
                    stream = self.streams[engname]
                    getattr(block, engname)(
                        lambda e, stream=stream: [f(e) for f in stream])


def _ln_stats(em, pb, tiles_f32, ONEB, XBC, SQS, SROW, T1, T2, rdst, qdst, epsc,
              ld_waits=(), tiles_ve=None, precise=False, rows_gp=False,
              sq_gp=False, rows_sqmean=False):
    mk = em.marks
    for s_ in ld_waits:
        em.wait("vector", s_)
    em.wait("vector", "PE", mk.get("stats_pe") if precise else None)
    em.wait("vector", "ST")
    em.wait("vector", "VE", tiles_ve if precise else None)
    for f in range(FC):
        em.emit("vector", "tensor_copy", XBC[:, f * GS:(f + 1) * GS], tiles_f32[f],
                inc=("VE", 1) if f == FC - 1 else None)
    ve_cast = em.cnt["VE"]
    if sq_gp:
        for s_ in ld_waits:
            em.wait("gpsimd", s_)
        em.wait("gpsimd", "PE", mk.get("stats_pe") if precise else None)
        em.wait("gpsimd", "VE", tiles_ve if precise else None)
        for f in range(FC):
            em.emit("gpsimd", "tensor_mul", SQS[:, f * GS:(f + 1) * GS],
                    tiles_f32[f], tiles_f32[f],
                    inc=("GP", 1) if f == FC - 1 else None)
        gp_sq = em.cnt["GP"]
        se_sq = em.cnt["SE"]
    else:
        for s_ in ld_waits:
            em.wait("scalar", s_)
        em.wait("scalar", "PE", mk.get("stats_pe") if precise else None)
        em.wait("scalar", "VE", tiles_ve if precise else None)
        for f in range(FC):
            em.emit("scalar", "activation", SQS[:, f * GS:(f + 1) * GS],
                    tiles_f32[f], AF.Square,
                    inc=("SE", 1) if f == FC - 1 else None)
        gp_sq = None
        se_sq = em.cnt["SE"]
    em.wait("tensor", "VE", max(ve_cast, mk.get("rows_ve", 0)) if precise else None)
    em.wait("tensor", "SE", se_sq if precise else None)
    if sq_gp:
        em.wait("tensor", "GP", gp_sq)
    em.wait("tensor", "LDW")
    em.wait("tensor", "LWI")
    em.wait("tensor", "LDI")
    for f in range(FC):
        em.emit("tensor", "matmul", pb[0][0:1, 0:GS], ONEB[:, 0:1],
                XBC[:, f * GS:(f + 1) * GS], start=(f == 0), stop=(f == FC - 1))
    for f in range(FC):
        em.emit("tensor", "matmul", pb[1][0:1, 0:GS], ONEB[:, 0:1],
                SQS[:, f * GS:(f + 1) * GS], start=(f == 0), stop=(f == FC - 1),
                inc=("PE", 1) if f == FC - 1 else None)
    mk["stats_pe"] = em.cnt["PE"]
    em.wait("vector", "PE", mk["stats_pe"] if precise else None)
    em.wait("vector", "VE", mk.get("rows_ve", 0) if precise else None)
    if rows_gp:
        em.wait("vector", "SE", mk.get("rows_se", 0))
        em.wait("vector", "GP", mk.get("rows_gp", 0))
    em.emit("vector", "tensor_scalar", SROW[0:1, 0:GS], pb[0][0:1, 0:GS],
            1.0 / D, None, ALU.mult)
    em.emit("vector", "tensor_scalar", SROW[0:1, GS:2 * GS], pb[1][0:1, 0:GS],
            1.0 / D, None, ALU.mult, inc=("VE", 1))
    mk["rows_ve"] = em.cnt["VE"]
    if rows_gp:
        em.wait("scalar", "GP", mk.get("rows_gp", 0))
        em.wait("scalar", "PE", mk.get("rows_cons_pe"))
        em.wait("scalar", "ST")
        em.wait("scalar", "VE")
        em.emit("scalar", "activation", T1[0:1, :], SROW[0:1, 0:GS], AF.Square,
                inc=("SE", 1))
        em.wait("gpsimd", "SE")
        em.wait("gpsimd", "VE")
        em.emit("gpsimd", "tensor_sub", T2[0:1, :], SROW[0:1, GS:2 * GS],
                T1[0:1, :], inc=("GP", 1))
        if rows_sqmean:
            em.wait("scalar", "GP")
            em.wait("scalar", "ST")
            em.emit("scalar", "activation", rdst, T2[0:1, :], AF.Sqrt,
                    bias=epsc[0:1, 0:1], inc=("SE", 1))
            em.wait("gpsimd", "PE", mk.get("rows_cons_pe"))
            em.wait("gpsimd", "ST")
            em.emit("gpsimd", "tensor_copy", qdst, SROW[0:1, 0:GS],
                    inc=("GP", 1))
            mk["rows_se"] = em.cnt["SE"]
            mk["rows_gp"] = em.cnt["GP"]
            mk["rows_ve"] = em.cnt["VE"]
            return
        em.wait("scalar", "GP")
        em.emit("scalar", "activation", T1[0:1, :], T2[0:1, :], AF.Sqrt,
                bias=epsc[0:1, 0:1], inc=("SE", 1))
        em.wait("vector", "SE")
        em.wait("vector", "PE", mk.get("rows_cons_pe"))
        em.emit("vector", "reciprocal", rdst, T1[0:1, :], inc=("VE", 1))
        em.wait("gpsimd", "VE")
        em.wait("gpsimd", "PE", mk.get("rows_cons_pe"))
        em.wait("gpsimd", "ST")
        em.emit("gpsimd", "tensor_mul", qdst, SROW[0:1, 0:GS], rdst,
                inc=("GP", 1))
        mk["rows_se"] = em.cnt["SE"]
        mk["rows_gp"] = em.cnt["GP"]
        mk["rows_ve"] = em.cnt["VE"]
        return
    em.wait("scalar", "VE")
    em.emit("scalar", "activation", T1[0:1, :], SROW[0:1, 0:GS], AF.Square,
            inc=("SE", 1))
    em.wait("vector", "SE")
    em.emit("vector", "tensor_sub", T2[0:1, :], SROW[0:1, GS:2 * GS], T1[0:1, :],
            inc=("VE", 1))
    if rows_sqmean:
        em.wait("scalar", "VE")
        em.wait("scalar", "ST")
        em.emit("scalar", "activation", rdst, T2[0:1, :], AF.Sqrt,
                bias=epsc[0:1, 0:1], inc=("SE", 1))
        em.wait("vector", "ST")
        em.emit("vector", "tensor_copy", qdst, SROW[0:1, 0:GS], inc=("VE", 1))
        mk["rows_ve"] = em.cnt["VE"]
        return
    em.wait("scalar", "VE")
    em.emit("scalar", "activation", T1[0:1, :], T2[0:1, :], AF.Sqrt,
            bias=epsc[0:1, 0:1], inc=("SE", 1))
    em.wait("vector", "SE")
    em.wait("vector", "PE", mk.get("rows_cons_pe") if precise else None)
    em.emit("vector", "reciprocal", rdst, T1[0:1, :], inc=("VE", 1))
    em.wait("vector", "VE")
    em.emit("vector", "tensor_mul", qdst, SROW[0:1, 0:GS], rdst, inc=("VE", 1))
    mk["rows_ve"] = em.cnt["VE"]


class Ar:

    def __init__(self, nc, nbytes=212800):
        self.nbytes = nbytes
        self.t = nc.alloc_sbuf_tensor("ARENA", [128, nbytes], mybir.dt.uint8)
        self.off = 0

    def alloc(self, name, shape, dtype):
        esz = mybir.dt.size(dtype)
        nb = shape[1] * esz
        off = (self.off + 63) & ~63
        self.off = off + nb
        assert self.off <= self.nbytes, (
            f"arena overflow at {name}: {self.off} > {self.nbytes}")
        view = self.t[:].bitcast(dtype)
        return view[0:shape[0], off // esz:(off // esz) + shape[1]]

    def reset(self):
        self.off = 0


def _mk_drams(nc, spec):
    dio = {}
    for name, (shape, dtype, kind) in spec.items():
        if kind == "in":
            dio[name] = nc.dram_tensor(name, shape, dtype, kind="ExternalInput").ap()
        elif kind == "out":
            dio[name] = nc.dram_tensor(name, shape, dtype, kind="ExternalOutput").ap()
        else:
            dio[name] = nc.dram_tensor(name, shape, dtype, kind="Internal",
                                       addr_space="Local").ap()
    return dio


def _standalone(spec):
    nc = bass.Bass("TRN2", target_bir_lowering=False, debug=False)
    dio = _mk_drams(nc, spec)
    ar = Ar(nc)
    pb = [nc.alloc_psum_tensor(f"pb{i}", [128, 512], F32) for i in range(8)]
    em = Em()
    for s in ALL_SEMS:
        em.sem(s)
    return nc, em, ar, dio, pb


ALL_SEMS = ("LDI", "LDW", "LWI", "LDR", "LXP", "LX0", "LX1", "LW0", "LW1",
            "LW2", "LB0", "LB1", "LB2", "LV0", "LV1", "LV2", "R30", "R31",
            "LNU", "SPQ", "ST", "GD", "CC", "PE", "VE", "SE", "GP",
            "LR0", "LR1", "SP0", "SP1", "NU0", "NU1", "LWO")


def A_DRAMS(n_cores=8):
    return {
        "x0": ([D, TOK], F32, "in"), "wrt": ([D, D], F32, "in"),
        "wft": ([D, D], F32, "in"), "attf": ([1, D], F32, "in"),
        "gam": ([128, FC], F32, "in"), "bet": ([128, FC], F32, "in"),
        "idn": ([128, 128], F32, "in"), "one": ([128, 128], F32, "in"),
        "msf": ([16, 8], F32, "in"), "mpt": ([16, 8], F32, "in"),
        "x1": ([D, TOK], F32, "out"),
        "cc_in_a": ([8, 136], F32, "int"), "cc_out_a": ([16, 136], F32, "int"),
        "rows_a": ([2, TOK], F32, "int"),
    }


def B_DRAMS():
    return {
        "x1": ([D, TOK], F32, "in"),
        "w1t": ([32 * 128, 1024], F32, "in"), "w2t": ([32 * 128, 1024], F32, "in"),
        "h1t": ([32 * 128, 4096], F32, "in"), "h2t": ([32 * 128, 4096], F32, "in"),
        "w3t": ([8 * 128, 4096], F32, "in"),
        "s1c": ([128, 32], F32, "in"), "s2c": ([128, 32], F32, "in"),
        "sh1c": ([128, 32], F32, "in"), "sh2c": ([128, 32], F32, "in"),
        "sw3c": ([128, 8], F32, "in"), "one": ([128, 128], F32, "in"),
        "x2": ([D, TOK], F32, "out"), "rows_b": ([2, TOK], F32, "int"),
    }


def C_DRAMS(n_cores=8):
    return {
        "x2": ([D, TOK], F32, "in"), "wqkv": ([D, 3 * D], F32, "in"),
        "wout": ([D, D], F32, "in"), "utm": ([128, 128], F32, "in"),
        "idn": ([128, 128], F32, "in"), "idlo": ([128, 64], F32, "in"),
        "one": ([128, 128], F32, "in"), "mkv": ([128, 64], F32, "in"),
        "sel16": ([16, 16 * 64], F32, "in"),
        "x3": ([D, TOK], F32, "out"),
        "rows_c": ([2, TOK], F32, "int"),
        "pq_sc": ([64, NH * TOK], BF16, "int"),
        "nu_sc": ([64, NH * TOK], BF16, "int"),
        "kv_in": ([64, 1056], F32, "int"), "kv_out": ([128, 1056], F32, "int"),
    }


def D_DRAMS():
    return {
        "x3": ([D, TOK], F32, "in"), "rwt": ([D, 32], F32, "in"),
        "rb": ([32, 1], F32, "in"), "w1e": ([32 * 128, 2048], F32, "in"),
        "b1t": ([128, 64], F32, "in"), "w2e": ([32 * 128, 2048], F32, "in"),
        "b2": ([32, 1024], F32, "in"), "sel32": ([32, 32 * 128], F32, "in"),
        "idn": ([128, 128], F32, "in"), "one": ([128, 128], F32, "in"),
        "x4": ([TOK, D], F32, "out"), "rows_d4": ([2, TOK], F32, "int"),
    }



def emit_A(nc, em, ar, dio, pb, n_cores=8, rows_next=None):
    RG = [[a, a + 1] for a in range(0, n_cores, 2)]
    x0_d = dio["x0"]; wrt_d = dio["wrt"]; wft_d = dio["wft"]
    attf_d = dio["attf"]; gam_d = dio["gam"]; bet_d = dio["bet"]
    idn_d = dio["idn"]; one_d = dio["one"]; msf_d = dio["msf"]
    mpt_d = dio["mpt"]; x1_d = dio["x1"]
    cc_in = dio["cc_in_a"]; cc_out = dio["cc_out_a"]; rows_d = dio["rows_a"]

    sb = ar.alloc
    WRT = sb("WRT", [128, FC * D], BF16)
    WFT = sb("WFT", [128, FC * D], BF16)
    XT = sb("XT", [128, 2 * FC * GS], F32)
    XBC = sb("XBC", [128, FC * GS], BF16)
    SQS = sb("SQS", [128, FC * GS], BF16)
    TTS = sb("TTS", [128, FC * GS], F32)
    HBB = sb("HBB", [128, FC * GS], BF16)
    FRS = sb("FRS", [128, D], F32)
    LKS = sb("LKS", [128, D], F32)
    AAS = sb("AAS", [128, D], F32)
    SRT = sb("SRT", [128, 32 * 8], F32)
    FTM = sb("FTM", [128, 32 * D], BF16)
    SRW = sb("SRW", [8, 256], F32)
    SRW2 = sb("SRW2", [8, 256], F32)
    SMAX = sb("SMAX", [8, 32], F32)
    SSUM = sb("SSUM", [8, 32], F32)
    SM = sb("SM", [8, 16], F32)
    WCOL = sb("WCOL", [128, 32 * 8], BF16)
    RR = sb("RR", [1, GS], F32)
    QQ = sb("QQ", [1, GS], F32)
    T1 = sb("T1", [1, GS], F32)
    T2 = sb("T2", [1, GS], F32)
    SROW = sb("SROW", [1, 2 * GS], F32)
    ATTB = sb("ATTB", [128, D], F32)
    ONEF = sb("ONEF", [128, 128], F32)
    ONEB = sb("ONEB", [128, 128], BF16)
    IDF = sb("IDF", [128, 128], F32)
    IDB = sb("IDB", [128, 128], BF16)
    GAMS = sb("GAMS", [128, FC], F32)
    BETS = sb("BETS", [128, FC], F32)
    MSF = sb("MSF", [16, 8], F32)
    MPT = sb("MPT", [16, 8], F32)
    PKT = sb("PKT", [8, 136], F32)
    GSB = sb("GSB", [16, 136], F32)
    ABS = sb("ABS", [8, 272], F32)
    CMB = sb("CMB", [8, 160], F32)
    CTXB = sb("CTXB", [8, 128], BF16)
    CTXC = sb("CTXC", [128, 8], BF16)
    FHC = sb("FHC", [128, FC], F32)
    VF = sb("VF", [128, 8], F32)
    CST = sb("CST", [128, 4], F32)
    VR = sb("VR", [8, 128], F32)
    YTv = FTM[:].bitcast(F32)

    def YTb(buf, f):
        return YTv[:, (buf * FC + f) * GS:(buf * FC + f + 1) * GS]

    def X(buf, f):
        return XT[:, (buf * FC + f) * GS:(buf * FC + f + 1) * GS]

    for ic in range(FC):
        em.emit("gpsimd", "dma_start", out=WRT[:, ic * D:(ic + 1) * D],
                in_=wrt_d[ic * 128:(ic + 1) * 128, :], inc=("LDW", 16))
        em.emit("gpsimd", "dma_start", out=WFT[:, ic * D:(ic + 1) * D],
                in_=wft_d[ic * 128:(ic + 1) * 128, :], inc=("LDW", 16))
    em.emit("gpsimd", "dma_start", out=ONEB[:], in_=one_d[:], inc=("LDW", 16))
    em.emit("gpsimd", "dma_start", out=IDB[:], in_=idn_d[:], inc=("LDW", 16))
    for dst, src in ((ONEF, one_d), (IDF, idn_d), (GAMS, gam_d), (BETS, bet_d),
                     (MSF, msf_d), (MPT, mpt_d)):
        em.emit("sync", "dma_start", out=dst[:], in_=src[:], inc=("LDI", 16))
    em.emit("sync", "dma_start", out=SROW[0:1, 0:D], in_=attf_d[:], inc=("LDI", 16))
    em.wait("tensor", "LDI")
    em.emit("tensor", "matmul", pb[2][:], ONEF[0:1, :], SROW[0:1, 0:512],
            start=True, stop=True)
    em.emit("tensor", "matmul", pb[3][:], ONEF[0:1, :], SROW[0:1, 512:1024],
            start=True, stop=True, inc=("PE", 1))
    em.wait("vector", "PE")
    em.emit("vector", "memset", CST[:, 0:1], EPS_LN)
    em.emit("vector", "tensor_copy", ATTB[:, 0:512], pb[2][:])
    em.emit("vector", "tensor_copy", ATTB[:, 512:1024], pb[3][:], inc=("VE", 1))

    war_x = [0, 0]
    fr_se = [0, 0]

    def _load_x0(g_):
        b_ = g_ % 2
        em.wait("sync", "VE", war_x[b_])
        for f in range(FC):
            em.emit("sync", "dma_start", out=X(b_, f),
                    in_=x0_d[f * 128:(f + 1) * 128, g_ * GS:(g_ + 1) * GS],
                    inc=("LX%d" % b_, 16))

    _load_x0(0)
    for g in range(NG):
        buf = g % 2
        if g + 1 < NG:
            _load_x0(g + 1)
        tiles = [X(buf, f) for f in range(FC)]
        _ln_stats(em, pb, tiles, ONEB, XBC, SQS, SROW, T1, T2,
                  RR[0:1, :], QQ[0:1, :], CST, ld_waits=("LX%d" % buf,),
                  tiles_ve=0, precise=True, rows_gp=True)
        em.wait("tensor", "VE")
        em.wait("tensor", "GP", em.marks["rows_gp"])
        em.emit("tensor", "matmul", pb[2][:], ONEF[0:1, :], RR[0:1, :],
                start=True, stop=True)
        em.emit("tensor", "matmul", pb[3][:], ONEF[0:1, :], QQ[0:1, :],
                start=True, stop=True, inc=("PE", 1))
        em.marks["rows_cons_pe"] = em.cnt["PE"]
        em.wait("sync", "VE")
        em.wait("sync", "GP", em.marks["rows_gp"])
        em.emit("sync", "dma_start", out=rows_d[0:1, g * GS:(g + 1) * GS],
                in_=RR[0:1, :], inc=("ST", 16))
        em.emit("sync", "dma_start", out=rows_d[1:2, g * GS:(g + 1) * GS],
                in_=QQ[0:1, :], inc=("ST", 16))
        em.wait("vector", "PE")
        for f in range(FC):
            em.emit("vector", "tensor_mul", TTS[:, f * GS:(f + 1) * GS],
                    X(buf, f), pb[2][:], inc=("VE", 1) if f == FC - 1 else None)
        em.wait("vector", "VE")
        for f in range(FC):
            em.emit("vector", "tensor_sub", HBB[:, f * GS:(f + 1) * GS],
                    TTS[:, f * GS:(f + 1) * GS], pb[3][:],
                    inc=("VE", 1) if f == FC - 1 else None)
        war_x[buf] = em.cnt["VE"]
        ve_hbb = em.cnt["VE"]
        for s in range(4):
            c = g * 4 + s
            bk = 4 if s % 2 == 0 else 6
            em.wait("tensor", "VE", ve_hbb)
            em.wait("tensor", "SE", fr_se[s % 2])
            for ic in range(FC):
                for half in range(2):
                    em.emit("tensor", "matmul", pb[bk + half][:],
                            HBB[:, ic * GS + s * 128: ic * GS + (s + 1) * 128],
                            WRT[:, ic * D + half * 512: ic * D + (half + 1) * 512],
                            start=(ic == 0), stop=(ic == FC - 1),
                            inc=("PE", 1) if (half == 1 and ic == FC - 1) else None)
            em.wait("scalar", "PE")
            em.wait("scalar", "VE")
            for half in range(2):
                em.emit("scalar", "activation", FRS[:, half * 512:(half + 1) * 512],
                        pb[bk + half][:], AF.Copy, inc=("SE", 1) if half == 1 else None)
            fr_se[s % 2] = em.cnt["SE"]
            em.wait("vector", "SE")
            em.emit("vector", "tensor_copy", FTM[:, c * D:c * D + 512], FRS[:, 0:512])
            em.emit("vector", "tensor_copy", FTM[:, c * D + 512:(c + 1) * D],
                    FRS[:, 512:1024])
            em.emit("vector", "scalar_tensor_tensor", LKS[:, 0:512], FRS[:, 0:512],
                    0.01, FRS[:, 0:512], ALU.mult, ALU.max)
            em.emit("vector", "scalar_tensor_tensor", LKS[:, 512:1024],
                    FRS[:, 512:1024], 0.01, FRS[:, 512:1024], ALU.mult, ALU.max,
                    inc=("VE", 1))
            em.wait("gpsimd", "VE")
            em.emit("gpsimd", "tensor_mul", AAS[:, 0:512], LKS[:, 0:512],
                    ATTB[:, 0:512])
            em.emit("gpsimd", "tensor_mul", AAS[:, 512:1024], LKS[:, 512:1024],
                    ATTB[:, 512:1024], inc=("GP", 1))
            em.wait("vector", "GP")
            em.emit("vector", "tensor_reduce", SRT[:, c * 8:(c + 1) * 8],
                    AAS[:].rearrange("p (s c) -> p s c", c=128), AX, ALU.add,
                    inc=("VE", 1))

    a2_ve = [0, 0]
    for c in range(32):
        bkp = c % 2
        em.wait("tensor", "VE", a2_ve[bkp])
        em.emit("tensor", "transpose", pb[bkp][0:8, 0:128],
                SRT[:, c * 8:(c + 1) * 8], IDF[:], inc=("PE", 1))
        em.wait("vector", "PE")
        em.emit("vector", "tensor_reduce", SMAX[:, c:c + 1], pb[bkp][0:8, 0:128],
                AX, ALU.max, inc=("VE", 1))
        a2_ve[bkp] = em.cnt["VE"]
    em.wait("vector", "VE")
    em.emit("vector", "tensor_reduce", SM[:, 0:1], SMAX[:, 0:32], AX, ALU.max,
            inc=("VE", 1))
    m_ve = em.cnt["VE"]
    a2b_mk = [0, 0]
    for c in range(32):
        bkp = c % 2
        sl = bkp * 128
        em.wait("tensor", "VE", max(a2_ve[bkp], a2b_mk[bkp]))
        em.wait("tensor", "SE", em.marks.get("a2_se_%d" % bkp, 0))
        em.emit("tensor", "transpose", pb[bkp][0:8, 0:128],
                SRT[:, c * 8:(c + 1) * 8], IDF[:], inc=("PE", 1))
        em.wait("vector", "PE")
        em.wait("vector", "VE", m_ve)
        em.emit("vector", "tensor_scalar", SRW[:, sl:sl + 128], pb[bkp][0:8, 0:128],
                SM[:, 0:1], None, ALU.subtract, inc=("VE", 1))
        em.wait("scalar", "VE")
        em.emit("scalar", "activation", SRW2[:, sl:sl + 128], SRW[:, sl:sl + 128],
                AF.Exp, inc=("SE", 1))
        em.marks["a2_se_%d" % bkp] = em.cnt["SE"]
        em.wait("vector", "SE")
        em.emit("vector", "tensor_reduce", SSUM[:, c:c + 1], SRW2[:, sl:sl + 128],
                AX, ALU.add, inc=("VE", 1))
        em.wait("tensor", "SE")
        em.wait("tensor", "VE", a2b_mk[bkp])
        em.emit("tensor", "transpose", pb[2 + bkp][0:128, 0:8],
                SRW2[:, sl:sl + 128], IDF[0:8, 0:8], inc=("PE", 1))
        em.wait("vector", "PE")
        em.emit("vector", "tensor_copy", WCOL[:, c * 8:(c + 1) * 8],
                pb[2 + bkp][0:128, 0:8], inc=("VE", 1))
        a2b_mk[bkp] = em.cnt["VE"]
    em.wait("vector", "VE")
    em.emit("vector", "tensor_reduce", SM[:, 1:2], SSUM[:, 0:32], AX, ALU.add,
            inc=("VE", 1))
    em.wait("tensor", "VE")
    for c in range(32):
        for h in range(8):
            em.emit("tensor", "matmul", pb[7][:, h:h + 1],
                    FTM[:, c * D + h * 128: c * D + (h + 1) * 128],
                    WCOL[:, c * 8 + h: c * 8 + h + 1],
                    start=(c == 0 and h == 0), stop=(c == 31 and h == 7),
                    inc=("PE", 1) if (c == 31 and h == 7) else None)
    em.wait("vector", "PE")
    em.emit("vector", "tensor_copy", VF[:], pb[7][:, 0:8], inc=("VE", 1))
    em.wait("tensor", "VE")
    em.emit("tensor", "transpose", pb[0][0:8, 0:128], VF[:], IDF[:], inc=("PE", 1))
    em.wait("vector", "PE")
    em.emit("vector", "tensor_copy", VR[:], pb[0][0:8, 0:128], inc=("VE", 1))
    em.wait("vector", "VE")
    em.emit("vector", "memset", PKT[:], 0.0, inc=("VE", 1))
    em.wait("vector", "VE")
    em.emit("vector", "tensor_copy", PKT[:, 0:2], SM[:, 0:2])
    em.emit("vector", "tensor_copy", PKT[:, 2:130], VR[:], inc=("VE", 1))
    em.wait("sync", "VE")
    em.emit("sync", "dma_start", out=cc_in[:], in_=PKT[:], inc=("ST", 16))
    em.wait("gpsimd", "ST")
    em.emit("gpsimd", "collective_compute", "AllGather", ALU.bypass,
            replica_groups=RG, ins=[cc_in[:]], outs=[cc_out[:]], inc=("CC", 1))
    em.wait("gpsimd", "CC")
    em.emit("gpsimd", "dma_start", out=GSB[:], in_=cc_out[:], inc=("GD", 16))
    em.wait("tensor", "GD")
    em.wait("tensor", "LDI")
    em.emit("tensor", "matmul", pb[1][0:8, 0:136], MSF[:], GSB[:],
            start=True, stop=False)
    em.emit("tensor", "matmul", pb[1][0:8, 136:272], MPT[:], GSB[:],
            start=False, stop=True, inc=("PE", 1))
    em.wait("vector", "PE")
    em.emit("vector", "tensor_copy", ABS[:], pb[1][0:8, 0:272], inc=("VE", 1))
    em.wait("vector", "VE")
    em.emit("vector", "tensor_tensor", CMB[:, 0:1], ABS[:, 0:1], ABS[:, 136:137],
            ALU.max, inc=("VE", 1))
    em.wait("vector", "VE")
    em.emit("vector", "tensor_sub", CMB[:, 1:2], ABS[:, 0:1], CMB[:, 0:1])
    em.emit("vector", "tensor_sub", CMB[:, 2:3], ABS[:, 136:137], CMB[:, 0:1],
            inc=("VE", 1))
    em.wait("scalar", "VE")
    em.emit("scalar", "activation", CMB[:, 1:3], CMB[:, 1:3], AF.Exp, inc=("SE", 1))
    em.wait("vector", "SE")
    em.emit("vector", "tensor_scalar", CMB[:, 3:132], ABS[:, 1:130], CMB[:, 1:2],
            None, ALU.mult, inc=("VE", 1))
    em.wait("vector", "VE")
    em.emit("vector", "scalar_tensor_tensor", CMB[:, 3:132], ABS[:, 137:266],
            CMB[:, 2:3], CMB[:, 3:132], ALU.mult, ALU.add, inc=("VE", 1))
    em.wait("vector", "VE")
    em.emit("vector", "reciprocal", CMB[:, 132:133], CMB[:, 3:4], inc=("VE", 1))
    em.wait("vector", "VE")
    em.emit("vector", "tensor_scalar", CTXB[:], CMB[:, 4:132], CMB[:, 132:133],
            None, ALU.mult, inc=("VE", 1))
    em.wait("tensor", "VE")
    em.emit("tensor", "transpose", pb[0][:].bitcast(BF16)[0:128, 0:8], CTXB[:],
            IDB[0:8, 0:8], inc=("PE", 1))
    em.wait("vector", "PE")
    em.emit("vector", "tensor_copy", CTXC[:], pb[0][:].bitcast(BF16)[0:128, 0:8],
            inc=("VE", 1))
    em.wait("tensor", "VE")
    for j in range(FC):
        for h in range(FC):
            em.emit("tensor", "matmul", pb[2][:, j:j + 1],
                    WFT[:, h * D + j * 128: h * D + (j + 1) * 128], CTXC[:, h:h + 1],
                    start=(j == 0 and h == 0), stop=(j == FC - 1 and h == FC - 1),
                    inc=("PE", 1) if (j == FC - 1 and h == FC - 1) else None)
    em.wait("vector", "PE")
    em.emit("vector", "tensor_copy", FHC[:], pb[2][:, 0:8], inc=("VE", 1))

    a3mk = {"b45": 0, "b23": 0, "rows": 0, "yt0": 0, "yt0se": 0,
            "yt1": 0, "yt1se": 0, "yt0gp": 0, "yt1gp": 0}

    def RL(b_):
        return (FRS if b_ == 0 else LKS)[0:1, 0:2 * GS]

    def _load_a3(g_):
        b_ = g_ % 2
        em.wait("sync", "VE", war_x[b_])
        for f in range(FC):
            em.emit("sync", "dma_start", out=X(b_, f),
                    in_=x0_d[f * 128:(f + 1) * 128, g_ * GS:(g_ + 1) * GS],
                    inc=("LX%d" % b_, 16))
        em.wait("sync", "PE", a3mk["rows"])
        em.wait("sync", "ST")
        em.emit("sync", "dma_start", out=RL(b_)[0:1, 0:GS],
                in_=rows_d[0:1, g_ * GS:(g_ + 1) * GS], inc=("LR%d" % b_, 16))
        em.emit("sync", "dma_start", out=RL(b_)[0:1, GS:2 * GS],
                in_=rows_d[1:2, g_ * GS:(g_ + 1) * GS], inc=("LR%d" % b_, 16))

    TTS2 = YTv[:, 16 * GS:24 * GS]
    HBF = HBB[:].bitcast(F32)

    def T3(b_):
        return (AAS if b_ == 0 else HBF)[0:1, 0:GS]

    def T4(b_):
        return (AAS if b_ == 0 else HBF)[0:1, GS:2 * GS]

    r1mk = {}
    yvk = {}

    def a3_h1(k):
        b_ = k % 2
        if k == 1:
            em.wait("sync", "VE")
            em.wait("sync", "SE")
            em.wait("sync", "GP")
        _load_a3(k)
        em.wait("tensor", "VE", a3mk["b45"])
        em.wait("tensor", "LR%d" % b_)
        em.emit("tensor", "matmul", pb[4][:], ONEF[0:1, :], RL(b_)[0:1, 0:GS],
                start=True, stop=True)
        em.emit("tensor", "matmul", pb[5][:], ONEF[0:1, :],
                RL(b_)[0:1, GS:2 * GS], start=True, stop=True, inc=("PE", 1))
        a3mk["rows"] = em.cnt["PE"]
        em.marks["rows_cons_pe"] = em.cnt["PE"]
        em.wait("vector", "PE", a3mk["rows"])
        em.wait("vector", "LX%d" % b_)
        em.wait("vector", "VE", a3mk["b45"])
        for f in range(FC):
            em.emit("vector", "tensor_mul", TTS[:, f * GS:(f + 1) * GS],
                    X(b_, f), pb[4][:], inc=("VE", 1) if f == FC - 1 else None)
        em.wait("vector", "VE")
        em.wait("vector", "SE", a3mk["yt%dse" % b_])
        em.wait("vector", "GP", a3mk["yt%dgp" % b_])
        em.wait("vector", "ST")
        for f in range(FC):
            em.emit("vector", "scalar_tensor_tensor", YTb(b_, f),
                    TTS[:, f * GS:(f + 1) * GS], FHC[:, f:f + 1], pb[5][:],
                    ALU.add, ALU.subtract, inc=("VE", 1) if f == FC - 1 else None)
        a3mk["b45"] = em.cnt["VE"]
        yvk[k] = em.cnt["VE"]

    def a3_h2(k):
        b_ = k % 2
        yt = [YTb(b_, f) for f in range(FC)]
        _ln_stats(em, pb, yt, ONEB, XBC, SQS, SROW, T1, T2, T3(b_), T4(b_),
                  CST, tiles_ve=yvk.pop(k), precise=True, rows_gp=True,
                  sq_gp=True)
        r1mk[k] = (em.marks["rows_se"], em.marks["rows_gp"], em.cnt["VE"])

    def a3_t1(g):
        buf = g % 2
        se1, gp1, ve1 = r1mk.pop(g)
        em.wait("tensor", "VE", max(a3mk["nrm"], ve1))
        em.wait("tensor", "SE", se1)
        em.wait("tensor", "GP", gp1)
        em.emit("tensor", "matmul", pb[2][:], ONEF[0:1, :], T3(buf),
                start=True, stop=True)
        em.emit("tensor", "matmul", pb[3][:], ONEF[0:1, :], T4(buf),
                start=True, stop=True, inc=("PE", 1))
        em.marks["rows_cons_pe"] = em.cnt["PE"]
        b23pe = em.cnt["PE"]
        em.wait("vector", "PE", b23pe)
        em.wait("vector", "VE", a3mk.get("addv", 0))
        for f in range(FC):
            em.emit("vector", "tensor_mul", TTS2[:, f * GS:(f + 1) * GS],
                    YTb(buf, f), pb[2][:], inc=("VE", 1) if f == FC - 1 else None)
        em.wait("vector", "VE")
        for f in range(FC):
            em.emit("vector", "tensor_sub", TTS2[:, f * GS:(f + 1) * GS],
                    TTS2[:, f * GS:(f + 1) * GS], pb[3][:],
                    inc=("VE", 1) if f == FC - 1 else None)
        a3mk["nrm"] = em.cnt["VE"]
        em.wait("vector", "VE")
        for f in range(FC):
            em.emit("vector", "tensor_scalar", TTS2[:, f * GS:(f + 1) * GS],
                    TTS2[:, f * GS:(f + 1) * GS], GAMS[:, f:f + 1],
                    BETS[:, f:f + 1], ALU.mult, ALU.add,
                    inc=("VE", 1) if f == FC - 1 else None)
        em.wait("vector", "VE")
        for f in range(FC):
            em.emit("vector", "tensor_add", YTb(buf, f),
                    TTS2[:, f * GS:(f + 1) * GS], X(buf, f),
                    inc=("VE", 1) if f == FC - 1 else None)
        war_x[buf] = em.cnt["VE"]
        x1_ve = em.cnt["VE"]
        a3mk["addv"] = x1_ve
        a3mk["x1ve"] = x1_ve
        em.wait("sync", "VE", x1_ve)
        for f in range(FC):
            em.emit("sync", "dma_start",
                    out=x1_d[f * 128:(f + 1) * 128, g * GS:(g + 1) * GS],
                    in_=YTb(buf, f), inc=("ST", 16))

    def a3_t2(g):
        buf = g % 2
        x1_ve = a3mk["x1ve"]
        if rows_next is not None:
            yt2 = [YTb(buf, f) for f in range(FC)]
            _ln_stats(em, pb, yt2, ONEB, XBC, SQS, SROW, T1, T2,
                      RR[0:1, :], QQ[0:1, :], CST, tiles_ve=x1_ve, precise=True,
                      rows_gp=True, rows_sqmean=True)
            a3mk["yt%dse" % buf] = em.cnt["SE"]
            a3mk["yt%dgp" % buf] = em.cnt["GP"]
            em.wait("sync", "VE", em.marks["rows_ve"])
            em.wait("sync", "SE", em.marks["rows_se"])
            em.wait("sync", "GP", em.marks["rows_gp"])
            em.emit("sync", "dma_start", out=rows_next[0:1, g * GS:(g + 1) * GS],
                    in_=RR[0:1, :], inc=("ST", 16))
            em.emit("sync", "dma_start", out=rows_next[1:2, g * GS:(g + 1) * GS],
                    in_=QQ[0:1, :], inc=("ST", 16))
        else:
            a3mk["yt%dse" % buf] = em.cnt["SE"]
            a3mk["yt%dgp" % buf] = em.cnt["GP"]

    em.wait("sync", "VE")
    em.wait("sync", "SE")
    em.wait("vector", "GP")
    a3mk["nrm"] = 0
    a3_h1(0)
    a3_h2(0)
    for g in range(NG):
        if g + 1 < NG:
            a3_h1(g + 1)
        a3_t1(g)
        if g + 1 < NG:
            a3_h2(g + 1)
        a3_t2(g)

    em.wait("sync", "ST")


def build_A(n_cores=8):
    nc, em, ar, dio, pb = _standalone(A_DRAMS(n_cores))
    emit_A(nc, em, ar, dio, pb, n_cores=n_cores)
    em.finalize(nc)
    return nc



def _sel_mats(n_cores):
    msf, mpt = [], []
    for c in range(n_cores):
        a = np.zeros((16, 8), np.float32)
        b = np.zeros((16, 8), np.float32)
        for h in range(8):
            a[(c % 2) * 8 + h, h] = 1.0
            b[((c % 2) ^ 1) * 8 + h, h] = 1.0
        msf.append(a)
        mpt.append(b)
    return msf, mpt


def _prep_A(inputs, n_cores=8):
    f32 = np.float32
    wrt = np.ascontiguousarray(np.asarray(inputs["L0_attn_Wr"], f32).T)
    wft = np.ascontiguousarray(np.asarray(inputs["L0_attn_Wf"], f32).T)
    attf = np.ascontiguousarray(
        np.tile(np.asarray(inputs["L0_attn_att_r"], f32), (1, 8)))
    gam = np.ascontiguousarray(
        np.asarray(inputs["L0_attn_ln_gamma"], f32).reshape(FC, 128).T)
    bet = np.ascontiguousarray(
        np.asarray(inputs["L0_attn_ln_beta"], f32).reshape(FC, 128).T)
    idn = np.eye(128, dtype=f32)
    one = np.ones((128, 128), f32)
    msf, mpt = _sel_mats(n_cores)
    x0 = np.asarray(inputs["L0_attn_h"], f32)
    maps = []
    for c in range(n_cores):
        b, hh = c // 2, c % 2
        xs = np.ascontiguousarray(x0[b, hh * TOK:(hh + 1) * TOK, :].T)
        maps.append(dict(x0=xs, wrt=wrt, wft=wft, attf=attf, gam=gam, bet=bet,
                         idn=idn, one=one, msf=msf[c], mpt=mpt[c]))
    return maps





def emit_B(nc, em, ar, dio, pb, skip_prepass=False, rows_next=None):
    x1_d = dio["x1"]; w1t_d = dio["w1t"]; w2t_d = dio["w2t"]
    h1t_d = dio["h1t"]; h2t_d = dio["h2t"]; w3t_d = dio["w3t"]
    s1_d = dio["s1c"]; s2_d = dio["s2c"]; sh1_d = dio["sh1c"]
    sh2_d = dio["sh2c"]; sw3_d = dio["sw3c"]; one_d = dio["one"]
    x2_d = dio["x2"]; rows_d = dio["rows_b"]

    sb = ar.alloc
    XT = sb("XT", [128, FC * GS], F32)
    TTS = sb("TTS", [128, GS], F32)
    XLN = sb("XLN", [128, FC * 1024], BF16)
    HA = sb("HA", [128, 32 * 1024], BF16)
    HB = sb("HB", [128, 32 * 1024], BF16)
    W12 = sb("W12", [128, 4 * 1024], BF16)
    WBLK = sb("WBLK", [128, 2 * 4096], BF16)
    X1R = sb("X1R", [128, 1024], F32)
    SCR = sb("SCR", [128, 2 * GS], F32)
    RR = sb("RR", [1, GS], F32)
    QQ = sb("QQ", [1, GS], F32)
    T1 = sb("T1", [1, GS], F32)
    T2 = sb("T2", [1, GS], F32)
    SROW = sb("SROW", [1, 2 * GS], F32)
    S1C = sb("S1C", [128, 32], F32)
    S2C = sb("S2C", [128, 32], F32)
    SH1C = sb("SH1C", [128, 32], F32)
    SH2C = sb("SH2C", [128, 32], F32)
    SW3C = sb("SW3C", [128, 8], F32)
    ONEF = sb("ONEF", [128, 128], F32)
    ONEB = sb("ONEB", [128, 128], BF16)
    CST = sb("CST", [128, 4], F32)
    XBC = HA[:].bitcast(BF16)[:, 0:FC * GS]
    SQS = HA[:].bitcast(BF16)[:, FC * GS:2 * FC * GS]

    for dst, src_ in ((S1C, s1_d), (S2C, s2_d), (SH1C, sh1_d), (SH2C, sh2_d),
                      (SW3C, sw3_d), (ONEF, one_d)):
        em.emit("sync", "dma_start", out=dst[:], in_=src_[:], inc=("LDI", 16))
    em.emit("gpsimd", "dma_start", out=ONEB[:], in_=one_d[:], inc=("LWI", 16))
    em.emit("vector", "memset", CST[:, 0:1], EPS_LN, inc=("VE", 1))
    for e_ in ("tensor", "vector", "scalar"):
        em.wait(e_, "LDI")
        em.wait(e_, "LWI")

    for g in (() if skip_prepass else range(NG)):
        em.wait("sync", "PE")
        em.wait("sync", "SE")
        for f in range(FC):
            em.emit("sync", "dma_start", out=XT[:, f * GS:(f + 1) * GS],
                    in_=x1_d[f * 128:(f + 1) * 128, g * GS:(g + 1) * GS],
                    inc=("LXP", 16))
        tiles = [XT[:, f * GS:(f + 1) * GS] for f in range(FC)]
        _ln_stats(em, pb, tiles, ONEB, XBC, SQS, SROW, T1, T2,
                  RR[0:1, 0:GS], QQ[0:1, 0:GS], CST, ld_waits=("LXP",),
                  rows_sqmean=True)
        em.wait("sync", "VE")
        em.wait("sync", "SE")
        em.emit("sync", "dma_start", out=rows_d[0:1, g * GS:(g + 1) * GS],
                in_=RR[0:1, 0:GS], inc=("ST", 16))
        em.emit("sync", "dma_start", out=rows_d[1:2, g * GS:(g + 1) * GS],
                in_=QQ[0:1, 0:GS], inc=("ST", 16))

    pe_w12 = [0, 0]
    pe_blk = [0, 0]
    bank_ve = {(p, h): 0 for p in range(2) for h in range(2)}
    bank_se = {(p, h): 0 for p in range(2) for h in range(2)}
    hbank_se = {(p, h): 0 for p in range(2) for h in range(2)}
    LNS = HB[:].bitcast(F32)[:, 0:4 * GS]
    SSB = HB[:].bitcast(BF16)[:, 8 * GS:16 * GS]
    XTH1 = HB[:].bitcast(F32)[:, 8 * GS:16 * GS]

    def _load_h0(t_):
        g0_ = t_ * 1024
        em.emit("sync", "dma_start", out=RR[0:1, 0:GS],
                in_=rows_d[0:1, g0_:g0_ + GS], inc=("LDR", 16))
        em.emit("sync", "dma_start", out=QQ[0:1, 0:GS],
                in_=rows_d[1:2, g0_:g0_ + GS], inc=("LDR", 16))
        for f in range(FC):
            em.emit("sync", "dma_start", out=XT[:, f * GS:(f + 1) * GS],
                    in_=x1_d[f * 128:(f + 1) * 128, g0_:g0_ + GS],
                    inc=("LX0", 16))

    def _ln_block(t_, h0_loaded):
        g0_ = t_ * 1024
        if not h0_loaded:
            em.wait("sync", "VE")
            em.wait("sync", "SE")
            em.wait("sync", "PE")
            em.wait("sync", "ST")
            _load_h0(t_)
        em.wait("sync", "PE")
        em.wait("sync", "SE")
        for f in range(FC):
            em.emit("sync", "dma_start", out=XTH1[:, f * GS:(f + 1) * GS],
                    in_=x1_d[f * 128:(f + 1) * 128, g0_ + GS:g0_ + 2 * GS],
                    inc=("LX1", 16))
        for half in range(2):
            bb = 2 * half
            if half == 1:
                em.wait("sync", "PE")
                em.wait("sync", "ST")
                em.emit("sync", "dma_start", out=RR[0:1, 0:GS],
                        in_=rows_d[0:1, g0_ + GS:g0_ + 2 * GS], inc=("LDR", 16))
                em.emit("sync", "dma_start", out=QQ[0:1, 0:GS],
                        in_=rows_d[1:2, g0_ + GS:g0_ + 2 * GS], inc=("LDR", 16))
            em.wait("vector", "LDR")
            em.wait("vector", "PE")
            em.wait("vector", "ST")
            em.emit("vector", "reciprocal", RR[0:1, 0:GS], RR[0:1, 0:GS],
                    inc=("VE", 1))
            em.wait("vector", "VE")
            em.emit("vector", "tensor_mul", QQ[0:1, 0:GS], QQ[0:1, 0:GS],
                    RR[0:1, 0:GS], inc=("VE", 1))
            em.wait("tensor", "LDR")
            em.wait("tensor", "VE")
            em.wait("tensor", "SE", max(bank_se[(0, 0)], bank_se[(0, 1)]))
            em.emit("tensor", "matmul", pb[bb][:], ONEF[0:1, :], RR[0:1, 0:GS],
                    start=True, stop=True)
            em.emit("tensor", "matmul", pb[bb + 1][:], ONEF[0:1, :],
                    QQ[0:1, 0:GS], start=True, stop=True, inc=("PE", 1))
            em.wait("vector", "PE")
            em.wait("vector", "LX%d" % half)
            em.wait("vector", "SE")
            em.wait("vector", "ST")
            src = XT if half == 0 else XTH1
            for w_ in range(2):
                em.wait("vector", "VE")
                for f4 in range(4):
                    f = w_ * 4 + f4
                    em.emit("vector", "tensor_mul",
                            LNS[:, f4 * GS:(f4 + 1) * GS],
                            src[:, f * GS:(f + 1) * GS], pb[bb][:],
                            inc=("VE", 1) if f4 == 3 else None)
                em.wait("vector", "VE")
                for f4 in range(4):
                    f = w_ * 4 + f4
                    em.emit("vector", "tensor_sub",
                            XLN[:, f * 1024 + half * GS:
                                f * 1024 + (half + 1) * GS],
                            LNS[:, f4 * GS:(f4 + 1) * GS], pb[bb + 1][:],
                            inc=("VE", 1) if f4 == 3 else None)
        return em.cnt["VE"]

    for k_ in bank_ve:
        bank_ve[k_] = max(bank_ve[k_], em.cnt["VE"])
        bank_se[k_] = max(bank_se[k_], em.cnt["SE"])
    ve_xln = _ln_block(0, False)
    for t in range(4):
        for k_ in bank_ve:
            bank_ve[k_] = max(bank_ve[k_], em.cnt["VE"])
            bank_se[k_] = max(bank_se[k_], em.cnt["SE"])
        if t + 1 < 4:
            em.wait("sync", "VE")
            em.wait("sync", "ST")
            _load_h0(t + 1)
        for jc in range(32):
            par = jc % 2
            lw = "LW%d" % par
            em.wait("gpsimd", "PE", pe_w12[par])
            em.emit("gpsimd", "dma_start", out=W12[:, par * 2048:par * 2048 + 1024],
                    in_=w1t_d[jc * 128:(jc + 1) * 128, :], inc=(lw, 16))
            em.emit("gpsimd", "dma_start",
                    out=W12[:, par * 2048 + 1024:par * 2048 + 2048],
                    in_=w2t_d[jc * 128:(jc + 1) * 128, :], inc=(lw, 16))
            em.wait("tensor", lw)
            em.wait("tensor", "VE", max(ve_xln, bank_ve[(par, 0)], bank_ve[(par, 1)]))
            em.wait("tensor", "SE", max(bank_se[(par, 0)], bank_se[(par, 1)]))
            for ic in range(FC):
                for half in range(2):
                    em.emit("tensor", "matmul", pb[4 * par + 2 * half][:],
                            W12[:, par * 2048 + ic * 128: par * 2048 + (ic + 1) * 128],
                            XLN[:, ic * 1024 + half * GS: ic * 1024 + (half + 1) * GS],
                            start=(ic == 0), stop=(ic == FC - 1))
            for ic in range(FC):
                for half in range(2):
                    em.emit("tensor", "matmul", pb[4 * par + 2 * half + 1][:],
                            W12[:, par * 2048 + 1024 + ic * 128:
                                par * 2048 + 1024 + (ic + 1) * 128],
                            XLN[:, ic * 1024 + half * GS: ic * 1024 + (half + 1) * GS],
                            start=(ic == 0), stop=(ic == FC - 1),
                            inc=("PE", 1) if (ic == FC - 1 and half == 1) else None)
            for half in range(2):
                em.wait("scalar", "PE")
                em.emit("scalar", "activation",
                        HA[:, jc * 1024 + half * GS: jc * 1024 + (half + 1) * GS],
                        pb[4 * par + 2 * half][:], AF.Silu, scale=S1C[:, jc:jc + 1],
                        inc=("SE", 1))
                em.wait("vector", "SE")
                em.emit("vector", "scalar_tensor_tensor",
                        HA[:, jc * 1024 + half * GS: jc * 1024 + (half + 1) * GS],
                        pb[4 * par + 2 * half + 1][:], S2C[:, jc:jc + 1],
                        HA[:, jc * 1024 + half * GS: jc * 1024 + (half + 1) * GS],
                        ALU.mult, ALU.mult, inc=("VE", 1))
                bank_se[(par, half)] = em.cnt["SE"]
                bank_ve[(par, half)] = em.cnt["VE"]
            pe_w12[par] = em.cnt["PE"]
        for stage, (wsrc, sdst, ssc, SRC_, DST_) in enumerate(
                ((h1t_d, None, SH1C, HA, HB), (h2t_d, None, SH2C, HB, HA))):
            src_ve = em.cnt["VE"]
            src_se = em.cnt["SE"]
            for kc in range(32):
                par = kc % 2
                lb = "LB%d" % par
                em.wait("gpsimd", "PE", pe_blk[par])
                em.emit("gpsimd", "dma_start", out=WBLK[:, par * 4096:(par + 1) * 4096],
                        in_=wsrc[kc * 128:(kc + 1) * 128, :], inc=(lb, 16))
                em.wait("tensor", lb)
                em.wait("tensor", "VE", src_ve)
                em.wait("tensor", "SE", max(src_se, hbank_se[(par, 0)],
                                            hbank_se[(par, 1)]))
                for jc in range(32):
                    for half in range(2):
                        em.emit("tensor", "matmul", pb[4 + 2 * par + half][:],
                                WBLK[:, par * 4096 + jc * 128: par * 4096 + (jc + 1) * 128],
                                SRC_[:, jc * 1024 + half * GS: jc * 1024 + (half + 1) * GS],
                                start=(jc == 0), stop=(jc == 31),
                                inc=("PE", 1) if (jc == 31 and half == 1) else None)
                for half in range(2):
                    em.wait("scalar", "PE")
                    em.emit("scalar", "activation",
                            DST_[:, kc * 1024 + half * GS: kc * 1024 + (half + 1) * GS],
                            pb[4 + 2 * par + half][:], AF.Gelu,
                            scale=ssc[:, kc:kc + 1], inc=("SE", 1))
                    hbank_se[(par, half)] = em.cnt["SE"]
                pe_blk[par] = em.cnt["PE"]
        if t + 1 < 4:
            ve_xln = _ln_block(t + 1, True)
        hc_se = em.cnt["SE"]
        for oc in range(FC):
            par = oc % 2
            r3 = "R%d" % (30 + par)
            em.wait("gpsimd", "PE", pe_blk[par])
            em.emit("gpsimd", "dma_start", out=WBLK[:, par * 4096:(par + 1) * 4096],
                    in_=w3t_d[oc * 128:(oc + 1) * 128, :], inc=("LB%d" % par, 16))
            em.wait("sync", "VE")
            em.wait("sync", "ST")
            em.wait("sync", "SE")
            em.emit("sync", "dma_start", out=X1R[:, 0:1024],
                    in_=x1_d[oc * 128:(oc + 1) * 128, t * 1024:(t + 1) * 1024],
                    inc=(r3, 16))
            em.wait("tensor", "LB%d" % par)
            em.wait("tensor", "SE", max(hc_se, hbank_se[(par, 0)],
                                        hbank_se[(par, 1)]))
            for kc in range(32):
                for half in range(2):
                    em.emit("tensor", "matmul", pb[4 + 2 * par + half][:],
                            WBLK[:, par * 4096 + kc * 128: par * 4096 + (kc + 1) * 128],
                            HA[:, kc * 1024 + half * GS: kc * 1024 + (half + 1) * GS],
                            start=(kc == 0), stop=(kc == 31),
                            inc=("PE", 1) if (kc == 31 and half == 1) else None)
            for half in range(2):
                bk = 4 + 2 * par + half
                em.wait("scalar", "PE")
                em.wait("scalar", "VE")
                em.emit("scalar", "activation", SCR[:, par * GS:(par + 1) * GS],
                        pb[bk][:], AF.Copy, scale=SW3C[:, oc:oc + 1], inc=("SE", 1))
                em.wait("vector", "SE")
                em.wait("vector", r3)
                em.wait("vector", "ST")
                em.emit("vector", "tensor_add", X1R[:, half * GS:(half + 1) * GS],
                        SCR[:, par * GS:(par + 1) * GS],
                        X1R[:, half * GS:(half + 1) * GS], inc=("VE", 1))
                hbank_se[(par, half)] = em.cnt["SE"]
                em.wait("sync", "VE")
                em.emit("sync", "dma_start",
                        out=x2_d[oc * 128:(oc + 1) * 128,
                                 t * 1024 + half * GS: t * 1024 + (half + 1) * GS],
                        in_=X1R[:, half * GS:(half + 1) * GS], inc=("ST", 16))
                if rows_next is not None:
                    sl = (2 * par + half) * GS
                    em.wait("vector", "VE")
                    em.emit("vector", "tensor_copy", SSB[:, sl:sl + GS],
                            X1R[:, half * GS:(half + 1) * GS], inc=("VE", 1))
                    em.wait("scalar", "VE")
                    em.emit("scalar", "activation",
                            SSB[:, 4 * GS + sl:4 * GS + sl + GS],
                            X1R[:, half * GS:(half + 1) * GS], AF.Square,
                            inc=("SE", 1))
                    em.wait("tensor", "VE")
                    em.wait("tensor", "SE")
                    em.emit("tensor", "matmul", pb[2 * half][0:1, 0:GS],
                            ONEB[:, 0:1], SSB[:, sl:sl + GS],
                            start=(oc == 0), stop=(oc == FC - 1))
                    em.emit("tensor", "matmul", pb[2 * half + 1][0:1, 0:GS],
                            ONEB[:, 0:1], SSB[:, 4 * GS + sl:4 * GS + sl + GS],
                            start=(oc == 0), stop=(oc == FC - 1),
                            inc=("PE", 1) if oc == FC - 1 else None)
            pe_blk[par] = em.cnt["PE"]
        if rows_next is not None:
            for half in range(2):
                em.wait("vector", "PE")
                em.wait("vector", "VE")
                em.emit("vector", "tensor_scalar", SROW[0:1, 0:GS],
                        pb[2 * half][0:1, 0:GS], 1.0 / D, None, ALU.mult)
                em.emit("vector", "tensor_scalar", SROW[0:1, GS:2 * GS],
                        pb[2 * half + 1][0:1, 0:GS], 1.0 / D, None, ALU.mult,
                        inc=("VE", 1))
                em.wait("scalar", "VE")
                em.emit("scalar", "activation", T1[0:1, :], SROW[0:1, 0:GS],
                        AF.Square, inc=("SE", 1))
                em.wait("vector", "SE")
                em.emit("vector", "tensor_sub", T2[0:1, :], SROW[0:1, GS:2 * GS],
                        T1[0:1, :], inc=("VE", 1))
                em.wait("scalar", "VE")
                em.emit("scalar", "activation", T1[0:1, :], T2[0:1, :], AF.Sqrt,
                        bias=CST[0:1, 0:1], inc=("SE", 1))
                em.wait("vector", "SE")
                em.wait("vector", "ST")
                em.emit("vector", "reciprocal", RR[0:1, 0:GS], T1[0:1, :],
                        inc=("VE", 1))
                em.wait("vector", "VE")
                em.emit("vector", "tensor_mul", QQ[0:1, 0:GS], SROW[0:1, 0:GS],
                        RR[0:1, 0:GS], inc=("VE", 1))
                em.wait("sync", "VE")
                g0r = t * 1024 + half * GS
                em.emit("sync", "dma_start", out=rows_next[0:1, g0r:g0r + GS],
                        in_=RR[0:1, 0:GS], inc=("ST", 16))
                em.emit("sync", "dma_start", out=rows_next[1:2, g0r:g0r + GS],
                        in_=QQ[0:1, 0:GS], inc=("ST", 16))
            for k_ in bank_ve:
                bank_ve[k_] = max(bank_ve[k_], em.cnt["VE"])
                bank_se[k_] = max(bank_se[k_], em.cnt["SE"])

    em.wait("sync", "ST")


def build_B():
    nc, em, ar, dio, pb = _standalone(B_DRAMS())
    emit_B(nc, em, ar, dio, pb)
    em.finalize(nc)
    return nc


def _prep_B(inputs, n_cores=8):
    f32 = np.float32

    def lhsT_blocks(w, IC, OC):
        wT = np.asarray(w, f32).T
        I, O = wT.shape
        t = wT.reshape(IC, 128, OC, 128).transpose(2, 1, 0, 3)
        return np.ascontiguousarray(t.reshape(OC * 128, IC * 128))

    def colmaj(s, C):
        return np.ascontiguousarray(np.asarray(s, f32).reshape(C, 128).T)

    shared = dict(
        w1t=lhsT_blocks(inputs["L0_ffn_w1_w"], 8, 32),
        w2t=lhsT_blocks(inputs["L0_ffn_w2_w"], 8, 32),
        h1t=lhsT_blocks(inputs["L0_ffn_h1_w"], 32, 32),
        h2t=lhsT_blocks(inputs["L0_ffn_h2_w"], 32, 32),
        w3t=lhsT_blocks(inputs["L0_ffn_w3_w"], 32, 8),
        s1c=colmaj(inputs["L0_ffn_w1_s"], 32),
        s2c=colmaj(inputs["L0_ffn_w2_s"], 32),
        sh1c=colmaj(inputs["L0_ffn_h1_s"], 32),
        sh2c=colmaj(inputs["L0_ffn_h2_s"], 32),
        sw3c=colmaj(inputs["L0_ffn_w3_s"], 8),
        one=np.ones((128, 128), f32),
    )
    return shared





NH = 16
HD = 64
CH = 128


def emit_C(nc, em, ar, dio, pb, n_cores=8, dbg=False, skip_prepass=False, rows_next=None):
    RG = [[a, a + 1] for a in range(0, n_cores, 2)]
    x2_d = dio["x2"]; wqkv_d = dio["wqkv"]; wout_d = dio["wout"]
    utm_d = dio["utm"]; idn_d = dio["idn"]; idlo_d = dio["idlo"]
    one_d = dio["one"]; mkv_d = dio["mkv"]; sel16_d = dio["sel16"]
    x3_d = dio["x3"]
    kch = 1
    if dbg:
        kvdbg_d = dio["kvdbg"]
    rows_d = dio["rows_c"]; pq_d = dio["pq_sc"]; nu_d = dio["nu_sc"]
    kv_in = dio["kv_in"]; kv_out = dio["kv_out"]

    sb = ar.alloc
    WQKV = sb("WQKV", [128, FC * 3 * D], BF16)
    WOUT = WQKV
    XT = sb("XT", [128, FC * GS], F32)
    XT2 = sb("XT2", [128, FC * GS], F32)
    XLN = sb("XLN", [128, FC * GS], BF16)
    TTS = sb("TTS", [128, 5 * GS], F32)
    AR1 = sb("AR1", [128, NH * GS], BF16)
    AR2 = sb("AR2", [128, NH * GS], BF16)
    ESC = sb("ESC", [128, 2 * GS], F32)
    RSC = sb("RSC", [128, 2 * GS], F32)
    KT = sb("KT", [128, NH * HD], BF16)
    VTG = sb("VTG", [128, 4 * 1024], BF16)
    MT = sb("MT", [128, NH * CH], BF16)
    DNC = sb("DNC", [128, 32 * NH], F32)
    SKV = sb("SKV", [64, NH * HD], F32)
    SKVB = sb("SKVB", [64, NH * HD], BF16)
    SKC = sb("SKC", [64, NH], F32)
    SKCB = sb("SKCB", [64, NH], BF16)
    AFT = sb("AFT", [128, FC * GS], BF16)
    NTS = sb("NTS", [64, 2 * GS], F32)
    RECC = sb("RECC", [128, NH], F32)
    RECR = sb("RECR", [16, 4 * CH], F32)
    PACK = sb("PACK", [64, 1056], F32)
    GSBc = sb("GSBc", [128, 1056], F32)
    KVU = sb("KVU", [64, 1056], F32)
    KVUB = sb("KVUB", [64, 1056], BF16)
    UTM = sb("UTM", [128, 128], F32)
    IDF = sb("IDF", [128, 128], F32)
    IDB = sb("IDB", [128, 128], BF16)
    IDLO = sb("IDLO", [128, 64], BF16)
    ONEF = sb("ONEF", [128, 128], F32)
    ONEB = sb("ONEB", [128, 128], BF16)
    MKV = sb("MKV", [128, kch * 64], F32)
    SEL16 = sb("SEL16", [16, 16 * 64], F32)
    RR = sb("RR", [1, GS], F32)
    QQ = sb("QQ", [1, GS], F32)
    T1 = sb("T1", [1, GS], F32)
    T2 = sb("T2", [1, GS], F32)
    SROW = sb("SROW", [1, 2 * GS], F32)
    CST = sb("CST", [128, 4], F32)
    XBC = AR1[:].bitcast(BF16)[:, 0:FC * GS]
    SQS = AR2[:].bitcast(BF16)[:, 0:FC * GS]
    RR2 = SROW[0:1, 0:GS]
    QQ2 = SROW[0:1, GS:2 * GS]

    def XTb(p):
        return XT if p == 0 else XT2

    def RRb(p):
        return RR if p == 0 else RR2

    def QQb(p):
        return QQ if p == 0 else QQ2

    def ARb(p):
        return AR1 if p == 0 else AR2

    def PQ(h):
        return AR1[0:64, h * GS:(h + 1) * GS]

    def NUG(h):
        return AR1[64:128, h * GS:(h + 1) * GS]

    def PK(h):
        return AR2[0:64, h * GS:(h + 1) * GS]

    def VC(ci, h):
        return VTG[:, ci * 1024 + h * HD:ci * 1024 + (h + 1) * HD]

    for dst, src_ in ((UTM, utm_d), (IDF, idn_d), (ONEF, one_d), (MKV, None),
                      ):
        if src_ is not None:
            em.emit("sync", "dma_start", out=dst[:], in_=src_[:], inc=("LDI", 16))
    for r4 in range(kch):
        em.emit("sync", "dma_start", out=MKV[:, r4 * 64:(r4 + 1) * 64],
                in_=mkv_d[r4 * 128:(r4 + 1) * 128, :], inc=("LDI", 16))
    for ic in range(FC):
        em.emit("gpsimd", "dma_start", out=WQKV[:, ic * 3 * D:(ic + 1) * 3 * D],
                in_=wqkv_d[ic * 128:(ic + 1) * 128, :], inc=("LWI", 16))
    em.emit("gpsimd", "dma_start", out=ONEB[:], in_=one_d[:], inc=("LWI", 16))
    em.emit("gpsimd", "dma_start", out=IDB[:], in_=idn_d[:], inc=("LWI", 16))
    em.emit("gpsimd", "dma_start", out=IDLO[:], in_=idlo_d[:], inc=("LWI", 16))
    em.emit("sync", "dma_start", out=SEL16[:], in_=sel16_d[:], inc=("LDI", 16))
    em.emit("vector", "memset", CST[:, 0:1], EPS_LN)
    em.emit("vector", "memset", CST[:, 1:2], EPS_ATT)
    em.emit("vector", "memset", SKV[:], 0.0)
    em.emit("vector", "memset", SKVB[:], 0.0)
    em.emit("vector", "memset", SKC[:], 0.0)
    em.emit("vector", "memset", SKCB[:], 0.0)
    em.emit("vector", "memset", DNC[:], 0.0, inc=("VE", 1))
    for e_ in ("tensor", "vector", "scalar"):
        em.wait(e_, "LDI")
        em.wait(e_, "LWI")

    for g in (() if skip_prepass else range(NG)):
        em.wait("sync", "PE")
        em.wait("sync", "SE")
        for f in range(FC):
            em.emit("sync", "dma_start", out=XT[:, f * GS:(f + 1) * GS],
                    in_=x2_d[f * 128:(f + 1) * 128, g * GS:(g + 1) * GS],
                    inc=("LXP", 16))
        tiles = [XT[:, f * GS:(f + 1) * GS] for f in range(FC)]
        _ln_stats(em, pb, tiles, ONEB, XBC, SQS, SROW, T1, T2,
                  RR[0:1, :], QQ[0:1, :], CST, ld_waits=("LXP",))
        em.wait("sync", "VE")
        em.emit("sync", "dma_start", out=rows_d[0:1, g * GS:(g + 1) * GS],
                in_=RR[0:1, :], inc=("ST", 16))
        em.emit("sync", "dma_start", out=rows_d[1:2, g * GS:(g + 1) * GS],
                in_=QQ[0:1, :], inc=("ST", 16))

    qkv_mk = [0, 0]
    qkv_se = [0, 0]
    tp_se = [0, 0]
    mt_ve = [0, 0]
    nu_se = [0, 0]
    dn_ve = 0
    dl_ve = 0

    def _load_grp(g_):
        p = g_ % 2
        g0_ = g_ * GS
        em.emit("sync", "dma_start", out=RRb(p)[0:1, :],
                in_=rows_d[0:1, g0_:g0_ + GS], inc=("LR%d" % p, 16))
        em.emit("sync", "dma_start", out=QQb(p)[0:1, :],
                in_=rows_d[1:2, g0_:g0_ + GS], inc=("LR%d" % p, 16))
        for f in range(FC):
            em.emit("sync", "dma_start", out=XTb(p)[:, f * GS:(f + 1) * GS],
                    in_=x2_d[f * 128:(f + 1) * 128, g0_:g0_ + GS],
                    inc=("LX%d" % p, 16))

    AFF = AFT[:].bitcast(F32)[:, 0:4 * GS]
    c_ln = {"t": 0, "x": 0}

    def _c_ln_t(k):
        p = k % 2
        em.wait("tensor", "LR%d" % p)
        em.wait("tensor", "VE", max(qkv_mk[1], c_ln["x"]))
        em.wait("tensor", "SE", qkv_se[1])
        em.emit("tensor", "matmul", pb[7][:], ONEF[0:1, :], RRb(p)[0:1, :],
                start=True, stop=True, inc=("PE", 1))
        pe_r = em.cnt["PE"]
        em.wait("vector", "PE", pe_r)
        em.wait("vector", "LX%d" % p)
        em.wait("vector", "VE")
        for f in range(4):
            em.emit("vector", "tensor_mul", TTS[:, (1 + f) * GS:(2 + f) * GS],
                    XTb(p)[:, f * GS:(f + 1) * GS], pb[7][:])
        for f in range(4):
            em.emit("vector", "tensor_mul", AFF[:, f * GS:(f + 1) * GS],
                    XTb(p)[:, (4 + f) * GS:(5 + f) * GS], pb[7][:],
                    inc=("VE", 1) if f == 3 else None)
        c_ln["t"] = em.cnt["VE"]

    def _c_ln_s(k):
        p = k % 2
        em.wait("tensor", "VE", c_ln["t"])
        em.emit("tensor", "matmul", pb[7][:], ONEF[0:1, :], QQb(p)[0:1, :],
                start=True, stop=True, inc=("PE", 1))
        pe_q = em.cnt["PE"]
        em.wait("vector", "PE", pe_q)
        em.wait("vector", "VE")
        for f in range(4):
            em.emit("vector", "tensor_sub", XLN[:, f * GS:(f + 1) * GS],
                    TTS[:, (1 + f) * GS:(2 + f) * GS], pb[7][:])
        for f in range(4):
            em.emit("vector", "tensor_sub", XLN[:, (4 + f) * GS:(5 + f) * GS],
                    AFF[:, f * GS:(f + 1) * GS], pb[7][:],
                    inc=("VE", 1) if f == 3 else None)
        c_ln["x"] = em.cnt["VE"]

    em.wait("sync", "PE")
    em.wait("sync", "VE")
    em.wait("sync", "SE")
    em.wait("sync", "ST")
    _load_grp(0)
    _c_ln_t(0)
    _c_ln_s(0)
    for g in range(NG):
        par = g % 2
        lx = "LX%d" % par
        g0 = g * GS
        if g + 1 < NG:
            em.wait("sync", "PE")
            em.wait("sync", "VE")
            em.wait("sync", "SE")
            em.wait("sync", "ST")
            _load_grp(g + 1)
        em.wait("tensor", "LWI")
        ve_xln = c_ln["x"]
        em.wait("tensor", "SPQ")
        em.wait("tensor", "LNU")
        for oc in range(16):
            bq = 6 + oc % 2
            em.wait("tensor", "VE", max(ve_xln, qkv_mk[oc % 2], dn_ve))
            em.wait("tensor", "SE", qkv_se[oc % 2])
            for ic in range(FC):
                em.emit("tensor", "matmul", pb[bq][:],
                        WQKV[:, ic * 3 * D + oc * 128: ic * 3 * D + (oc + 1) * 128],
                        XLN[:, ic * GS:(ic + 1) * GS],
                        start=(ic == 0), stop=(ic == FC - 1),
                        inc=("PE", 1) if ic == FC - 1 else None)
            h0, h1 = 2 * (oc % 8), 2 * (oc % 8) + 1
            op = (oc % 2) * GS
            em.wait("scalar", "PE")
            em.emit("scalar", "activation", RSC[:, op:op + GS], pb[bq][:],
                    AF.Relu, scale=-1.0, inc=("SE", 1))
            em.wait("scalar", "SE")
            em.emit("scalar", "activation", ESC[:, op:op + GS],
                    RSC[:, op:op + GS], AF.Exp, scale=-1.0, inc=("SE", 1))
            dst = PQ if oc < 8 else PK
            em.wait("vector", "SE")
            em.wait("vector", "PE")
            em.emit("vector", "scalar_tensor_tensor", dst(h0),
                    pb[bq][0:64, :], 0.0, ESC[0:64, op:op + GS],
                    ALU.max, ALU.add)
            em.emit("vector", "scalar_tensor_tensor", dst(h1), pb[bq][64:128, :],
                    0.0, ESC[64:128, op:op + GS], ALU.max, ALU.add,
                    inc=("VE", 1))
            qkv_mk[oc % 2] = em.cnt["VE"]
            qkv_se[oc % 2] = em.cnt["SE"]
        for sc in range(4):
            for vb in range(2):
                p = (sc * 2 + vb) % 2
                bq = 6 + p
                em.wait("tensor", "VE", max(ve_xln, qkv_mk[p], dn_ve))
                em.wait("tensor", "SE", qkv_se[p])
                for ic in range(FC):
                    em.emit("tensor", "matmul", pb[bq][:],
                            XLN[:, ic * GS + sc * CH:ic * GS + (sc + 1) * CH],
                            WQKV[:, ic * 3 * D + 2048 + vb * 512:
                                 ic * 3 * D + 2048 + (vb + 1) * 512],
                            start=(ic == 0), stop=(ic == FC - 1),
                            inc=("PE", 1) if ic == FC - 1 else None)
                em.wait("vector", "PE")
                em.emit("vector", "tensor_copy",
                        VTG[:, sc * 1024 + vb * 512:sc * 1024 + (vb + 1) * 512],
                        pb[bq][:], inc=("VE", 1))
                qkv_mk[p] = em.cnt["VE"]
        em.wait("sync", "VE")
        em.emit("sync", "dma_start", out=pq_d[:, g * NH * GS:(g + 1) * NH * GS],
                in_=AR1[0:64, :], inc=("SPQ", 16))
        ve_qkv = em.cnt["VE"]
        for ci in range(4):
            c = g * 4 + ci
            t0 = ci * CH
            for h in range(NH):
                hp = h % 2
                bm = 2 + hp
                em.wait("tensor", "VE", max(dl_ve, mt_ve[hp], ve_qkv))
                em.emit("tensor", "matmul", pb[bm][0:128, 0:128], PK(h)[:, t0:t0 + CH],
                        PQ(h)[:, t0:t0 + CH], start=True, stop=True, inc=("PE", 1))
                em.wait("vector", "PE")
                em.emit("vector", "tensor_mul", MT[:, h * CH:(h + 1) * CH],
                        pb[bm][0:128, 0:128], UTM[:], inc=("VE", 1))
                mt_ve[hp] = em.cnt["VE"]
            for h in range(NH):
                hp = h % 2
                em.wait("tensor", "SE", tp_se[hp])
                em.wait("tensor", "VE", max(dl_ve, ve_qkv))
                em.emit("tensor", "transpose", pb[hp][:].bitcast(BF16)[0:128, 0:64],
                        PK(h)[:, t0:t0 + CH], IDB[0:64, 0:64], inc=("PE", 1))
                em.wait("scalar", "PE")
                em.emit("scalar", "copy", KT[:, h * HD:(h + 1) * HD],
                        pb[hp][:].bitcast(BF16)[0:128, 0:64], inc=("SE", 1))
                tp_se[hp] = em.cnt["SE"]
            for h in range(NH):
                hp = h % 2
                bn = 4 + hp
                em.wait("tensor", "VE", mt_ve[hp])
                em.wait("tensor", "SE", nu_se[hp])
                em.emit("tensor", "matmul", pb[bn][0:64, 0:128],
                        SKVB[:, h * HD:(h + 1) * HD], PQ(h)[:, t0:t0 + CH],
                        start=True, stop=False)
                em.emit("tensor", "matmul", pb[bn][0:64, 0:128],
                        VC(ci, h), MT[:, h * CH:(h + 1) * CH],
                        start=False, stop=True, inc=("PE", 1))
                em.wait("scalar", "PE")
                em.wait("scalar", "LNU")
                em.emit("scalar", "copy", NUG(h)[:, t0:t0 + CH], pb[bn][0:64, 0:128],
                        inc=("SE", 1))
                nu_se[hp] = em.cnt["SE"]
            em.wait("tensor", "VE", dn_ve)
            for h in range(NH):
                em.emit("tensor", "matmul", pb[6][0:128, h:h + 1],
                        PQ(h)[:, t0:t0 + CH], SKCB[:, h:h + 1],
                        start=(h == 0), stop=False)
                em.emit("tensor", "matmul", pb[6][0:128, h:h + 1],
                        MT[:, h * CH:(h + 1) * CH], ONEB[:, 0:1],
                        start=False, stop=(h == NH - 1),
                        inc=("PE", 1) if h == NH - 1 else None)
            em.wait("vector", "PE")
            em.emit("vector", "tensor_copy", DNC[:, c * NH:(c + 1) * NH],
                    pb[6][0:128, 0:16], inc=("VE", 1))
            dn_ve = em.cnt["VE"]
            em.wait("tensor", "SE", tp_se[1])
            em.wait("tensor", "VE", max(mt_ve[0], mt_ve[1]))
            for h in range(NH):
                bs = h // 8
                em.emit("tensor", "matmul",
                        pb[bs][0:64, (h % 8) * HD:(h % 8 + 1) * HD],
                        KT[:, h * HD:(h + 1) * HD], VC(ci, h),
                        start=(h % 8 == 0), stop=(h % 8 == 7))
            for h in range(NH):
                em.emit("tensor", "matmul", pb[2][0:64, h:h + 1],
                        KT[:, h * HD:(h + 1) * HD], ONEB[:, 0:1],
                        start=(h == 0), stop=(h == NH - 1),
                        inc=("PE", 1) if h == NH - 1 else None)
            em.wait("vector", "PE")
            em.emit("vector", "tensor_add", SKV[:, 0:8 * HD], SKV[:, 0:8 * HD],
                    pb[0][0:64, 0:512])
            em.emit("vector", "tensor_add", SKV[:, 8 * HD:16 * HD],
                    SKV[:, 8 * HD:16 * HD], pb[1][0:64, 0:512])
            em.emit("vector", "tensor_add", SKC[:], SKC[:], pb[2][0:64, 0:16],
                    inc=("VE", 1))
            em.wait("vector", "VE")
            em.emit("vector", "tensor_copy", SKVB[:], SKV[:])
            em.emit("vector", "tensor_copy", SKCB[:], SKC[:], inc=("VE", 1))
            dl_ve = em.cnt["VE"]
            if g + 1 < NG and ci == 1:
                _c_ln_t(g + 1)
            if g + 1 < NG and ci == 2:
                _c_ln_s(g + 1)
        em.wait("sync", "SE")
        em.emit("sync", "dma_start", out=nu_d[:, g * NH * GS:(g + 1) * NH * GS],
                in_=AR1[64:128, :], inc=("LNU", 16))

    em.wait("gpsimd", "PE")
    for ic in range(FC):
        em.emit("gpsimd", "dma_start", out=WOUT[:, ic * D:(ic + 1) * D],
                in_=wout_d[ic * 128:(ic + 1) * 128, :], inc=("LWO", 16))
    em.wait("vector", "VE")
    em.emit("vector", "memset", PACK[:], 0.0, inc=("VE", 1))
    em.wait("vector", "VE")
    for h in range(NH):
        em.emit("vector", "tensor_copy", PACK[:, h * 66:h * 66 + 64],
                SKV[:, h * HD:(h + 1) * HD],
                inc=("VE", 1) if h == NH - 1 else None)
    em.wait("vector", "VE")
    for h in range(NH):
        em.emit("vector", "tensor_copy", PACK[:, h * 66 + 64:h * 66 + 65],
                SKC[:, h:h + 1], inc=("VE", 1) if h == NH - 1 else None)
    em.wait("sync", "VE")
    em.emit("sync", "dma_start", out=kv_in[:], in_=PACK[:], inc=("ST", 16))
    em.wait("gpsimd", "ST")
    em.emit("gpsimd", "collective_compute", "AllGather", ALU.bypass,
            replica_groups=RG, ins=[kv_in[:]], outs=[kv_out[:]], inc=("CC", 1))
    em.wait("gpsimd", "CC")
    for r4 in range(kch):
        em.emit("gpsimd", "dma_start", out=GSBc[:],
                in_=kv_out[r4 * 128:(r4 + 1) * 128, :], inc=("GD", 16))
        em.wait("tensor", "GD")
        em.wait("tensor", "LDI")
        em.wait("tensor", "VE" if r4 == 0 else "PE")
        em.emit("tensor", "matmul", pb[0][0:64, 0:512], MKV[:, r4 * 64:(r4 + 1) * 64],
                GSBc[:, 0:512], start=(r4 == 0), stop=(r4 == kch - 1))
        em.emit("tensor", "matmul", pb[1][0:64, 0:512], MKV[:, r4 * 64:(r4 + 1) * 64],
                GSBc[:, 512:1024], start=(r4 == 0), stop=(r4 == kch - 1))
        em.emit("tensor", "matmul", pb[2][0:64, 0:32], MKV[:, r4 * 64:(r4 + 1) * 64],
                GSBc[:, 1024:1056], start=(r4 == 0), stop=(r4 == kch - 1),
                inc=("PE", 1))
    em.wait("vector", "PE")
    em.emit("vector", "tensor_copy", KVU[:, 0:512], pb[0][0:64, 0:512])
    em.emit("vector", "tensor_copy", KVU[:, 512:1024], pb[1][0:64, 0:512])
    em.emit("vector", "tensor_copy", KVU[:, 1024:1056], pb[2][0:64, 0:32],
            inc=("VE", 1))
    em.wait("vector", "VE")
    em.emit("vector", "tensor_copy", KVUB[:], KVU[:], inc=("VE", 1))
    if dbg:
        em.wait("sync", "VE")
        em.emit("sync", "dma_start", out=kvdbg_d[:, 0:1056], in_=KVU[:],
                inc=("ST", 16))
        em.emit("sync", "dma_start", out=kvdbg_d[:, 1056:2112], in_=PACK[:],
                inc=("ST", 16))

    corr_mk = [0, 0]
    nts_mk = [0, 0]
    wo_mk = {1: 0, 7: 0}

    def _load_corr(g_):
        p = g_ % 2
        g0_ = g_ * GS
        em.emit("sync", "dma_start", out=ARb(p)[0:64, :],
                in_=pq_d[:, g_ * NH * GS:(g_ + 1) * NH * GS],
                inc=("SP%d" % p, 16))
        em.emit("sync", "dma_start", out=ARb(p)[64:128, :],
                in_=nu_d[:, g_ * NH * GS:(g_ + 1) * NH * GS],
                inc=("SP%d" % p, 16))
        for f in range(FC):
            em.emit("sync", "dma_start", out=XTb(p)[:, f * GS:(f + 1) * GS],
                    in_=x2_d[f * 128:(f + 1) * 128, g0_:g0_ + GS],
                    inc=("LX%d" % p, 16))

    em.wait("sync", "PE")
    em.wait("sync", "VE")
    em.wait("sync", "SE")
    em.wait("sync", "SPQ")
    em.wait("sync", "LNU")
    _load_corr(0)
    for g in range(NG):
        par = g % 2
        lx = "LX%d" % par
        g0 = g * GS
        if g + 1 < NG:
            em.wait("sync", "PE")
            em.wait("sync", "VE")
            _load_corr(g + 1)
        for ci in range(4):
            c = g * 4 + ci
            em.wait("tensor", "SP%d" % par)
            em.wait("tensor", "VE")
            for h in range(NH):
                em.emit("tensor", "matmul", pb[6][0:128, h:h + 1],
                        ARb(par)[0:64, h * GS + ci * CH: h * GS + (ci + 1) * CH],
                        KVUB[:, h * 66 + 64:h * 66 + 65],
                        start=(h == 0), stop=(h == NH - 1),
                        inc=("PE", 1) if h == NH - 1 else None)
            em.wait("vector", "PE")
            em.emit("vector", "scalar_tensor_tensor", RECC[:], pb[6][0:128, 0:16],
                    CST[:, 1:2], DNC[:, c * NH:(c + 1) * NH], ALU.add, ALU.add,
                    inc=("VE", 1))
            em.wait("vector", "VE")
            em.emit("vector", "reciprocal", RECC[:], RECC[:], inc=("VE", 1))
            em.wait("tensor", "VE")
            em.emit("tensor", "transpose", pb[7][0:16, 0:128], RECC[:], IDF[:],
                    inc=("PE", 1))
            em.wait("vector", "PE")
            em.emit("vector", "tensor_copy", RECR[:, ci * CH:(ci + 1) * CH],
                    pb[7][0:16, 0:128], inc=("VE", 1))
        rec_ready = em.cnt["VE"]
        for h in range(NH):
            hp = h % 2
            bc = 4 + hp
            br = 2 + hp
            em.wait("tensor", "SP%d" % par)
            em.wait("tensor", "VE", max(rec_ready, corr_mk[hp]))
            em.emit("tensor", "matmul", pb[bc][0:64, 0:512],
                    KVUB[:, h * 66:h * 66 + 64],
                    ARb(par)[0:64, h * GS:(h + 1) * GS],
                    start=True, stop=True)
            em.emit("tensor", "matmul", pb[br][0:64, 0:512], SEL16[:, h * 64:(h + 1) * 64],
                    RECR[:], start=True, stop=True, inc=("PE", 1))
            em.wait("vector", "PE")
            em.wait("vector", "VE", nts_mk[hp])
            em.emit("vector", "tensor_add", NTS[:, hp * GS:(hp + 1) * GS],
                    ARb(par)[64:128, h * GS:(h + 1) * GS],
                    pb[bc][0:64, 0:512], inc=("VE", 1))
            em.wait("vector", "VE")
            em.emit("vector", "tensor_mul",
                    AFT[hp * 64:hp * 64 + 64, (h // 2) * GS:(h // 2 + 1) * GS],
                    NTS[:, hp * GS:(hp + 1) * GS], pb[br][0:64, 0:512],
                    inc=("VE", 1))
            corr_mk[hp] = em.cnt["VE"]
            nts_mk[hp] = em.cnt["VE"]
        aft_ve = em.cnt["VE"]
        em.wait("tensor", "LWO")
        for oc in range(FC):
            bo = 1 if oc % 2 == 0 else 7
            em.wait("tensor", "VE", aft_ve)
            em.wait("tensor", "SE", wo_mk[bo])
            for fc in range(FC):
                em.emit("tensor", "matmul", pb[bo][:],
                        WOUT[:, fc * D + oc * 128: fc * D + (oc + 1) * 128],
                        AFT[:, fc * GS:(fc + 1) * GS],
                        start=(fc == 0), stop=(fc == FC - 1),
                        inc=("PE", 1) if fc == FC - 1 else None)
            em.wait("scalar", "PE")
            em.wait("scalar", lx)
            em.wait("scalar", "VE")
            em.wait("scalar", "ST")
            em.emit("scalar", "activation", TTS[:, GS:2 * GS], pb[bo][:], AF.Copy,
                    inc=("SE", 1))
            wo_mk[bo] = em.cnt["SE"]
            em.wait("vector", "SE")
            em.emit("vector", "tensor_add", TTS[:, GS:2 * GS], TTS[:, GS:2 * GS],
                    XTb(par)[:, oc * GS:(oc + 1) * GS], inc=("VE", 1))
            em.wait("sync", "VE")
            em.emit("sync", "dma_start",
                    out=x3_d[oc * 128:(oc + 1) * 128, g0:g0 + GS],
                    in_=TTS[:, GS:2 * GS], inc=("ST", 16))
            if rows_next is not None:
                sl = (oc % 2) * GS
                em.wait("vector", "VE")
                em.emit("vector", "tensor_copy", XLN[:, sl:sl + GS],
                        TTS[:, GS:2 * GS], inc=("VE", 1))
                em.wait("scalar", "VE")
                em.emit("scalar", "activation", XLN[:, 2 * GS + sl:2 * GS + sl + GS],
                        TTS[:, GS:2 * GS], AF.Square, inc=("SE", 1))
                if oc == 0:
                    em.wait("tensor", "VE")
                em.wait("tensor", "SE")
                em.emit("tensor", "matmul", pb[0][0:1, 0:GS], ONEB[:, 0:1],
                        XLN[:, sl:sl + GS], start=(oc == 0), stop=(oc == FC - 1))
                em.emit("tensor", "matmul", pb[2][0:1, 0:GS], ONEB[:, 0:1],
                        XLN[:, 2 * GS + sl:2 * GS + sl + GS],
                        start=(oc == 0), stop=(oc == FC - 1),
                        inc=("PE", 1) if oc == FC - 1 else None)
        if rows_next is not None:
            em.wait("vector", "PE")
            em.wait("vector", "VE")
            em.emit("vector", "tensor_scalar", SROW[0:1, 0:GS], pb[0][0:1, 0:GS],
                    1.0 / D, None, ALU.mult)
            em.emit("vector", "tensor_scalar", SROW[0:1, GS:2 * GS],
                    pb[2][0:1, 0:GS], 1.0 / D, None, ALU.mult, inc=("VE", 1))
            em.wait("scalar", "VE")
            em.emit("scalar", "activation", T1[0:1, :], SROW[0:1, 0:GS], AF.Square,
                    inc=("SE", 1))
            em.wait("vector", "SE")
            em.emit("vector", "tensor_sub", T2[0:1, :], SROW[0:1, GS:2 * GS],
                    T1[0:1, :], inc=("VE", 1))
            em.wait("scalar", "VE")
            em.emit("scalar", "activation", T1[0:1, :], T2[0:1, :], AF.Sqrt,
                    bias=CST[0:1, 0:1], inc=("SE", 1))
            em.wait("vector", "SE")
            em.wait("vector", "ST")
            em.emit("vector", "reciprocal", RR[0:1, :], T1[0:1, :], inc=("VE", 1))
            em.wait("vector", "VE")
            em.emit("vector", "tensor_mul", QQ[0:1, :], SROW[0:1, 0:GS], RR[0:1, :],
                    inc=("VE", 1))
            em.wait("sync", "VE")
            em.emit("sync", "dma_start", out=rows_next[0:1, g0:g0 + GS],
                    in_=RR[0:1, :], inc=("ST", 16))
            em.emit("sync", "dma_start", out=rows_next[1:2, g0:g0 + GS],
                    in_=QQ[0:1, :], inc=("ST", 16))

    em.wait("sync", "ST")


def build_C(n_cores=8, dbg=False):
    spec = C_DRAMS(n_cores)
    if dbg:
        spec["kvdbg"] = ([64, 2 * 1056], F32, "out")
    nc, em, ar, dio, pb = _standalone(spec)
    emit_C(nc, em, ar, dio, pb, n_cores=n_cores, dbg=dbg)
    em.finalize(nc)
    return nc


def _kv_sel_mats(n_cores):
    out = []
    for c in range(n_cores):
        m = np.zeros((128, 64), np.float32)
        if c % 2 == 1:
            for i in range(64):
                m[i, i] = 1.0
        out.append(m)
    return out


def _prep_C(inputs, n_cores=8):
    f32 = np.float32
    wqkv = np.ascontiguousarray(np.asarray(inputs["L1_attn_W_qkv"], f32))
    wout = np.ascontiguousarray(np.asarray(inputs["L1_attn_W_out"], f32))
    utm = np.triu(np.ones((128, 128), f32))
    idn = np.eye(128, dtype=f32)
    idlo = np.zeros((128, 64), f32)
    for i in range(64):
        idlo[64 + i, i] = 1.0
    one = np.ones((128, 128), f32)
    sel16 = np.zeros((16, 16 * 64), f32)
    for h in range(16):
        sel16[h, h * 64:(h + 1) * 64] = 1.0
    mkv = _kv_sel_mats(n_cores)
    shared = dict(wqkv=wqkv, wout=wout, utm=utm, idn=idn, idlo=idlo, one=one,
                  sel16=sel16)
    return [dict(mkv=mkv[c], **shared) for c in range(n_cores)]





def emit_D(nc, em, ar, dio, pb, skip_prepass=False):
    x3_d = dio["x3"]; rwt_d = dio["rwt"]; rb_d = dio["rb"]
    w1e_d = dio["w1e"]; b1t_d = dio["b1t"]; w2e_d = dio["w2e"]
    b2_d = dio["b2"]; sel32_d = dio["sel32"]; idn_d = dio["idn"]
    one_d = dio["one"]; x4_d = dio["x4"]; rows_d = dio["rows_d4"]

    sb = ar.alloc
    XT = sb("XT", [128, FC * GS], F32)
    XT2 = sb("XT2", [128, FC * GS], F32)
    XLF = sb("XLF", [128, FC * GS], F32)
    XLN = sb("XLN", [128, FC * GS], BF16)
    HM = sb("HM", [128, 32 * 1024], BF16)
    TTS = sb("TTS", [128, 5 * GS], F32)
    W1E = sb("W1E", [128, 3 * 2048], BF16)
    W2E = sb("W2E", [128, 3 * 2048], BF16)
    B2R = sb("B2R", [32, 1024], BF16)
    B1T = sb("B1T", [128, 64], F32)
    RWT = sb("RWT", [128, FC * 32], F32)
    RB = sb("RB", [32, 1], F32)
    LG = sb("LG", [32, GS], F32)
    LGT = sb("LGT", [128, 4 * 32], F32)
    MX = sb("MX", [128, 4], F32)
    OHT = sb("OHT", [128, 4 * 32], F32)
    OHE = sb("OHE", [32, GS], BF16)
    SEL32 = sb("SEL32", [32, 32 * 128], BF16)
    SCR = sb("SCR", [128, 4 * GS], F32)
    OT = sb("OT", [128, 1024], F32)
    X4B = sb("X4B", [128, 1024], F32)
    IDF = sb("IDF", [128, 128], F32)
    ONEF = sb("ONEF", [128, 128], F32)
    ONEB = sb("ONEB", [128, 128], BF16)
    RR = sb("RR", [1, GS], F32)
    QQ = sb("QQ", [1, GS], F32)
    RR2 = sb("RR2", [1, GS], F32)
    QQ2 = sb("QQ2", [1, GS], F32)
    T1 = sb("T1", [1, GS], F32)
    T2 = sb("T2", [1, GS], F32)
    SROW = sb("SROW", [1, 2 * GS], F32)
    CST = sb("CST", [128, 4], F32)
    XBC = HM[:].bitcast(BF16)[:, 0:FC * GS]
    SQS = HM[:].bitcast(BF16)[:, FC * GS:2 * FC * GS]

    def XTb(p):
        return XT if p == 0 else XT2

    def RRb(p):
        return RR if p == 0 else RR2

    def QQb(p):
        return QQ if p == 0 else QQ2

    def _load_tile(t_):
        p = t_ % 2
        g0_ = t_ * GS
        em.emit("sync", "dma_start", out=RRb(p)[0:1, :],
                in_=rows_d[0:1, g0_:g0_ + GS], inc=("LR%d" % p, 16))
        em.emit("sync", "dma_start", out=QQb(p)[0:1, :],
                in_=rows_d[1:2, g0_:g0_ + GS], inc=("LR%d" % p, 16))
        for f in range(FC):
            em.emit("sync", "dma_start", out=XTb(p)[:, f * GS:(f + 1) * GS],
                    in_=x3_d[f * 128:(f + 1) * 128, g0_:g0_ + GS],
                    inc=("LX%d" % p, 16))

    for dst, src_ in ((B1T, b1t_d), (RWT, None), (RB, rb_d), (IDF, idn_d),
                      (ONEF, one_d)):
        if src_ is not None:
            em.emit("sync", "dma_start", out=dst[:], in_=src_[:], inc=("LDI", 16))
    for ic in range(FC):
        em.emit("sync", "dma_start", out=RWT[:, ic * 32:(ic + 1) * 32],
                in_=rwt_d[ic * 128:(ic + 1) * 128, :], inc=("LDI", 16))
    em.emit("gpsimd", "dma_start", out=ONEB[:], in_=one_d[:], inc=("LWI", 16))
    em.emit("gpsimd", "dma_start", out=SEL32[:], in_=sel32_d[:], inc=("LWI", 16))
    em.emit("gpsimd", "dma_start", out=B2R[:], in_=b2_d[:], inc=("LWI", 16))
    em.emit("vector", "memset", CST[:, 0:1], EPS_LN, inc=("VE", 1))
    for e_ in ("tensor", "vector", "scalar"):
        em.wait(e_, "LDI")
        em.wait(e_, "LWI")

    for g in (() if skip_prepass else range(NG)):
        em.wait("sync", "PE")
        em.wait("sync", "SE")
        for f in range(FC):
            em.emit("sync", "dma_start", out=XT[:, f * GS:(f + 1) * GS],
                    in_=x3_d[f * 128:(f + 1) * 128, g * GS:(g + 1) * GS],
                    inc=("LXP", 16))
        tiles = [XT[:, f * GS:(f + 1) * GS] for f in range(FC)]
        _ln_stats(em, pb, tiles, ONEB, XBC, SQS, SROW, T1, T2,
                  RR[0:1, :], QQ[0:1, :], CST, ld_waits=("LXP",))
        em.wait("sync", "VE")
        em.emit("sync", "dma_start", out=rows_d[0:1, g * GS:(g + 1) * GS],
                in_=RR[0:1, :], inc=("ST", 16))
        em.emit("sync", "dma_start", out=rows_d[1:2, g * GS:(g + 1) * GS],
                in_=QQ[0:1, :], inc=("ST", 16))

    pe_w1 = [0, 0, 0]
    pe_w2 = [0, 0, 0]
    mb_ve = [0, 0]
    dbank_se = {(p, j): 0 for p in range(2) for j in range(2)}
    def _ln_waves(t_):
        p = t_ % 2
        em.wait("tensor", "LR%d" % p)
        em.wait("tensor", "VE")
        em.wait("tensor", "SE")
        em.emit("tensor", "matmul", pb[4][:], ONEF[0:1, :], RRb(p)[0:1, :],
                start=True, stop=True)
        em.emit("tensor", "matmul", pb[5][:], ONEF[0:1, :], QQb(p)[0:1, :],
                start=True, stop=True, inc=("PE", 1))
        em.wait("vector", "PE")
        em.wait("vector", "LX%d" % p)
        for w_ in range(2):
            em.wait("vector", "VE")
            for f4 in range(4):
                f = w_ * 4 + f4
                em.emit("vector", "tensor_mul",
                        TTS[:, (1 + f4) * GS:(2 + f4) * GS],
                        XTb(p)[:, f * GS:(f + 1) * GS], pb[4][:],
                        inc=("VE", 1) if f4 == 3 else None)
            em.wait("vector", "VE")
            for f4 in range(4):
                f = w_ * 4 + f4
                em.emit("vector", "tensor_sub", XLF[:, f * GS:(f + 1) * GS],
                        TTS[:, (1 + f4) * GS:(2 + f4) * GS], pb[5][:],
                        inc=("VE", 1) if f4 == 3 else None)
        em.wait("vector", "VE")
        for f in range(FC):
            em.emit("vector", "tensor_copy", XLN[:, f * GS:(f + 1) * GS],
                    XLF[:, f * GS:(f + 1) * GS],
                    inc=("VE", 1) if f == FC - 1 else None)

    em.wait("sync", "PE")
    em.wait("sync", "VE")
    em.wait("sync", "ST")
    _load_tile(0)
    _ln_waves(0)
    for t in range(NG):
        par = t % 2
        lx = "LX%d" % par
        g0 = t * GS
        if t + 1 < NG:
            em.wait("sync", "PE")
            em.wait("sync", "VE")
            em.wait("sync", "ST")
            _load_tile(t + 1)
        em.wait("tensor", "VE")
        em.wait("tensor", "LDI")
        for ic in range(FC):
            em.emit("tensor", "matmul", pb[0][0:32, 0:GS],
                    RWT[:, ic * 32:(ic + 1) * 32], XLF[:, ic * GS:(ic + 1) * GS],
                    start=(ic == 0), stop=(ic == FC - 1),
                    inc=("PE", 1) if ic == FC - 1 else None)
        em.wait("vector", "PE")
        em.emit("vector", "tensor_scalar", LG[:], pb[0][0:32, 0:GS], RB[:],
                None, ALU.add, inc=("VE", 1))
        for q in range(4):
            em.wait("tensor", "VE")
            em.emit("tensor", "transpose", pb[1][0:128, 0:32],
                    LG[:, q * 128:(q + 1) * 128], IDF[0:32, 0:32], inc=("PE", 1))
            em.wait("vector", "PE")
            em.emit("vector", "tensor_copy", LGT[:, q * 32:(q + 1) * 32],
                    pb[1][0:128, 0:32], inc=("VE", 1))
        em.wait("vector", "VE")
        em.emit("vector", "tensor_reduce", MX[:],
                LGT[:].rearrange("p (q e) -> p q e", e=32), AX, ALU.max,
                inc=("VE", 1))
        em.wait("vector", "VE")
        for q in range(4):
            em.emit("vector", "tensor_scalar", OHT[:, q * 32:(q + 1) * 32],
                    LGT[:, q * 32:(q + 1) * 32], MX[:, q:q + 1], None, ALU.is_ge,
                    inc=("VE", 1) if q == 3 else None)
        for q in range(4):
            em.wait("tensor", "VE")
            em.emit("tensor", "transpose", pb[1][0:32, 0:128],
                    OHT[:, q * 32:(q + 1) * 32], IDF[:], inc=("PE", 1))
            em.wait("vector", "PE")
            em.emit("vector", "tensor_copy", OHE[:, q * 128:(q + 1) * 128],
                    pb[1][0:32, 0:128], inc=("VE", 1))
        ve_ohe = em.cnt["VE"]
        for k_ in dbank_se:
            dbank_se[k_] = max(dbank_se[k_], em.cnt["SE"])
        mb_ve = [max(mb_ve[0], em.cnt["VE"]), max(mb_ve[1], em.cnt["VE"])]
        for e in range(32):
            ep = e % 2
            wp = e % 3
            lw = "LW%d" % wp
            em.wait("gpsimd", "PE", pe_w1[wp])
            em.emit("gpsimd", "dma_start",
                    out=W1E[:, wp * 2048:(wp + 1) * 2048],
                    in_=w1e_d[e * 128:(e + 1) * 128, :], inc=(lw, 16))
            em.wait("tensor", lw)
            em.wait("tensor", "LWI")
            em.wait("tensor", "VE", max(ve_ohe, mb_ve[ep]))
            em.emit("tensor", "matmul", pb[4 + ep][:],
                    SEL32[:, e * 128:(e + 1) * 128], OHE[:],
                    start=True, stop=True)
            for jc in range(2):
                bh = 2 * ep + jc
                em.wait("tensor", "SE", dbank_se[(ep, jc)])
                for ic in range(FC):
                    em.emit("tensor", "matmul", pb[bh][:],
                            W1E[:, wp * 2048 + ic * 256 + jc * 128:
                                wp * 2048 + ic * 256 + (jc + 1) * 128],
                            XLN[:, ic * GS:(ic + 1) * GS],
                            start=(ic == 0), stop=(ic == FC - 1),
                            inc=("PE", 1) if ic == FC - 1 else None)
                sc_sl = (2 * ep + jc) * GS
                em.wait("scalar", "PE")
                em.emit("scalar", "activation", SCR[:, sc_sl:sc_sl + GS],
                        pb[bh][:], AF.Relu,
                        bias=B1T[:, jc * 32 + e:jc * 32 + e + 1], inc=("SE", 1))
                em.wait("vector", "SE")
                em.emit("vector", "tensor_mul",
                        HM[:, e * 1024 + jc * GS: e * 1024 + (jc + 1) * GS],
                        SCR[:, sc_sl:sc_sl + GS], pb[4 + ep][:], inc=("VE", 1))
                dbank_se[(ep, jc)] = em.cnt["SE"]
            mb_ve[ep] = em.cnt["VE"]
            pe_w1[wp] = em.cnt["PE"]
        for e in range(32):
            ep = e % 2
            wp = e % 3
            lv = "LV%d" % wp
            em.wait("gpsimd", "PE", pe_w2[wp])
            em.emit("gpsimd", "dma_start",
                    out=W2E[:, wp * 2048:(wp + 1) * 2048],
                    in_=w2e_d[e * 128:(e + 1) * 128, :], inc=(lv, 16))
            em.wait("tensor", lv)
            if e == 0:
                em.wait("tensor", "VE")
                em.wait("tensor", "SE")
            for q in range(4):
                for jc in range(2):
                    for half in range(2):
                        em.emit("tensor", "matmul", pb[2 * q + half][:],
                                HM[:, e * 1024 + jc * GS + q * 128:
                                   e * 1024 + jc * GS + (q + 1) * 128],
                                W2E[:, wp * 2048 + jc * 1024 + half * GS:
                                    wp * 2048 + jc * 1024 + (half + 1) * GS],
                                start=(e == 0 and jc == 0), stop=False,
                                inc=("PE", 1) if (q == 3 and half == 1
                                                  and jc == 1) else None)
            pe_w2[wp] = em.cnt["PE"]
        em.wait("tensor", "LWI")
        for q in range(4):
            for half in range(2):
                bo = 2 * q + half
                em.emit("tensor", "matmul", pb[bo][:],
                        OHE[:, q * 128:(q + 1) * 128],
                        B2R[:, half * GS:(half + 1) * GS],
                        start=False, stop=True,
                        inc=("PE", 1) if (q == 3 and half == 1) else None)
        for q in range(4):
            em.wait("scalar", "PE")
            em.wait("scalar", "ST")
            em.emit("scalar", "activation", OT[:, 0:GS], pb[2 * q][:], AF.Relu)
            em.emit("scalar", "activation", OT[:, GS:2 * GS], pb[2 * q + 1][:],
                    AF.Relu, inc=("SE", 1))
            tr_mk = [0, 0]
            for f in range(FC):
                fb = 2 * q + f % 2
                em.wait("tensor", "SE")
                em.wait("tensor", "VE", tr_mk[f % 2])
                em.emit("tensor", "transpose", pb[fb][0:128, 0:128],
                        XTb(par)[:, f * GS + q * 128: f * GS + (q + 1) * 128],
                        IDF[:], inc=("PE", 1))
                em.wait("vector", "PE")
                em.wait("vector", "ST")
                em.emit("vector", "tensor_add", X4B[:, f * 128:(f + 1) * 128],
                        OT[:, f * 128:(f + 1) * 128], pb[fb][0:128, 0:128],
                        inc=("VE", 1))
                tr_mk[f % 2] = em.cnt["VE"]
            em.wait("sync", "VE")
            em.emit("sync", "dma_start",
                    out=x4_d[g0 + q * 128: g0 + (q + 1) * 128, :], in_=X4B[:],
                    inc=("ST", 16))
            if q == 2 and t + 1 < NG:
                _ln_waves(t + 1)

    em.wait("sync", "ST")


def build_D():
    nc, em, ar, dio, pb = _standalone(D_DRAMS())
    emit_D(nc, em, ar, dio, pb)
    em.finalize(nc)
    return nc


def _prep_D(inputs):
    f32 = np.float32
    rwt = np.ascontiguousarray(np.asarray(inputs["L1_ffn_router_w"], f32).T)
    rb = np.ascontiguousarray(np.asarray(inputs["L1_ffn_router_b"], f32).reshape(32, 1))
    w1e = np.ascontiguousarray(
        np.asarray(inputs["L1_ffn_weights1"], f32).reshape(32, 8, 128, 256)
        .transpose(0, 2, 1, 3).reshape(32 * 128, 2048))
    b1t = np.ascontiguousarray(
        np.asarray(inputs["L1_ffn_bias1"], f32).T.reshape(2, 128, 32)
        .transpose(1, 0, 2).reshape(128, 64))
    w2e = np.ascontiguousarray(
        np.asarray(inputs["L1_ffn_weights2"], f32).reshape(32, 2, 128, 1024)
        .transpose(0, 2, 1, 3).reshape(32 * 128, 2048))
    b2 = np.ascontiguousarray(np.asarray(inputs["L1_ffn_bias2"], f32))
    sel32 = np.zeros((32, 32 * 128), f32)
    for e in range(32):
        sel32[e, e * 128:(e + 1) * 128] = 1.0
    idn = np.eye(128, dtype=f32)
    one = np.ones((128, 128), f32)
    return dict(rwt=rwt, rb=rb, w1e=w1e, b1t=b1t, w2e=w2e, b2=b2,
                sel32=sel32, idn=idn, one=one)


def _phase_barrier(em):
    for eng in ENGINES:
        for s in em.cnt:
            em.wait(eng, s)


def build_fused(n_cores=8):
    nc = bass.Bass("TRN2", target_bir_lowering=False, debug=False)
    spec = {}
    spec.update(A_DRAMS(n_cores))
    spec.update(B_DRAMS())
    spec.update(C_DRAMS(n_cores))
    spec.update(D_DRAMS())
    for k in ("x1", "x2", "x3"):
        shape, dt_, _ = spec[k]
        spec[k] = (shape, dt_, "int")
    dio = _mk_drams(nc, spec)
    ar = Ar(nc)
    pb = [nc.alloc_psum_tensor(f"pb{i}", [128, 512], F32) for i in range(8)]
    em = Em()
    for s in ALL_SEMS:
        em.sem(s)
    emit_A(nc, em, ar, dio, pb, n_cores=n_cores, rows_next=dio["rows_b"])
    _phase_barrier(em)
    ar.reset()
    emit_B(nc, em, ar, dio, pb, skip_prepass=True, rows_next=dio["rows_c"])
    _phase_barrier(em)
    ar.reset()
    emit_C(nc, em, ar, dio, pb, n_cores=n_cores, skip_prepass=True,
           rows_next=dio["rows_d4"])
    _phase_barrier(em)
    ar.reset()
    emit_D(nc, em, ar, dio, pb, skip_prepass=True)
    em.finalize(nc)
    return nc


def _prep_fused(inputs, n_cores=8):
    maps_a = _prep_A(inputs, n_cores=n_cores)
    shared_b = _prep_B(inputs)
    maps_c = _prep_C(inputs, n_cores=n_cores)
    shared_d = _prep_D(inputs)
    maps = []
    for c in range(n_cores):
        m = {}
        m.update(maps_a[c])
        m.update(shared_b)
        m.update(maps_c[c])
        m.update(shared_d)
        m.pop("x2", None)
        m.pop("x3", None)
        maps.append(m)
    return maps


_NCORES = 8
_FUSED = True


def kernel_fused(**inputs):
    n = _NCORES
    maps = _prep_fused(inputs, n_cores=n)
    nc = build_fused(n_cores=n)
    res = run_bass_kernel_spmd(nc, maps, core_ids=list(range(n)))
    out = np.empty((4, 8192, 1024), np.float32)
    for c in range(n):
        b, hh = c // 2, c % 2
        out[b, hh * TOK:(hh + 1) * TOK, :] = res.results[c]["x4"]
    return out


def kernel(**inputs):
    if _FUSED:
        return kernel_fused(**inputs)
    return kernel_unfused(**inputs)


def kernel_unfused(**inputs):
    n = _NCORES
    maps_a = _prep_A(inputs, n_cores=n)
    shared_b = _prep_B(inputs)
    maps_c = _prep_C(inputs, n_cores=n)
    shared_d = _prep_D(inputs)

    nc_a = build_A(n_cores=n)
    res_a = run_bass_kernel_spmd(nc_a, maps_a, core_ids=list(range(n)))
    x1s = [r["x1"] for r in res_a.results]

    nc_b = build_B()
    res_b = run_bass_kernel_spmd(
        nc_b, [dict(x1=x1s[c], **shared_b) for c in range(n)],
        core_ids=list(range(n)))
    x2s = [r["x2"] for r in res_b.results]

    nc_c = build_C(n_cores=n)
    for c in range(n):
        maps_c[c]["x2"] = x2s[c]
    res_c = run_bass_kernel_spmd(nc_c, maps_c, core_ids=list(range(n)))
    x3s = [r["x3"] for r in res_c.results]

    nc_d = build_D()
    res_d = run_bass_kernel_spmd(
        nc_d, [dict(x3=x3s[c], **shared_d) for c in range(n)],
        core_ids=list(range(n)))

    out = np.empty((4, 8192, 1024), np.float32)
    for c in range(n):
        b, hh = c // 2, c % 2
        out[b, hh * TOK:(hh + 1) * TOK, :] = res_d.results[c]["x4"]
    return out


# version 25
# speedup vs baseline: 1.1878x; 1.1878x over previous
import numpy as np
import concourse.bass as bass
import concourse.mybir as mybir
from concourse.bass_utils import run_bass_kernel_spmd

F32 = mybir.dt.float32
BF16 = mybir.dt.bfloat16
AF = mybir.ActivationFunctionType
ALU = mybir.AluOpType
AX = mybir.AxisListType.X

ENGINES = ("sync", "gpsimd", "tensor", "vector", "scalar")

TOK = 4096
GS = 512
NG = TOK // GS
D = 1024
FC = D // 128
EPS_LN = 1e-5
EPS_ATT = 1e-6


class Em:

    def __init__(self):
        self.streams = {e: [] for e in ENGINES}
        self.cnt = {}
        self.sems = {}
        self._lastwait = {}
        self.marks = {}

    def sem(self, name):
        self.cnt.setdefault(name, 0)
        return name

    def emit(self, engine, op, *args, inc=None, **kwargs):
        if inc is not None:
            self.cnt[inc[0]] += inc[1]

        def run(eng, op=op, args=args, kwargs=kwargs, inc=inc, em=self):
            r = getattr(eng, op)(*args, **kwargs)
            if inc is not None:
                r.then_inc(em.sems[inc[0]], inc[1])
        self.streams[engine].append(run)

    def wait(self, engine, sem, val=None):
        v = self.cnt.get(sem, 0) if val is None else val
        if v <= 0:
            return
        key = (engine, sem)
        if self._lastwait.get(key, -1) >= v:
            return
        self._lastwait[key] = v

        def run(eng, em=self, s=sem, v=v):
            eng.wait_ge(em.sems[s], v)
        self.streams[engine].append(run)

    def finalize(self, nc):
        import contextlib
        with contextlib.ExitStack() as st:
            for name in self.cnt:
                self.sems[name] = st.enter_context(nc.semaphore(name))
            with nc.Block() as block:
                for engname in ENGINES:
                    stream = self.streams[engname]
                    getattr(block, engname)(
                        lambda e, stream=stream: [f(e) for f in stream])


def _ln_stats(em, pb, tiles_f32, ONEB, XBC, SQS, SROW, T1, T2, rdst, qdst, epsc,
              ld_waits=(), tiles_ve=None, precise=False, rows_gp=False,
              sq_gp=False, rows_sqmean=False):
    mk = em.marks
    for s_ in ld_waits:
        em.wait("vector", s_)
    em.wait("vector", "PE", mk.get("stats_pe") if precise else None)
    em.wait("vector", "ST")
    em.wait("vector", "VE", tiles_ve if precise else None)
    for f in range(FC):
        em.emit("vector", "tensor_copy", XBC[:, f * GS:(f + 1) * GS], tiles_f32[f],
                inc=("VE", 1) if f == FC - 1 else None)
    ve_cast = em.cnt["VE"]
    if sq_gp:
        for s_ in ld_waits:
            em.wait("gpsimd", s_)
        em.wait("gpsimd", "PE", mk.get("stats_pe") if precise else None)
        em.wait("gpsimd", "VE", tiles_ve if precise else None)
        for f in range(FC):
            em.emit("gpsimd", "tensor_mul", SQS[:, f * GS:(f + 1) * GS],
                    tiles_f32[f], tiles_f32[f],
                    inc=("GP", 1) if f == FC - 1 else None)
        gp_sq = em.cnt["GP"]
        se_sq = em.cnt["SE"]
    else:
        for s_ in ld_waits:
            em.wait("scalar", s_)
        em.wait("scalar", "PE", mk.get("stats_pe") if precise else None)
        em.wait("scalar", "VE", tiles_ve if precise else None)
        for f in range(FC):
            em.emit("scalar", "activation", SQS[:, f * GS:(f + 1) * GS],
                    tiles_f32[f], AF.Square,
                    inc=("SE", 1) if f == FC - 1 else None)
        gp_sq = None
        se_sq = em.cnt["SE"]
    em.wait("tensor", "VE", max(ve_cast, mk.get("rows_ve", 0)) if precise else None)
    em.wait("tensor", "SE", se_sq if precise else None)
    if sq_gp:
        em.wait("tensor", "GP", gp_sq)
    em.wait("tensor", "LDW")
    em.wait("tensor", "LWI")
    em.wait("tensor", "LDI")
    for f in range(FC):
        em.emit("tensor", "matmul", pb[0][0:1, 0:GS], ONEB[:, 0:1],
                XBC[:, f * GS:(f + 1) * GS], start=(f == 0), stop=(f == FC - 1))
    for f in range(FC):
        em.emit("tensor", "matmul", pb[1][0:1, 0:GS], ONEB[:, 0:1],
                SQS[:, f * GS:(f + 1) * GS], start=(f == 0), stop=(f == FC - 1),
                inc=("PE", 1) if f == FC - 1 else None)
    mk["stats_pe"] = em.cnt["PE"]
    em.wait("vector", "PE", mk["stats_pe"] if precise else None)
    em.wait("vector", "VE", mk.get("rows_ve", 0) if precise else None)
    if rows_gp:
        em.wait("vector", "SE", mk.get("rows_se", 0))
        em.wait("vector", "GP", mk.get("rows_gp", 0))
    em.emit("vector", "tensor_scalar", SROW[0:1, 0:GS], pb[0][0:1, 0:GS],
            1.0 / D, None, ALU.mult)
    em.emit("vector", "tensor_scalar", SROW[0:1, GS:2 * GS], pb[1][0:1, 0:GS],
            1.0 / D, None, ALU.mult, inc=("VE", 1))
    mk["rows_ve"] = em.cnt["VE"]
    if rows_gp:
        em.wait("scalar", "GP", mk.get("rows_gp", 0))
        em.wait("scalar", "PE", mk.get("rows_cons_pe"))
        em.wait("scalar", "ST")
        em.wait("scalar", "VE")
        em.emit("scalar", "activation", T1[0:1, :], SROW[0:1, 0:GS], AF.Square,
                inc=("SE", 1))
        em.wait("gpsimd", "SE")
        em.wait("gpsimd", "VE")
        em.emit("gpsimd", "tensor_sub", T2[0:1, :], SROW[0:1, GS:2 * GS],
                T1[0:1, :], inc=("GP", 1))
        if rows_sqmean:
            em.wait("scalar", "GP")
            em.wait("scalar", "ST")
            em.emit("scalar", "activation", rdst, T2[0:1, :], AF.Sqrt,
                    bias=epsc[0:1, 0:1], inc=("SE", 1))
            em.wait("gpsimd", "PE", mk.get("rows_cons_pe"))
            em.wait("gpsimd", "ST")
            em.emit("gpsimd", "tensor_copy", qdst, SROW[0:1, 0:GS],
                    inc=("GP", 1))
            mk["rows_se"] = em.cnt["SE"]
            mk["rows_gp"] = em.cnt["GP"]
            mk["rows_ve"] = em.cnt["VE"]
            return
        em.wait("scalar", "GP")
        em.emit("scalar", "activation", T1[0:1, :], T2[0:1, :], AF.Sqrt,
                bias=epsc[0:1, 0:1], inc=("SE", 1))
        em.wait("vector", "SE")
        em.wait("vector", "PE", mk.get("rows_cons_pe"))
        em.emit("vector", "reciprocal", rdst, T1[0:1, :], inc=("VE", 1))
        em.wait("gpsimd", "VE")
        em.wait("gpsimd", "PE", mk.get("rows_cons_pe"))
        em.wait("gpsimd", "ST")
        em.emit("gpsimd", "tensor_mul", qdst, SROW[0:1, 0:GS], rdst,
                inc=("GP", 1))
        mk["rows_se"] = em.cnt["SE"]
        mk["rows_gp"] = em.cnt["GP"]
        mk["rows_ve"] = em.cnt["VE"]
        return
    em.wait("scalar", "VE")
    em.emit("scalar", "activation", T1[0:1, :], SROW[0:1, 0:GS], AF.Square,
            inc=("SE", 1))
    em.wait("vector", "SE")
    em.emit("vector", "tensor_sub", T2[0:1, :], SROW[0:1, GS:2 * GS], T1[0:1, :],
            inc=("VE", 1))
    if rows_sqmean:
        em.wait("scalar", "VE")
        em.wait("scalar", "ST")
        em.emit("scalar", "activation", rdst, T2[0:1, :], AF.Sqrt,
                bias=epsc[0:1, 0:1], inc=("SE", 1))
        em.wait("vector", "ST")
        em.emit("vector", "tensor_copy", qdst, SROW[0:1, 0:GS], inc=("VE", 1))
        mk["rows_ve"] = em.cnt["VE"]
        return
    em.wait("scalar", "VE")
    em.emit("scalar", "activation", T1[0:1, :], T2[0:1, :], AF.Sqrt,
            bias=epsc[0:1, 0:1], inc=("SE", 1))
    em.wait("vector", "SE")
    em.wait("vector", "PE", mk.get("rows_cons_pe") if precise else None)
    em.emit("vector", "reciprocal", rdst, T1[0:1, :], inc=("VE", 1))
    em.wait("vector", "VE")
    em.emit("vector", "tensor_mul", qdst, SROW[0:1, 0:GS], rdst, inc=("VE", 1))
    mk["rows_ve"] = em.cnt["VE"]


class Ar:

    def __init__(self, nc, nbytes=212800):
        self.nbytes = nbytes
        self.t = nc.alloc_sbuf_tensor("ARENA", [128, nbytes], mybir.dt.uint8)
        self.off = 0

    def alloc(self, name, shape, dtype):
        esz = mybir.dt.size(dtype)
        nb = shape[1] * esz
        off = (self.off + 63) & ~63
        self.off = off + nb
        assert self.off <= self.nbytes, (
            f"arena overflow at {name}: {self.off} > {self.nbytes}")
        view = self.t[:].bitcast(dtype)
        return view[0:shape[0], off // esz:(off // esz) + shape[1]]

    def reset(self):
        self.off = 0


def _mk_drams(nc, spec):
    dio = {}
    for name, (shape, dtype, kind) in spec.items():
        if kind == "in":
            dio[name] = nc.dram_tensor(name, shape, dtype, kind="ExternalInput").ap()
        elif kind == "out":
            dio[name] = nc.dram_tensor(name, shape, dtype, kind="ExternalOutput").ap()
        else:
            dio[name] = nc.dram_tensor(name, shape, dtype, kind="Internal",
                                       addr_space="Local").ap()
    return dio


def _standalone(spec):
    nc = bass.Bass("TRN2", target_bir_lowering=False, debug=False)
    dio = _mk_drams(nc, spec)
    ar = Ar(nc)
    pb = [nc.alloc_psum_tensor(f"pb{i}", [128, 512], F32) for i in range(8)]
    em = Em()
    for s in ALL_SEMS:
        em.sem(s)
    return nc, em, ar, dio, pb


ALL_SEMS = ("LDI", "LDW", "LWI", "LDR", "LXP", "LX0", "LX1", "LW0", "LW1",
            "LW2", "LB0", "LB1", "LB2", "LV0", "LV1", "LV2", "R30", "R31",
            "LNU", "SPQ", "ST", "GD", "CC", "PE", "VE", "SE", "GP",
            "LR0", "LR1", "SP0", "SP1", "NU0", "NU1", "LWO")


def A_DRAMS(n_cores=8):
    return {
        "x0": ([D, TOK], F32, "in"), "wrt": ([D, D], F32, "in"),
        "wft": ([D, D], F32, "in"), "attf": ([1, D], F32, "in"),
        "gam": ([128, FC], F32, "in"), "bet": ([128, FC], F32, "in"),
        "idn": ([128, 128], F32, "in"), "one": ([128, 128], F32, "in"),
        "msf": ([16, 8], F32, "in"), "mpt": ([16, 8], F32, "in"),
        "x1": ([D, TOK], F32, "out"),
        "cc_in_a": ([8, 136], F32, "int"), "cc_out_a": ([16, 136], F32, "int"),
        "rows_a": ([2, TOK], F32, "int"),
    }


def B_DRAMS():
    return {
        "x1": ([D, TOK], F32, "in"),
        "w1t": ([32 * 128, 1024], F32, "in"), "w2t": ([32 * 128, 1024], F32, "in"),
        "h1t": ([32 * 128, 4096], F32, "in"), "h2t": ([32 * 128, 4096], F32, "in"),
        "w3t": ([8 * 128, 4096], F32, "in"),
        "s1c": ([128, 32], F32, "in"), "s2c": ([128, 32], F32, "in"),
        "sh1c": ([128, 32], F32, "in"), "sh2c": ([128, 32], F32, "in"),
        "sw3c": ([128, 8], F32, "in"), "one": ([128, 128], F32, "in"),
        "x2": ([D, TOK], F32, "out"), "rows_b": ([2, TOK], F32, "int"),
    }


def C_DRAMS(n_cores=8):
    return {
        "x2": ([D, TOK], F32, "in"), "wqkv": ([D, 3 * D], F32, "in"),
        "wout": ([D, D], F32, "in"), "utm": ([128, 128], F32, "in"),
        "idn": ([128, 128], F32, "in"), "idlo": ([128, 64], F32, "in"),
        "one": ([128, 128], F32, "in"), "mkv": ([128, 64], F32, "in"),
        "sel16": ([16, 16 * 64], F32, "in"),
        "x3": ([D, TOK], F32, "out"),
        "rows_c": ([2, TOK], F32, "int"),
        "pq_sc": ([64, NH * TOK], BF16, "int"),
        "nu_sc": ([64, NH * TOK], BF16, "int"),
        "kv_in": ([64, 1056], F32, "int"), "kv_out": ([128, 1056], F32, "int"),
    }


def D_DRAMS():
    return {
        "x3": ([D, TOK], F32, "in"), "rwt": ([D, 32], F32, "in"),
        "rb": ([32, 1], F32, "in"), "w1e": ([32 * 128, 2048], F32, "in"),
        "b1t": ([128, 64], F32, "in"), "w2e": ([32 * 128, 2048], F32, "in"),
        "b2": ([32, 1024], F32, "in"), "sel32": ([32, 32 * 128], F32, "in"),
        "idn": ([128, 128], F32, "in"), "one": ([128, 128], F32, "in"),
        "x4": ([TOK, D], F32, "out"), "rows_d4": ([2, TOK], F32, "int"),
    }



def emit_A(nc, em, ar, dio, pb, n_cores=8, rows_next=None):
    RG = [[a, a + 1] for a in range(0, n_cores, 2)]
    x0_d = dio["x0"]; wrt_d = dio["wrt"]; wft_d = dio["wft"]
    attf_d = dio["attf"]; gam_d = dio["gam"]; bet_d = dio["bet"]
    idn_d = dio["idn"]; one_d = dio["one"]; msf_d = dio["msf"]
    mpt_d = dio["mpt"]; x1_d = dio["x1"]
    cc_in = dio["cc_in_a"]; cc_out = dio["cc_out_a"]; rows_d = dio["rows_a"]

    sb = ar.alloc
    WRT = sb("WRT", [128, FC * D], BF16)
    WFT = sb("WFT", [128, FC * D], BF16)
    XT = sb("XT", [128, 2 * FC * GS], F32)
    XBC = sb("XBC", [128, FC * GS], BF16)
    SQS = sb("SQS", [128, FC * GS], BF16)
    TTS = sb("TTS", [128, FC * GS], F32)
    HBB = sb("HBB", [128, FC * GS], BF16)
    FRS = sb("FRS", [128, D], F32)
    LKS = sb("LKS", [128, D], F32)
    AAS = sb("AAS", [128, D], F32)
    SRT = sb("SRT", [128, 32 * 8], F32)
    FTM = sb("FTM", [128, 32 * D], BF16)
    SRW = sb("SRW", [8, 256], F32)
    SRW2 = sb("SRW2", [8, 256], F32)
    SMAX = sb("SMAX", [8, 32], F32)
    SSUM = sb("SSUM", [8, 32], F32)
    SM = sb("SM", [8, 16], F32)
    WCOL = sb("WCOL", [128, 32 * 8], BF16)
    RR = sb("RR", [1, GS], F32)
    QQ = sb("QQ", [1, GS], F32)
    T1 = sb("T1", [1, GS], F32)
    T2 = sb("T2", [1, GS], F32)
    SROW = sb("SROW", [1, 2 * GS], F32)
    ATTB = sb("ATTB", [128, D], F32)
    ONEF = sb("ONEF", [128, 128], F32)
    ONEB = sb("ONEB", [128, 128], BF16)
    IDF = sb("IDF", [128, 128], F32)
    IDB = sb("IDB", [128, 128], BF16)
    GAMS = sb("GAMS", [128, FC], F32)
    BETS = sb("BETS", [128, FC], F32)
    MSF = sb("MSF", [16, 8], F32)
    MPT = sb("MPT", [16, 8], F32)
    PKT = sb("PKT", [8, 136], F32)
    GSB = sb("GSB", [16, 136], F32)
    ABS = sb("ABS", [8, 272], F32)
    CMB = sb("CMB", [8, 160], F32)
    CTXB = sb("CTXB", [8, 128], BF16)
    CTXC = sb("CTXC", [128, 8], BF16)
    FHC = sb("FHC", [128, FC], F32)
    VF = sb("VF", [128, 8], F32)
    CST = sb("CST", [128, 4], F32)
    VR = sb("VR", [8, 128], F32)
    YTv = FTM[:].bitcast(F32)

    def YTb(buf, f):
        return YTv[:, (buf * FC + f) * GS:(buf * FC + f + 1) * GS]

    def X(buf, f):
        return XT[:, (buf * FC + f) * GS:(buf * FC + f + 1) * GS]

    for ic in range(FC):
        em.emit("gpsimd", "dma_start", out=WRT[:, ic * D:(ic + 1) * D],
                in_=wrt_d[ic * 128:(ic + 1) * 128, :], inc=("LDW", 16))
        em.emit("gpsimd", "dma_start", out=WFT[:, ic * D:(ic + 1) * D],
                in_=wft_d[ic * 128:(ic + 1) * 128, :], inc=("LDW", 16))
    em.emit("gpsimd", "dma_start", out=ONEB[:], in_=one_d[:], inc=("LDW", 16))
    em.emit("gpsimd", "dma_start", out=IDB[:], in_=idn_d[:], inc=("LDW", 16))
    for dst, src in ((ONEF, one_d), (IDF, idn_d), (GAMS, gam_d), (BETS, bet_d),
                     (MSF, msf_d), (MPT, mpt_d)):
        em.emit("sync", "dma_start", out=dst[:], in_=src[:], inc=("LDI", 16))
    em.emit("sync", "dma_start", out=SROW[0:1, 0:D], in_=attf_d[:], inc=("LDI", 16))
    em.wait("tensor", "LDI")
    em.emit("tensor", "matmul", pb[2][:], ONEF[0:1, :], SROW[0:1, 0:512],
            start=True, stop=True)
    em.emit("tensor", "matmul", pb[3][:], ONEF[0:1, :], SROW[0:1, 512:1024],
            start=True, stop=True, inc=("PE", 1))
    em.wait("vector", "PE")
    em.emit("vector", "memset", CST[:, 0:1], EPS_LN)
    em.emit("vector", "tensor_copy", ATTB[:, 0:512], pb[2][:])
    em.emit("vector", "tensor_copy", ATTB[:, 512:1024], pb[3][:], inc=("VE", 1))

    war_x = [0, 0]
    fr_se = [0, 0]

    def _load_x0(g_):
        b_ = g_ % 2
        em.wait("sync", "VE", war_x[b_])
        for f in range(FC):
            em.emit("sync", "dma_start", out=X(b_, f),
                    in_=x0_d[f * 128:(f + 1) * 128, g_ * GS:(g_ + 1) * GS],
                    inc=("LX%d" % b_, 16))

    _load_x0(0)
    for g in range(NG):
        buf = g % 2
        if g + 1 < NG:
            _load_x0(g + 1)
        tiles = [X(buf, f) for f in range(FC)]
        _ln_stats(em, pb, tiles, ONEB, XBC, SQS, SROW, T1, T2,
                  RR[0:1, :], QQ[0:1, :], CST, ld_waits=("LX%d" % buf,),
                  tiles_ve=0, precise=True, rows_gp=True)
        em.wait("tensor", "VE")
        em.wait("tensor", "GP", em.marks["rows_gp"])
        em.emit("tensor", "matmul", pb[2][:], ONEF[0:1, :], RR[0:1, :],
                start=True, stop=True)
        em.emit("tensor", "matmul", pb[3][:], ONEF[0:1, :], QQ[0:1, :],
                start=True, stop=True, inc=("PE", 1))
        em.marks["rows_cons_pe"] = em.cnt["PE"]
        em.wait("sync", "VE")
        em.wait("sync", "GP", em.marks["rows_gp"])
        em.emit("sync", "dma_start", out=rows_d[0:1, g * GS:(g + 1) * GS],
                in_=RR[0:1, :], inc=("ST", 16))
        em.emit("sync", "dma_start", out=rows_d[1:2, g * GS:(g + 1) * GS],
                in_=QQ[0:1, :], inc=("ST", 16))
        em.wait("vector", "PE")
        for f in range(FC):
            em.emit("vector", "tensor_mul", TTS[:, f * GS:(f + 1) * GS],
                    X(buf, f), pb[2][:], inc=("VE", 1) if f == FC - 1 else None)
        em.wait("vector", "VE")
        for f in range(FC):
            em.emit("vector", "tensor_sub", HBB[:, f * GS:(f + 1) * GS],
                    TTS[:, f * GS:(f + 1) * GS], pb[3][:],
                    inc=("VE", 1) if f == FC - 1 else None)
        war_x[buf] = em.cnt["VE"]
        ve_hbb = em.cnt["VE"]
        for s in range(4):
            c = g * 4 + s
            bk = 4 if s % 2 == 0 else 6
            em.wait("tensor", "VE", ve_hbb)
            em.wait("tensor", "SE", fr_se[s % 2])
            for ic in range(FC):
                for half in range(2):
                    em.emit("tensor", "matmul", pb[bk + half][:],
                            HBB[:, ic * GS + s * 128: ic * GS + (s + 1) * 128],
                            WRT[:, ic * D + half * 512: ic * D + (half + 1) * 512],
                            start=(ic == 0), stop=(ic == FC - 1),
                            inc=("PE", 1) if (half == 1 and ic == FC - 1) else None)
            em.wait("scalar", "PE")
            em.wait("scalar", "VE")
            for half in range(2):
                em.emit("scalar", "activation", FRS[:, half * 512:(half + 1) * 512],
                        pb[bk + half][:], AF.Copy, inc=("SE", 1) if half == 1 else None)
            fr_se[s % 2] = em.cnt["SE"]
            em.wait("vector", "SE")
            em.emit("vector", "tensor_copy", FTM[:, c * D:c * D + 512], FRS[:, 0:512])
            em.emit("vector", "tensor_copy", FTM[:, c * D + 512:(c + 1) * D],
                    FRS[:, 512:1024])
            em.emit("vector", "scalar_tensor_tensor", LKS[:, 0:512], FRS[:, 0:512],
                    0.01, FRS[:, 0:512], ALU.mult, ALU.max)
            em.emit("vector", "scalar_tensor_tensor", LKS[:, 512:1024],
                    FRS[:, 512:1024], 0.01, FRS[:, 512:1024], ALU.mult, ALU.max,
                    inc=("VE", 1))
            em.wait("gpsimd", "VE")
            em.emit("gpsimd", "tensor_mul", AAS[:, 0:512], LKS[:, 0:512],
                    ATTB[:, 0:512])
            em.emit("gpsimd", "tensor_mul", AAS[:, 512:1024], LKS[:, 512:1024],
                    ATTB[:, 512:1024], inc=("GP", 1))
            em.wait("vector", "GP")
            em.emit("vector", "tensor_reduce", SRT[:, c * 8:(c + 1) * 8],
                    AAS[:].rearrange("p (s c) -> p s c", c=128), AX, ALU.add,
                    inc=("VE", 1))

    a2_ve = [0, 0]
    for c in range(32):
        bkp = c % 2
        em.wait("tensor", "VE", a2_ve[bkp])
        em.emit("tensor", "transpose", pb[bkp][0:8, 0:128],
                SRT[:, c * 8:(c + 1) * 8], IDF[:], inc=("PE", 1))
        em.wait("vector", "PE")
        em.emit("vector", "tensor_reduce", SMAX[:, c:c + 1], pb[bkp][0:8, 0:128],
                AX, ALU.max, inc=("VE", 1))
        a2_ve[bkp] = em.cnt["VE"]
    em.wait("vector", "VE")
    em.emit("vector", "tensor_reduce", SM[:, 0:1], SMAX[:, 0:32], AX, ALU.max,
            inc=("VE", 1))
    m_ve = em.cnt["VE"]
    a2b_mk = [0, 0]
    for c in range(32):
        bkp = c % 2
        sl = bkp * 128
        em.wait("tensor", "VE", max(a2_ve[bkp], a2b_mk[bkp]))
        em.wait("tensor", "SE", em.marks.get("a2_se_%d" % bkp, 0))
        em.emit("tensor", "transpose", pb[bkp][0:8, 0:128],
                SRT[:, c * 8:(c + 1) * 8], IDF[:], inc=("PE", 1))
        em.wait("vector", "PE")
        em.wait("vector", "VE", m_ve)
        em.emit("vector", "tensor_scalar", SRW[:, sl:sl + 128], pb[bkp][0:8, 0:128],
                SM[:, 0:1], None, ALU.subtract, inc=("VE", 1))
        em.wait("scalar", "VE")
        em.emit("scalar", "activation", SRW2[:, sl:sl + 128], SRW[:, sl:sl + 128],
                AF.Exp, inc=("SE", 1))
        em.marks["a2_se_%d" % bkp] = em.cnt["SE"]
        em.wait("vector", "SE")
        em.emit("vector", "tensor_reduce", SSUM[:, c:c + 1], SRW2[:, sl:sl + 128],
                AX, ALU.add, inc=("VE", 1))
        em.wait("tensor", "SE")
        em.wait("tensor", "VE", a2b_mk[bkp])
        em.emit("tensor", "transpose", pb[2 + bkp][0:128, 0:8],
                SRW2[:, sl:sl + 128], IDF[0:8, 0:8], inc=("PE", 1))
        em.wait("vector", "PE")
        em.emit("vector", "tensor_copy", WCOL[:, c * 8:(c + 1) * 8],
                pb[2 + bkp][0:128, 0:8], inc=("VE", 1))
        a2b_mk[bkp] = em.cnt["VE"]
    em.wait("vector", "VE")
    em.emit("vector", "tensor_reduce", SM[:, 1:2], SSUM[:, 0:32], AX, ALU.add,
            inc=("VE", 1))
    em.wait("tensor", "VE")
    for c in range(32):
        for h in range(8):
            em.emit("tensor", "matmul", pb[7][:, h:h + 1],
                    FTM[:, c * D + h * 128: c * D + (h + 1) * 128],
                    WCOL[:, c * 8 + h: c * 8 + h + 1],
                    start=(c == 0 and h == 0), stop=(c == 31 and h == 7),
                    inc=("PE", 1) if (c == 31 and h == 7) else None)
    em.wait("vector", "PE")
    em.emit("vector", "tensor_copy", VF[:], pb[7][:, 0:8], inc=("VE", 1))
    em.wait("tensor", "VE")
    em.emit("tensor", "transpose", pb[0][0:8, 0:128], VF[:], IDF[:], inc=("PE", 1))
    em.wait("vector", "PE")
    em.emit("vector", "tensor_copy", VR[:], pb[0][0:8, 0:128], inc=("VE", 1))
    em.wait("vector", "VE")
    em.emit("vector", "memset", PKT[:], 0.0, inc=("VE", 1))
    em.wait("vector", "VE")
    em.emit("vector", "tensor_copy", PKT[:, 0:2], SM[:, 0:2])
    em.emit("vector", "tensor_copy", PKT[:, 2:130], VR[:], inc=("VE", 1))
    em.wait("sync", "VE")
    em.emit("sync", "dma_start", out=cc_in[:], in_=PKT[:], inc=("ST", 16))
    em.wait("gpsimd", "ST")
    em.emit("gpsimd", "collective_compute", "AllGather", ALU.bypass,
            replica_groups=RG, ins=[cc_in[:]], outs=[cc_out[:]], inc=("CC", 1))
    em.wait("gpsimd", "CC")
    em.emit("gpsimd", "dma_start", out=GSB[:], in_=cc_out[:], inc=("GD", 16))
    em.wait("tensor", "GD")
    em.wait("tensor", "LDI")
    em.emit("tensor", "matmul", pb[1][0:8, 0:136], MSF[:], GSB[:],
            start=True, stop=False)
    em.emit("tensor", "matmul", pb[1][0:8, 136:272], MPT[:], GSB[:],
            start=False, stop=True, inc=("PE", 1))
    em.wait("vector", "PE")
    em.emit("vector", "tensor_copy", ABS[:], pb[1][0:8, 0:272], inc=("VE", 1))
    em.wait("vector", "VE")
    em.emit("vector", "tensor_tensor", CMB[:, 0:1], ABS[:, 0:1], ABS[:, 136:137],
            ALU.max, inc=("VE", 1))
    em.wait("vector", "VE")
    em.emit("vector", "tensor_sub", CMB[:, 1:2], ABS[:, 0:1], CMB[:, 0:1])
    em.emit("vector", "tensor_sub", CMB[:, 2:3], ABS[:, 136:137], CMB[:, 0:1],
            inc=("VE", 1))
    em.wait("scalar", "VE")
    em.emit("scalar", "activation", CMB[:, 1:3], CMB[:, 1:3], AF.Exp, inc=("SE", 1))
    em.wait("vector", "SE")
    em.emit("vector", "tensor_scalar", CMB[:, 3:132], ABS[:, 1:130], CMB[:, 1:2],
            None, ALU.mult, inc=("VE", 1))
    em.wait("vector", "VE")
    em.emit("vector", "scalar_tensor_tensor", CMB[:, 3:132], ABS[:, 137:266],
            CMB[:, 2:3], CMB[:, 3:132], ALU.mult, ALU.add, inc=("VE", 1))
    em.wait("vector", "VE")
    em.emit("vector", "reciprocal", CMB[:, 132:133], CMB[:, 3:4], inc=("VE", 1))
    em.wait("vector", "VE")
    em.emit("vector", "tensor_scalar", CTXB[:], CMB[:, 4:132], CMB[:, 132:133],
            None, ALU.mult, inc=("VE", 1))
    em.wait("tensor", "VE")
    em.emit("tensor", "transpose", pb[0][:].bitcast(BF16)[0:128, 0:8], CTXB[:],
            IDB[0:8, 0:8], inc=("PE", 1))
    em.wait("vector", "PE")
    em.emit("vector", "tensor_copy", CTXC[:], pb[0][:].bitcast(BF16)[0:128, 0:8],
            inc=("VE", 1))
    em.wait("tensor", "VE")
    for j in range(FC):
        for h in range(FC):
            em.emit("tensor", "matmul", pb[2][:, j:j + 1],
                    WFT[:, h * D + j * 128: h * D + (j + 1) * 128], CTXC[:, h:h + 1],
                    start=(j == 0 and h == 0), stop=(j == FC - 1 and h == FC - 1),
                    inc=("PE", 1) if (j == FC - 1 and h == FC - 1) else None)
    em.wait("vector", "PE")
    em.emit("vector", "tensor_copy", FHC[:], pb[2][:, 0:8], inc=("VE", 1))

    a3mk = {"b45": 0, "b23": 0, "rows": 0, "yt0": 0, "yt0se": 0,
            "yt1": 0, "yt1se": 0, "yt0gp": 0, "yt1gp": 0}

    def RL(b_):
        return (FRS if b_ == 0 else LKS)[0:1, 0:2 * GS]

    def _load_a3(g_):
        b_ = g_ % 2
        em.wait("sync", "PE", a3mk["rows"])
        em.wait("sync", "ST")
        em.emit("sync", "dma_start", out=RL(b_)[0:1, 0:GS],
                in_=rows_d[0:1, g_ * GS:(g_ + 1) * GS], inc=("LR%d" % b_, 16))
        em.emit("sync", "dma_start", out=RL(b_)[0:1, GS:2 * GS],
                in_=rows_d[1:2, g_ * GS:(g_ + 1) * GS], inc=("LR%d" % b_, 16))
        em.wait("sync", "VE", war_x[b_])
        for f in range(FC):
            em.emit("sync", "dma_start", out=X(b_, f),
                    in_=x0_d[f * 128:(f + 1) * 128, g_ * GS:(g_ + 1) * GS],
                    inc=("LX%d" % b_, 16))

    TTS2 = YTv[:, 16 * GS:24 * GS]
    HBF = HBB[:].bitcast(F32)

    def T3(b_):
        return (AAS if b_ == 0 else HBF)[0:1, 0:GS]

    def T4(b_):
        return (AAS if b_ == 0 else HBF)[0:1, GS:2 * GS]

    r1mk = {}
    yvk = {}

    def a3_h1(k):
        b_ = k % 2
        if k == 1:
            em.wait("sync", "VE")
            em.wait("sync", "SE")
            em.wait("sync", "GP")
        _load_a3(k)
        em.wait("tensor", "VE", a3mk["b45"])
        em.wait("tensor", "LR%d" % b_)
        em.emit("tensor", "matmul", pb[4][:], ONEF[0:1, :], RL(b_)[0:1, 0:GS],
                start=True, stop=True)
        em.emit("tensor", "matmul", pb[5][:], ONEF[0:1, :],
                RL(b_)[0:1, GS:2 * GS], start=True, stop=True, inc=("PE", 1))
        a3mk["rows"] = em.cnt["PE"]
        em.marks["rows_cons_pe"] = em.cnt["PE"]
        em.wait("vector", "PE", a3mk["rows"])
        em.wait("vector", "LX%d" % b_)
        em.wait("vector", "VE", a3mk["b45"])
        for f in range(FC):
            em.emit("vector", "tensor_mul", TTS[:, f * GS:(f + 1) * GS],
                    X(b_, f), pb[4][:], inc=("VE", 1) if f == FC - 1 else None)
        em.wait("vector", "VE")
        em.wait("vector", "SE", a3mk["yt%dse" % b_])
        em.wait("vector", "GP", a3mk["yt%dgp" % b_])
        em.wait("vector", "ST")
        for f in range(FC):
            em.emit("vector", "scalar_tensor_tensor", YTb(b_, f),
                    TTS[:, f * GS:(f + 1) * GS], FHC[:, f:f + 1], pb[5][:],
                    ALU.add, ALU.subtract, inc=("VE", 1) if f == FC - 1 else None)
        a3mk["b45"] = em.cnt["VE"]
        yvk[k] = em.cnt["VE"]

    def a3_h2(k):
        b_ = k % 2
        yt = [YTb(b_, f) for f in range(FC)]
        _ln_stats(em, pb, yt, ONEB, XBC, SQS, SROW, T1, T2, T3(b_), T4(b_),
                  CST, tiles_ve=yvk.pop(k), precise=True, rows_gp=True,
                  sq_gp=True)
        r1mk[k] = (em.marks["rows_se"], em.marks["rows_gp"], em.cnt["VE"])

    def a3_t1(g):
        buf = g % 2
        se1, gp1, ve1 = r1mk.pop(g)
        em.wait("tensor", "VE", max(a3mk["nrm"], ve1))
        em.wait("tensor", "SE", se1)
        em.wait("tensor", "GP", gp1)
        em.emit("tensor", "matmul", pb[2][:], ONEF[0:1, :], T3(buf),
                start=True, stop=True)
        em.emit("tensor", "matmul", pb[3][:], ONEF[0:1, :], T4(buf),
                start=True, stop=True, inc=("PE", 1))
        em.marks["rows_cons_pe"] = em.cnt["PE"]
        b23pe = em.cnt["PE"]
        em.wait("vector", "PE", b23pe)
        em.wait("vector", "VE", a3mk.get("addv", 0))
        for f in range(FC):
            em.emit("vector", "tensor_mul", TTS2[:, f * GS:(f + 1) * GS],
                    YTb(buf, f), pb[2][:], inc=("VE", 1) if f == FC - 1 else None)
        em.wait("vector", "VE")
        for f in range(FC):
            em.emit("vector", "tensor_sub", TTS2[:, f * GS:(f + 1) * GS],
                    TTS2[:, f * GS:(f + 1) * GS], pb[3][:],
                    inc=("VE", 1) if f == FC - 1 else None)
        a3mk["nrm"] = em.cnt["VE"]
        em.wait("vector", "VE")
        for f in range(FC):
            em.emit("vector", "tensor_scalar", TTS2[:, f * GS:(f + 1) * GS],
                    TTS2[:, f * GS:(f + 1) * GS], GAMS[:, f:f + 1],
                    BETS[:, f:f + 1], ALU.mult, ALU.add,
                    inc=("VE", 1) if f == FC - 1 else None)
        em.wait("vector", "VE")
        for f in range(FC):
            em.emit("vector", "tensor_add", YTb(buf, f),
                    TTS2[:, f * GS:(f + 1) * GS], X(buf, f),
                    inc=("VE", 1) if f == FC - 1 else None)
        war_x[buf] = em.cnt["VE"]
        x1_ve = em.cnt["VE"]
        a3mk["addv"] = x1_ve
        a3mk["x1ve"] = x1_ve
        em.wait("sync", "VE", x1_ve)
        for f in range(FC):
            em.emit("sync", "dma_start",
                    out=x1_d[f * 128:(f + 1) * 128, g * GS:(g + 1) * GS],
                    in_=YTb(buf, f), inc=("ST", 16))

    def a3_t2(g):
        buf = g % 2
        x1_ve = a3mk["x1ve"]
        if rows_next is not None:
            yt2 = [YTb(buf, f) for f in range(FC)]
            _ln_stats(em, pb, yt2, ONEB, XBC, SQS, SROW, T1, T2,
                      RR[0:1, :], QQ[0:1, :], CST, tiles_ve=x1_ve, precise=True,
                      rows_gp=True, rows_sqmean=True)
            a3mk["yt%dse" % buf] = em.cnt["SE"]
            a3mk["yt%dgp" % buf] = em.cnt["GP"]
            em.wait("sync", "VE", em.marks["rows_ve"])
            em.wait("sync", "SE", em.marks["rows_se"])
            em.wait("sync", "GP", em.marks["rows_gp"])
            em.emit("sync", "dma_start", out=rows_next[0:1, g * GS:(g + 1) * GS],
                    in_=RR[0:1, :], inc=("ST", 16))
            em.emit("sync", "dma_start", out=rows_next[1:2, g * GS:(g + 1) * GS],
                    in_=QQ[0:1, :], inc=("ST", 16))
        else:
            a3mk["yt%dse" % buf] = em.cnt["SE"]
            a3mk["yt%dgp" % buf] = em.cnt["GP"]

    em.wait("sync", "VE")
    em.wait("sync", "SE")
    em.wait("vector", "GP")
    a3mk["nrm"] = 0
    a3_h1(0)
    a3_h2(0)
    for g in range(NG):
        if g + 1 < NG:
            a3_h1(g + 1)
        a3_t1(g)
        if g + 1 < NG:
            a3_h2(g + 1)
        a3_t2(g)

    em.wait("sync", "ST")


def build_A(n_cores=8):
    nc, em, ar, dio, pb = _standalone(A_DRAMS(n_cores))
    emit_A(nc, em, ar, dio, pb, n_cores=n_cores)
    em.finalize(nc)
    return nc



def _sel_mats(n_cores):
    msf, mpt = [], []
    for c in range(n_cores):
        a = np.zeros((16, 8), np.float32)
        b = np.zeros((16, 8), np.float32)
        for h in range(8):
            a[(c % 2) * 8 + h, h] = 1.0
            b[((c % 2) ^ 1) * 8 + h, h] = 1.0
        msf.append(a)
        mpt.append(b)
    return msf, mpt


def _prep_A(inputs, n_cores=8):
    f32 = np.float32
    wrt = np.ascontiguousarray(np.asarray(inputs["L0_attn_Wr"], f32).T)
    wft = np.ascontiguousarray(np.asarray(inputs["L0_attn_Wf"], f32).T)
    attf = np.ascontiguousarray(
        np.tile(np.asarray(inputs["L0_attn_att_r"], f32), (1, 8)))
    gam = np.ascontiguousarray(
        np.asarray(inputs["L0_attn_ln_gamma"], f32).reshape(FC, 128).T)
    bet = np.ascontiguousarray(
        np.asarray(inputs["L0_attn_ln_beta"], f32).reshape(FC, 128).T)
    idn = np.eye(128, dtype=f32)
    one = np.ones((128, 128), f32)
    msf, mpt = _sel_mats(n_cores)
    x0 = np.asarray(inputs["L0_attn_h"], f32)
    maps = []
    for c in range(n_cores):
        b, hh = c // 2, c % 2
        xs = np.ascontiguousarray(x0[b, hh * TOK:(hh + 1) * TOK, :].T)
        maps.append(dict(x0=xs, wrt=wrt, wft=wft, attf=attf, gam=gam, bet=bet,
                         idn=idn, one=one, msf=msf[c], mpt=mpt[c]))
    return maps





def emit_B(nc, em, ar, dio, pb, skip_prepass=False, rows_next=None):
    x1_d = dio["x1"]; w1t_d = dio["w1t"]; w2t_d = dio["w2t"]
    h1t_d = dio["h1t"]; h2t_d = dio["h2t"]; w3t_d = dio["w3t"]
    s1_d = dio["s1c"]; s2_d = dio["s2c"]; sh1_d = dio["sh1c"]
    sh2_d = dio["sh2c"]; sw3_d = dio["sw3c"]; one_d = dio["one"]
    x2_d = dio["x2"]; rows_d = dio["rows_b"]

    sb = ar.alloc
    XT = sb("XT", [128, FC * GS], F32)
    TTS = sb("TTS", [128, GS], F32)
    XLN = sb("XLN", [128, FC * 1024], BF16)
    HA = sb("HA", [128, 32 * 1024], BF16)
    HB = sb("HB", [128, 32 * 1024], BF16)
    W12 = sb("W12", [128, 4 * 1024], BF16)
    WBLK = sb("WBLK", [128, 2 * 4096], BF16)
    X1R = sb("X1R", [128, 1024], F32)
    SCR = sb("SCR", [128, 2 * GS], F32)
    RR = sb("RR", [1, GS], F32)
    QQ = sb("QQ", [1, GS], F32)
    T1 = sb("T1", [1, GS], F32)
    T2 = sb("T2", [1, GS], F32)
    SROW = sb("SROW", [1, 2 * GS], F32)
    S1C = sb("S1C", [128, 32], F32)
    S2C = sb("S2C", [128, 32], F32)
    SH1C = sb("SH1C", [128, 32], F32)
    SH2C = sb("SH2C", [128, 32], F32)
    SW3C = sb("SW3C", [128, 8], F32)
    ONEF = sb("ONEF", [128, 128], F32)
    ONEB = sb("ONEB", [128, 128], BF16)
    CST = sb("CST", [128, 4], F32)
    XBC = HA[:].bitcast(BF16)[:, 0:FC * GS]
    SQS = HA[:].bitcast(BF16)[:, FC * GS:2 * FC * GS]

    for dst, src_ in ((S1C, s1_d), (S2C, s2_d), (SH1C, sh1_d), (SH2C, sh2_d),
                      (SW3C, sw3_d), (ONEF, one_d)):
        em.emit("sync", "dma_start", out=dst[:], in_=src_[:], inc=("LDI", 16))
    em.emit("gpsimd", "dma_start", out=ONEB[:], in_=one_d[:], inc=("LWI", 16))
    em.emit("vector", "memset", CST[:, 0:1], EPS_LN, inc=("VE", 1))
    for e_ in ("tensor", "vector", "scalar"):
        em.wait(e_, "LDI")
        em.wait(e_, "LWI")

    for g in (() if skip_prepass else range(NG)):
        em.wait("sync", "PE")
        em.wait("sync", "SE")
        for f in range(FC):
            em.emit("sync", "dma_start", out=XT[:, f * GS:(f + 1) * GS],
                    in_=x1_d[f * 128:(f + 1) * 128, g * GS:(g + 1) * GS],
                    inc=("LXP", 16))
        tiles = [XT[:, f * GS:(f + 1) * GS] for f in range(FC)]
        _ln_stats(em, pb, tiles, ONEB, XBC, SQS, SROW, T1, T2,
                  RR[0:1, 0:GS], QQ[0:1, 0:GS], CST, ld_waits=("LXP",),
                  rows_sqmean=True)
        em.wait("sync", "VE")
        em.wait("sync", "SE")
        em.emit("sync", "dma_start", out=rows_d[0:1, g * GS:(g + 1) * GS],
                in_=RR[0:1, 0:GS], inc=("ST", 16))
        em.emit("sync", "dma_start", out=rows_d[1:2, g * GS:(g + 1) * GS],
                in_=QQ[0:1, 0:GS], inc=("ST", 16))

    pe_w12 = [0, 0]
    pe_blk = [0, 0]
    bank_ve = {(p, h): 0 for p in range(2) for h in range(2)}
    bank_se = {(p, h): 0 for p in range(2) for h in range(2)}
    hbank_se = {(p, h): 0 for p in range(2) for h in range(2)}
    LNS = HB[:].bitcast(F32)[:, 0:4 * GS]
    SSB = HB[:].bitcast(BF16)[:, 8 * GS:16 * GS]
    XTH1 = HB[:].bitcast(F32)[:, 8 * GS:16 * GS]

    def _load_h0(t_):
        g0_ = t_ * 1024
        em.emit("sync", "dma_start", out=RR[0:1, 0:GS],
                in_=rows_d[0:1, g0_:g0_ + GS], inc=("LDR", 16))
        em.emit("sync", "dma_start", out=QQ[0:1, 0:GS],
                in_=rows_d[1:2, g0_:g0_ + GS], inc=("LDR", 16))
        for f in range(FC):
            em.emit("sync", "dma_start", out=XT[:, f * GS:(f + 1) * GS],
                    in_=x1_d[f * 128:(f + 1) * 128, g0_:g0_ + GS],
                    inc=("LX0", 16))

    def _ln_block(t_, h0_loaded):
        g0_ = t_ * 1024
        if not h0_loaded:
            em.wait("sync", "VE")
            em.wait("sync", "SE")
            em.wait("sync", "PE")
            em.wait("sync", "ST")
            _load_h0(t_)
        em.wait("sync", "PE")
        em.wait("sync", "SE")
        for f in range(FC):
            em.emit("sync", "dma_start", out=XTH1[:, f * GS:(f + 1) * GS],
                    in_=x1_d[f * 128:(f + 1) * 128, g0_ + GS:g0_ + 2 * GS],
                    inc=("LX1", 16))
        for half in range(2):
            bb = 2 * half
            if half == 1:
                em.wait("sync", "PE")
                em.wait("sync", "ST")
                em.emit("sync", "dma_start", out=RR[0:1, 0:GS],
                        in_=rows_d[0:1, g0_ + GS:g0_ + 2 * GS], inc=("LDR", 16))
                em.emit("sync", "dma_start", out=QQ[0:1, 0:GS],
                        in_=rows_d[1:2, g0_ + GS:g0_ + 2 * GS], inc=("LDR", 16))
            em.wait("vector", "LDR")
            em.wait("vector", "PE")
            em.wait("vector", "ST")
            em.emit("vector", "reciprocal", RR[0:1, 0:GS], RR[0:1, 0:GS],
                    inc=("VE", 1))
            em.wait("vector", "VE")
            em.emit("vector", "tensor_mul", QQ[0:1, 0:GS], QQ[0:1, 0:GS],
                    RR[0:1, 0:GS], inc=("VE", 1))
            em.wait("tensor", "LDR")
            em.wait("tensor", "VE")
            em.wait("tensor", "SE", max(bank_se[(0, 0)], bank_se[(0, 1)]))
            em.emit("tensor", "matmul", pb[bb][:], ONEF[0:1, :], RR[0:1, 0:GS],
                    start=True, stop=True)
            em.emit("tensor", "matmul", pb[bb + 1][:], ONEF[0:1, :],
                    QQ[0:1, 0:GS], start=True, stop=True, inc=("PE", 1))
            em.wait("vector", "PE")
            em.wait("vector", "LX%d" % half)
            em.wait("vector", "SE")
            em.wait("vector", "ST")
            src = XT if half == 0 else XTH1
            for w_ in range(2):
                em.wait("vector", "VE")
                for f4 in range(4):
                    f = w_ * 4 + f4
                    em.emit("vector", "tensor_mul",
                            LNS[:, f4 * GS:(f4 + 1) * GS],
                            src[:, f * GS:(f + 1) * GS], pb[bb][:],
                            inc=("VE", 1) if f4 == 3 else None)
                em.wait("vector", "VE")
                for f4 in range(4):
                    f = w_ * 4 + f4
                    em.emit("vector", "tensor_sub",
                            XLN[:, f * 1024 + half * GS:
                                f * 1024 + (half + 1) * GS],
                            LNS[:, f4 * GS:(f4 + 1) * GS], pb[bb + 1][:],
                            inc=("VE", 1) if f4 == 3 else None)
        return em.cnt["VE"]

    for k_ in bank_ve:
        bank_ve[k_] = max(bank_ve[k_], em.cnt["VE"])
        bank_se[k_] = max(bank_se[k_], em.cnt["SE"])
    ve_xln = _ln_block(0, False)
    for t in range(4):
        for k_ in bank_ve:
            bank_ve[k_] = max(bank_ve[k_], em.cnt["VE"])
            bank_se[k_] = max(bank_se[k_], em.cnt["SE"])
        if t + 1 < 4:
            em.wait("sync", "VE")
            em.wait("sync", "ST")
            _load_h0(t + 1)
        for jc in range(32):
            par = jc % 2
            lw = "LW%d" % par
            em.wait("gpsimd", "PE", pe_w12[par])
            em.emit("gpsimd", "dma_start", out=W12[:, par * 2048:par * 2048 + 1024],
                    in_=w1t_d[jc * 128:(jc + 1) * 128, :], inc=(lw, 16))
            em.emit("gpsimd", "dma_start",
                    out=W12[:, par * 2048 + 1024:par * 2048 + 2048],
                    in_=w2t_d[jc * 128:(jc + 1) * 128, :], inc=(lw, 16))
            em.wait("tensor", lw)
            em.wait("tensor", "VE", max(ve_xln, bank_ve[(par, 0)], bank_ve[(par, 1)]))
            em.wait("tensor", "SE", max(bank_se[(par, 0)], bank_se[(par, 1)]))
            for ic in range(FC):
                for half in range(2):
                    em.emit("tensor", "matmul", pb[4 * par + 2 * half][:],
                            W12[:, par * 2048 + ic * 128: par * 2048 + (ic + 1) * 128],
                            XLN[:, ic * 1024 + half * GS: ic * 1024 + (half + 1) * GS],
                            start=(ic == 0), stop=(ic == FC - 1))
            for ic in range(FC):
                for half in range(2):
                    em.emit("tensor", "matmul", pb[4 * par + 2 * half + 1][:],
                            W12[:, par * 2048 + 1024 + ic * 128:
                                par * 2048 + 1024 + (ic + 1) * 128],
                            XLN[:, ic * 1024 + half * GS: ic * 1024 + (half + 1) * GS],
                            start=(ic == 0), stop=(ic == FC - 1),
                            inc=("PE", 1) if (ic == FC - 1 and half == 1) else None)
            for half in range(2):
                em.wait("scalar", "PE")
                em.emit("scalar", "activation",
                        HA[:, jc * 1024 + half * GS: jc * 1024 + (half + 1) * GS],
                        pb[4 * par + 2 * half][:], AF.Silu, scale=S1C[:, jc:jc + 1],
                        inc=("SE", 1))
                em.wait("vector", "SE")
                em.emit("vector", "scalar_tensor_tensor",
                        HA[:, jc * 1024 + half * GS: jc * 1024 + (half + 1) * GS],
                        pb[4 * par + 2 * half + 1][:], S2C[:, jc:jc + 1],
                        HA[:, jc * 1024 + half * GS: jc * 1024 + (half + 1) * GS],
                        ALU.mult, ALU.mult, inc=("VE", 1))
                bank_se[(par, half)] = em.cnt["SE"]
                bank_ve[(par, half)] = em.cnt["VE"]
            pe_w12[par] = em.cnt["PE"]
        for stage, (wsrc, sdst, ssc, SRC_, DST_) in enumerate(
                ((h1t_d, None, SH1C, HA, HB), (h2t_d, None, SH2C, HB, HA))):
            src_ve = em.cnt["VE"]
            src_se = em.cnt["SE"]
            for kc in range(32):
                par = kc % 2
                lb = "LB%d" % par
                em.wait("gpsimd", "PE", pe_blk[par])
                em.emit("gpsimd", "dma_start", out=WBLK[:, par * 4096:(par + 1) * 4096],
                        in_=wsrc[kc * 128:(kc + 1) * 128, :], inc=(lb, 16))
                em.wait("tensor", lb)
                em.wait("tensor", "VE", src_ve)
                em.wait("tensor", "SE", max(src_se, hbank_se[(par, 0)],
                                            hbank_se[(par, 1)]))
                for jc in range(32):
                    for half in range(2):
                        em.emit("tensor", "matmul", pb[4 + 2 * par + half][:],
                                WBLK[:, par * 4096 + jc * 128: par * 4096 + (jc + 1) * 128],
                                SRC_[:, jc * 1024 + half * GS: jc * 1024 + (half + 1) * GS],
                                start=(jc == 0), stop=(jc == 31),
                                inc=("PE", 1) if (jc == 31 and half == 1) else None)
                for half in range(2):
                    em.wait("scalar", "PE")
                    em.emit("scalar", "activation",
                            DST_[:, kc * 1024 + half * GS: kc * 1024 + (half + 1) * GS],
                            pb[4 + 2 * par + half][:], AF.Gelu,
                            scale=ssc[:, kc:kc + 1], inc=("SE", 1))
                    hbank_se[(par, half)] = em.cnt["SE"]
                pe_blk[par] = em.cnt["PE"]
        if t + 1 < 4:
            ve_xln = _ln_block(t + 1, True)
        hc_se = em.cnt["SE"]
        for oc in range(FC):
            par = oc % 2
            r3 = "R%d" % (30 + par)
            em.wait("gpsimd", "PE", pe_blk[par])
            em.emit("gpsimd", "dma_start", out=WBLK[:, par * 4096:(par + 1) * 4096],
                    in_=w3t_d[oc * 128:(oc + 1) * 128, :], inc=("LB%d" % par, 16))
            em.wait("sync", "VE")
            em.wait("sync", "ST")
            em.wait("sync", "SE")
            em.emit("sync", "dma_start", out=X1R[:, 0:1024],
                    in_=x1_d[oc * 128:(oc + 1) * 128, t * 1024:(t + 1) * 1024],
                    inc=(r3, 16))
            em.wait("tensor", "LB%d" % par)
            em.wait("tensor", "SE", max(hc_se, hbank_se[(par, 0)],
                                        hbank_se[(par, 1)]))
            for kc in range(32):
                for half in range(2):
                    em.emit("tensor", "matmul", pb[4 + 2 * par + half][:],
                            WBLK[:, par * 4096 + kc * 128: par * 4096 + (kc + 1) * 128],
                            HA[:, kc * 1024 + half * GS: kc * 1024 + (half + 1) * GS],
                            start=(kc == 0), stop=(kc == 31),
                            inc=("PE", 1) if (kc == 31 and half == 1) else None)
            for half in range(2):
                bk = 4 + 2 * par + half
                em.wait("scalar", "PE")
                em.wait("scalar", "VE")
                em.emit("scalar", "activation", SCR[:, par * GS:(par + 1) * GS],
                        pb[bk][:], AF.Copy, scale=SW3C[:, oc:oc + 1], inc=("SE", 1))
                em.wait("vector", "SE")
                em.wait("vector", r3)
                em.wait("vector", "ST")
                em.emit("vector", "tensor_add", X1R[:, half * GS:(half + 1) * GS],
                        SCR[:, par * GS:(par + 1) * GS],
                        X1R[:, half * GS:(half + 1) * GS], inc=("VE", 1))
                hbank_se[(par, half)] = em.cnt["SE"]
                em.wait("sync", "VE")
                em.emit("sync", "dma_start",
                        out=x2_d[oc * 128:(oc + 1) * 128,
                                 t * 1024 + half * GS: t * 1024 + (half + 1) * GS],
                        in_=X1R[:, half * GS:(half + 1) * GS], inc=("ST", 16))
                if rows_next is not None:
                    sl = (2 * par + half) * GS
                    em.wait("vector", "VE")
                    em.emit("vector", "tensor_copy", SSB[:, sl:sl + GS],
                            X1R[:, half * GS:(half + 1) * GS], inc=("VE", 1))
                    em.wait("scalar", "VE")
                    em.emit("scalar", "activation",
                            SSB[:, 4 * GS + sl:4 * GS + sl + GS],
                            X1R[:, half * GS:(half + 1) * GS], AF.Square,
                            inc=("SE", 1))
                    em.wait("tensor", "VE")
                    em.wait("tensor", "SE")
                    em.emit("tensor", "matmul", pb[2 * half][0:1, 0:GS],
                            ONEB[:, 0:1], SSB[:, sl:sl + GS],
                            start=(oc == 0), stop=(oc == FC - 1))
                    em.emit("tensor", "matmul", pb[2 * half + 1][0:1, 0:GS],
                            ONEB[:, 0:1], SSB[:, 4 * GS + sl:4 * GS + sl + GS],
                            start=(oc == 0), stop=(oc == FC - 1),
                            inc=("PE", 1) if oc == FC - 1 else None)
            pe_blk[par] = em.cnt["PE"]
        if rows_next is not None:
            for half in range(2):
                em.wait("vector", "PE")
                em.wait("vector", "VE")
                em.emit("vector", "tensor_scalar", SROW[0:1, 0:GS],
                        pb[2 * half][0:1, 0:GS], 1.0 / D, None, ALU.mult)
                em.emit("vector", "tensor_scalar", SROW[0:1, GS:2 * GS],
                        pb[2 * half + 1][0:1, 0:GS], 1.0 / D, None, ALU.mult,
                        inc=("VE", 1))
                em.wait("scalar", "VE")
                em.emit("scalar", "activation", T1[0:1, :], SROW[0:1, 0:GS],
                        AF.Square, inc=("SE", 1))
                em.wait("vector", "SE")
                em.emit("vector", "tensor_sub", T2[0:1, :], SROW[0:1, GS:2 * GS],
                        T1[0:1, :], inc=("VE", 1))
                em.wait("scalar", "VE")
                em.emit("scalar", "activation", T1[0:1, :], T2[0:1, :], AF.Sqrt,
                        bias=CST[0:1, 0:1], inc=("SE", 1))
                em.wait("vector", "SE")
                em.wait("vector", "ST")
                em.emit("vector", "reciprocal", RR[0:1, 0:GS], T1[0:1, :],
                        inc=("VE", 1))
                em.wait("vector", "VE")
                em.emit("vector", "tensor_mul", QQ[0:1, 0:GS], SROW[0:1, 0:GS],
                        RR[0:1, 0:GS], inc=("VE", 1))
                em.wait("sync", "VE")
                g0r = t * 1024 + half * GS
                em.emit("sync", "dma_start", out=rows_next[0:1, g0r:g0r + GS],
                        in_=RR[0:1, 0:GS], inc=("ST", 16))
                em.emit("sync", "dma_start", out=rows_next[1:2, g0r:g0r + GS],
                        in_=QQ[0:1, 0:GS], inc=("ST", 16))
            for k_ in bank_ve:
                bank_ve[k_] = max(bank_ve[k_], em.cnt["VE"])
                bank_se[k_] = max(bank_se[k_], em.cnt["SE"])

    em.wait("sync", "ST")


def build_B():
    nc, em, ar, dio, pb = _standalone(B_DRAMS())
    emit_B(nc, em, ar, dio, pb)
    em.finalize(nc)
    return nc


def _prep_B(inputs, n_cores=8):
    f32 = np.float32

    def lhsT_blocks(w, IC, OC):
        wT = np.asarray(w, f32).T
        I, O = wT.shape
        t = wT.reshape(IC, 128, OC, 128).transpose(2, 1, 0, 3)
        return np.ascontiguousarray(t.reshape(OC * 128, IC * 128))

    def colmaj(s, C):
        return np.ascontiguousarray(np.asarray(s, f32).reshape(C, 128).T)

    shared = dict(
        w1t=lhsT_blocks(inputs["L0_ffn_w1_w"], 8, 32),
        w2t=lhsT_blocks(inputs["L0_ffn_w2_w"], 8, 32),
        h1t=lhsT_blocks(inputs["L0_ffn_h1_w"], 32, 32),
        h2t=lhsT_blocks(inputs["L0_ffn_h2_w"], 32, 32),
        w3t=lhsT_blocks(inputs["L0_ffn_w3_w"], 32, 8),
        s1c=colmaj(inputs["L0_ffn_w1_s"], 32),
        s2c=colmaj(inputs["L0_ffn_w2_s"], 32),
        sh1c=colmaj(inputs["L0_ffn_h1_s"], 32),
        sh2c=colmaj(inputs["L0_ffn_h2_s"], 32),
        sw3c=colmaj(inputs["L0_ffn_w3_s"], 8),
        one=np.ones((128, 128), f32),
    )
    return shared





NH = 16
HD = 64
CH = 128


def emit_C(nc, em, ar, dio, pb, n_cores=8, dbg=False, skip_prepass=False, rows_next=None):
    RG = [[a, a + 1] for a in range(0, n_cores, 2)]
    x2_d = dio["x2"]; wqkv_d = dio["wqkv"]; wout_d = dio["wout"]
    utm_d = dio["utm"]; idn_d = dio["idn"]; idlo_d = dio["idlo"]
    one_d = dio["one"]; mkv_d = dio["mkv"]; sel16_d = dio["sel16"]
    x3_d = dio["x3"]
    kch = 1
    if dbg:
        kvdbg_d = dio["kvdbg"]
    rows_d = dio["rows_c"]; pq_d = dio["pq_sc"]; nu_d = dio["nu_sc"]
    kv_in = dio["kv_in"]; kv_out = dio["kv_out"]

    sb = ar.alloc
    WQKV = sb("WQKV", [128, FC * 3 * D], BF16)
    WOUT = WQKV
    XT = sb("XT", [128, FC * GS], F32)
    XT2 = sb("XT2", [128, FC * GS], F32)
    XLN = sb("XLN", [128, FC * GS], BF16)
    TTS = sb("TTS", [128, 5 * GS], F32)
    AR1 = sb("AR1", [128, NH * GS], BF16)
    AR2 = sb("AR2", [128, NH * GS], BF16)
    ESC = sb("ESC", [128, 2 * GS], F32)
    RSC = sb("RSC", [128, 2 * GS], F32)
    KT = sb("KT", [128, NH * HD], BF16)
    VTG = sb("VTG", [128, 4 * 1024], BF16)
    MT = sb("MT", [128, NH * CH], BF16)
    DNC = sb("DNC", [128, 32 * NH], F32)
    SKV = sb("SKV", [64, NH * HD], F32)
    SKVB = sb("SKVB", [64, NH * HD], BF16)
    SKC = sb("SKC", [64, NH], F32)
    SKCB = sb("SKCB", [64, NH], BF16)
    AFT = sb("AFT", [128, FC * GS], BF16)
    NTS = sb("NTS", [64, 2 * GS], F32)
    RECC = sb("RECC", [128, NH], F32)
    RECR = sb("RECR", [16, 4 * CH], F32)
    PACK = sb("PACK", [64, 1056], F32)
    GSBc = sb("GSBc", [128, 1056], F32)
    KVU = sb("KVU", [64, 1056], F32)
    KVUB = sb("KVUB", [64, 1056], BF16)
    UTM = sb("UTM", [128, 128], F32)
    IDF = sb("IDF", [128, 128], F32)
    IDB = sb("IDB", [128, 128], BF16)
    IDLO = sb("IDLO", [128, 64], BF16)
    ONEF = sb("ONEF", [128, 128], F32)
    ONEB = sb("ONEB", [128, 128], BF16)
    MKV = sb("MKV", [128, kch * 64], F32)
    SEL16 = sb("SEL16", [16, 16 * 64], F32)
    RR = sb("RR", [1, GS], F32)
    QQ = sb("QQ", [1, GS], F32)
    T1 = sb("T1", [1, GS], F32)
    T2 = sb("T2", [1, GS], F32)
    SROW = sb("SROW", [1, 2 * GS], F32)
    CST = sb("CST", [128, 4], F32)
    XBC = AR1[:].bitcast(BF16)[:, 0:FC * GS]
    SQS = AR2[:].bitcast(BF16)[:, 0:FC * GS]
    RR2 = SROW[0:1, 0:GS]
    QQ2 = SROW[0:1, GS:2 * GS]

    def XTb(p):
        return XT if p == 0 else XT2

    def RRb(p):
        return RR if p == 0 else RR2

    def QQb(p):
        return QQ if p == 0 else QQ2

    def ARb(p):
        return AR1 if p == 0 else AR2

    def PQ(h):
        return AR1[0:64, h * GS:(h + 1) * GS]

    def NUG(h):
        return AR1[64:128, h * GS:(h + 1) * GS]

    def PK(h):
        return AR2[0:64, h * GS:(h + 1) * GS]

    def VC(ci, h):
        return VTG[:, ci * 1024 + h * HD:ci * 1024 + (h + 1) * HD]

    for dst, src_ in ((UTM, utm_d), (IDF, idn_d), (ONEF, one_d), (MKV, None),
                      ):
        if src_ is not None:
            em.emit("sync", "dma_start", out=dst[:], in_=src_[:], inc=("LDI", 16))
    for r4 in range(kch):
        em.emit("sync", "dma_start", out=MKV[:, r4 * 64:(r4 + 1) * 64],
                in_=mkv_d[r4 * 128:(r4 + 1) * 128, :], inc=("LDI", 16))
    for ic in range(FC):
        em.emit("gpsimd", "dma_start", out=WQKV[:, ic * 3 * D:(ic + 1) * 3 * D],
                in_=wqkv_d[ic * 128:(ic + 1) * 128, :], inc=("LWI", 16))
    em.emit("gpsimd", "dma_start", out=ONEB[:], in_=one_d[:], inc=("LWI", 16))
    em.emit("gpsimd", "dma_start", out=IDB[:], in_=idn_d[:], inc=("LWI", 16))
    em.emit("gpsimd", "dma_start", out=IDLO[:], in_=idlo_d[:], inc=("LWI", 16))
    em.emit("sync", "dma_start", out=SEL16[:], in_=sel16_d[:], inc=("LDI", 16))
    em.emit("vector", "memset", CST[:, 0:1], EPS_LN)
    em.emit("vector", "memset", CST[:, 1:2], EPS_ATT)
    em.emit("vector", "memset", SKV[:], 0.0)
    em.emit("vector", "memset", SKVB[:], 0.0)
    em.emit("vector", "memset", SKC[:], 0.0)
    em.emit("vector", "memset", SKCB[:], 0.0)
    em.emit("vector", "memset", DNC[:], 0.0, inc=("VE", 1))
    for e_ in ("tensor", "vector", "scalar"):
        em.wait(e_, "LDI")
        em.wait(e_, "LWI")

    for g in (() if skip_prepass else range(NG)):
        em.wait("sync", "PE")
        em.wait("sync", "SE")
        for f in range(FC):
            em.emit("sync", "dma_start", out=XT[:, f * GS:(f + 1) * GS],
                    in_=x2_d[f * 128:(f + 1) * 128, g * GS:(g + 1) * GS],
                    inc=("LXP", 16))
        tiles = [XT[:, f * GS:(f + 1) * GS] for f in range(FC)]
        _ln_stats(em, pb, tiles, ONEB, XBC, SQS, SROW, T1, T2,
                  RR[0:1, :], QQ[0:1, :], CST, ld_waits=("LXP",))
        em.wait("sync", "VE")
        em.emit("sync", "dma_start", out=rows_d[0:1, g * GS:(g + 1) * GS],
                in_=RR[0:1, :], inc=("ST", 16))
        em.emit("sync", "dma_start", out=rows_d[1:2, g * GS:(g + 1) * GS],
                in_=QQ[0:1, :], inc=("ST", 16))

    qkv_mk = [0, 0]
    qkv_se = [0, 0]
    tp_se = [0, 0]
    mt_ve = [0, 0]
    nu_se = [0, 0]
    dn_ve = 0
    dl_ve = 0

    def _load_grp(g_):
        p = g_ % 2
        g0_ = g_ * GS
        em.emit("sync", "dma_start", out=RRb(p)[0:1, :],
                in_=rows_d[0:1, g0_:g0_ + GS], inc=("LR%d" % p, 16))
        em.emit("sync", "dma_start", out=QQb(p)[0:1, :],
                in_=rows_d[1:2, g0_:g0_ + GS], inc=("LR%d" % p, 16))
        for f in range(FC):
            em.emit("sync", "dma_start", out=XTb(p)[:, f * GS:(f + 1) * GS],
                    in_=x2_d[f * 128:(f + 1) * 128, g0_:g0_ + GS],
                    inc=("LX%d" % p, 16))

    AFF = AFT[:].bitcast(F32)[:, 0:4 * GS]
    c_ln = {"t": 0, "x": 0}

    def _c_ln_t(k):
        p = k % 2
        em.wait("tensor", "LR%d" % p)
        em.wait("tensor", "VE", max(qkv_mk[1], c_ln["x"]))
        em.wait("tensor", "SE", qkv_se[1])
        em.emit("tensor", "matmul", pb[7][:], ONEF[0:1, :], RRb(p)[0:1, :],
                start=True, stop=True, inc=("PE", 1))
        pe_r = em.cnt["PE"]
        em.wait("vector", "PE", pe_r)
        em.wait("vector", "LX%d" % p)
        em.wait("vector", "VE")
        for f in range(4):
            em.emit("vector", "tensor_mul", TTS[:, (1 + f) * GS:(2 + f) * GS],
                    XTb(p)[:, f * GS:(f + 1) * GS], pb[7][:])
        for f in range(4):
            em.emit("vector", "tensor_mul", AFF[:, f * GS:(f + 1) * GS],
                    XTb(p)[:, (4 + f) * GS:(5 + f) * GS], pb[7][:],
                    inc=("VE", 1) if f == 3 else None)
        c_ln["t"] = em.cnt["VE"]

    def _c_ln_s(k):
        p = k % 2
        em.wait("tensor", "VE", c_ln["t"])
        em.emit("tensor", "matmul", pb[7][:], ONEF[0:1, :], QQb(p)[0:1, :],
                start=True, stop=True, inc=("PE", 1))
        pe_q = em.cnt["PE"]
        em.wait("vector", "PE", pe_q)
        em.wait("vector", "VE")
        for f in range(4):
            em.emit("vector", "tensor_sub", XLN[:, f * GS:(f + 1) * GS],
                    TTS[:, (1 + f) * GS:(2 + f) * GS], pb[7][:])
        for f in range(4):
            em.emit("vector", "tensor_sub", XLN[:, (4 + f) * GS:(5 + f) * GS],
                    AFF[:, f * GS:(f + 1) * GS], pb[7][:],
                    inc=("VE", 1) if f == 3 else None)
        c_ln["x"] = em.cnt["VE"]

    em.wait("sync", "PE")
    em.wait("sync", "VE")
    em.wait("sync", "SE")
    em.wait("sync", "ST")
    _load_grp(0)
    _c_ln_t(0)
    _c_ln_s(0)
    for g in range(NG):
        par = g % 2
        lx = "LX%d" % par
        g0 = g * GS
        if g + 1 < NG:
            em.wait("sync", "PE")
            em.wait("sync", "VE")
            em.wait("sync", "SE")
            em.wait("sync", "ST")
            _load_grp(g + 1)
        em.wait("tensor", "LWI")
        ve_xln = c_ln["x"]
        em.wait("tensor", "SPQ")
        em.wait("tensor", "LNU")
        for oc in range(16):
            bq = 6 + oc % 2
            em.wait("tensor", "VE", max(ve_xln, qkv_mk[oc % 2], dn_ve))
            em.wait("tensor", "SE", qkv_se[oc % 2])
            for ic in range(FC):
                em.emit("tensor", "matmul", pb[bq][:],
                        WQKV[:, ic * 3 * D + oc * 128: ic * 3 * D + (oc + 1) * 128],
                        XLN[:, ic * GS:(ic + 1) * GS],
                        start=(ic == 0), stop=(ic == FC - 1),
                        inc=("PE", 1) if ic == FC - 1 else None)
            h0, h1 = 2 * (oc % 8), 2 * (oc % 8) + 1
            op = (oc % 2) * GS
            em.wait("scalar", "PE")
            em.emit("scalar", "activation", RSC[:, op:op + GS], pb[bq][:],
                    AF.Relu, scale=-1.0, inc=("SE", 1))
            em.wait("scalar", "SE")
            em.emit("scalar", "activation", ESC[:, op:op + GS],
                    RSC[:, op:op + GS], AF.Exp, scale=-1.0, inc=("SE", 1))
            dst = PQ if oc < 8 else PK
            em.wait("vector", "SE")
            em.wait("vector", "PE")
            em.emit("vector", "scalar_tensor_tensor", dst(h0),
                    pb[bq][0:64, :], 0.0, ESC[0:64, op:op + GS],
                    ALU.max, ALU.add)
            em.emit("vector", "scalar_tensor_tensor", dst(h1), pb[bq][64:128, :],
                    0.0, ESC[64:128, op:op + GS], ALU.max, ALU.add,
                    inc=("VE", 1))
            qkv_mk[oc % 2] = em.cnt["VE"]
            qkv_se[oc % 2] = em.cnt["SE"]
        for sc in range(4):
            for vb in range(2):
                p = (sc * 2 + vb) % 2
                bq = 6 + p
                em.wait("tensor", "VE", max(ve_xln, qkv_mk[p], dn_ve))
                em.wait("tensor", "SE", qkv_se[p])
                for ic in range(FC):
                    em.emit("tensor", "matmul", pb[bq][:],
                            XLN[:, ic * GS + sc * CH:ic * GS + (sc + 1) * CH],
                            WQKV[:, ic * 3 * D + 2048 + vb * 512:
                                 ic * 3 * D + 2048 + (vb + 1) * 512],
                            start=(ic == 0), stop=(ic == FC - 1),
                            inc=("PE", 1) if ic == FC - 1 else None)
                em.wait("vector", "PE")
                em.emit("vector", "tensor_copy",
                        VTG[:, sc * 1024 + vb * 512:sc * 1024 + (vb + 1) * 512],
                        pb[bq][:], inc=("VE", 1))
                qkv_mk[p] = em.cnt["VE"]
        em.wait("sync", "VE")
        em.emit("sync", "dma_start", out=pq_d[:, g * NH * GS:(g + 1) * NH * GS],
                in_=AR1[0:64, :], inc=("SPQ", 16))
        ve_qkv = em.cnt["VE"]
        for ci in range(4):
            c = g * 4 + ci
            t0 = ci * CH
            for h in range(NH):
                hp = h % 2
                bm = 2 + hp
                em.wait("tensor", "VE", max(dl_ve, mt_ve[hp], ve_qkv))
                em.emit("tensor", "matmul", pb[bm][0:128, 0:128], PK(h)[:, t0:t0 + CH],
                        PQ(h)[:, t0:t0 + CH], start=True, stop=True, inc=("PE", 1))
                em.wait("vector", "PE")
                em.emit("vector", "tensor_mul", MT[:, h * CH:(h + 1) * CH],
                        pb[bm][0:128, 0:128], UTM[:], inc=("VE", 1))
                mt_ve[hp] = em.cnt["VE"]
            for h in range(NH):
                hp = h % 2
                em.wait("tensor", "SE", tp_se[hp])
                em.wait("tensor", "VE", max(dl_ve, ve_qkv))
                em.emit("tensor", "transpose", pb[hp][:].bitcast(BF16)[0:128, 0:64],
                        PK(h)[:, t0:t0 + CH], IDB[0:64, 0:64], inc=("PE", 1))
                em.wait("scalar", "PE")
                em.emit("scalar", "copy", KT[:, h * HD:(h + 1) * HD],
                        pb[hp][:].bitcast(BF16)[0:128, 0:64], inc=("SE", 1))
                tp_se[hp] = em.cnt["SE"]
            for h in range(NH):
                hp = h % 2
                bn = 4 + hp
                em.wait("tensor", "VE", mt_ve[hp])
                em.wait("tensor", "SE", nu_se[hp])
                em.emit("tensor", "matmul", pb[bn][0:64, 0:128],
                        SKVB[:, h * HD:(h + 1) * HD], PQ(h)[:, t0:t0 + CH],
                        start=True, stop=False)
                em.emit("tensor", "matmul", pb[bn][0:64, 0:128],
                        VC(ci, h), MT[:, h * CH:(h + 1) * CH],
                        start=False, stop=True, inc=("PE", 1))
                em.wait("scalar", "PE")
                em.wait("scalar", "LNU")
                em.emit("scalar", "copy", NUG(h)[:, t0:t0 + CH], pb[bn][0:64, 0:128],
                        inc=("SE", 1))
                nu_se[hp] = em.cnt["SE"]
            em.wait("tensor", "VE", dn_ve)
            for h in range(NH):
                em.emit("tensor", "matmul", pb[6][0:128, h:h + 1],
                        PQ(h)[:, t0:t0 + CH], SKCB[:, h:h + 1],
                        start=(h == 0), stop=False)
                em.emit("tensor", "matmul", pb[6][0:128, h:h + 1],
                        MT[:, h * CH:(h + 1) * CH], ONEB[:, 0:1],
                        start=False, stop=(h == NH - 1),
                        inc=("PE", 1) if h == NH - 1 else None)
            em.wait("vector", "PE")
            em.emit("vector", "tensor_copy", DNC[:, c * NH:(c + 1) * NH],
                    pb[6][0:128, 0:16], inc=("VE", 1))
            dn_ve = em.cnt["VE"]
            em.wait("tensor", "SE", tp_se[1])
            em.wait("tensor", "VE", max(mt_ve[0], mt_ve[1]))
            for h in range(NH):
                bs = h // 8
                em.emit("tensor", "matmul",
                        pb[bs][0:64, (h % 8) * HD:(h % 8 + 1) * HD],
                        KT[:, h * HD:(h + 1) * HD], VC(ci, h),
                        start=(h % 8 == 0), stop=(h % 8 == 7))
            for h in range(NH):
                em.emit("tensor", "matmul", pb[2][0:64, h:h + 1],
                        KT[:, h * HD:(h + 1) * HD], ONEB[:, 0:1],
                        start=(h == 0), stop=(h == NH - 1),
                        inc=("PE", 1) if h == NH - 1 else None)
            em.wait("vector", "PE")
            em.emit("vector", "tensor_add", SKV[:, 0:8 * HD], SKV[:, 0:8 * HD],
                    pb[0][0:64, 0:512])
            em.emit("vector", "tensor_add", SKV[:, 8 * HD:16 * HD],
                    SKV[:, 8 * HD:16 * HD], pb[1][0:64, 0:512])
            em.emit("vector", "tensor_add", SKC[:], SKC[:], pb[2][0:64, 0:16],
                    inc=("VE", 1))
            em.wait("vector", "VE")
            em.emit("vector", "tensor_copy", SKVB[:], SKV[:])
            em.emit("vector", "tensor_copy", SKCB[:], SKC[:], inc=("VE", 1))
            dl_ve = em.cnt["VE"]
            if g + 1 < NG and ci == 1:
                _c_ln_t(g + 1)
            if g + 1 < NG and ci == 2:
                _c_ln_s(g + 1)
        em.wait("sync", "SE")
        em.emit("sync", "dma_start", out=nu_d[:, g * NH * GS:(g + 1) * NH * GS],
                in_=AR1[64:128, :], inc=("LNU", 16))

    em.wait("gpsimd", "PE")
    for ic in range(FC):
        em.emit("gpsimd", "dma_start", out=WOUT[:, ic * D:(ic + 1) * D],
                in_=wout_d[ic * 128:(ic + 1) * 128, :], inc=("LWO", 16))
    em.wait("vector", "VE")
    em.emit("vector", "memset", PACK[:], 0.0, inc=("VE", 1))
    em.wait("vector", "VE")
    for h in range(NH):
        em.emit("vector", "tensor_copy", PACK[:, h * 66:h * 66 + 64],
                SKV[:, h * HD:(h + 1) * HD],
                inc=("VE", 1) if h == NH - 1 else None)
    em.wait("vector", "VE")
    for h in range(NH):
        em.emit("vector", "tensor_copy", PACK[:, h * 66 + 64:h * 66 + 65],
                SKC[:, h:h + 1], inc=("VE", 1) if h == NH - 1 else None)
    em.wait("sync", "VE")
    em.emit("sync", "dma_start", out=kv_in[:], in_=PACK[:], inc=("ST", 16))
    em.wait("gpsimd", "ST")
    em.emit("gpsimd", "collective_compute", "AllGather", ALU.bypass,
            replica_groups=RG, ins=[kv_in[:]], outs=[kv_out[:]], inc=("CC", 1))
    em.wait("gpsimd", "CC")
    for r4 in range(kch):
        em.emit("gpsimd", "dma_start", out=GSBc[:],
                in_=kv_out[r4 * 128:(r4 + 1) * 128, :], inc=("GD", 16))
        em.wait("tensor", "GD")
        em.wait("tensor", "LDI")
        em.wait("tensor", "VE" if r4 == 0 else "PE")
        em.emit("tensor", "matmul", pb[0][0:64, 0:512], MKV[:, r4 * 64:(r4 + 1) * 64],
                GSBc[:, 0:512], start=(r4 == 0), stop=(r4 == kch - 1))
        em.emit("tensor", "matmul", pb[1][0:64, 0:512], MKV[:, r4 * 64:(r4 + 1) * 64],
                GSBc[:, 512:1024], start=(r4 == 0), stop=(r4 == kch - 1))
        em.emit("tensor", "matmul", pb[2][0:64, 0:32], MKV[:, r4 * 64:(r4 + 1) * 64],
                GSBc[:, 1024:1056], start=(r4 == 0), stop=(r4 == kch - 1),
                inc=("PE", 1))
    em.wait("vector", "PE")
    em.emit("vector", "tensor_copy", KVU[:, 0:512], pb[0][0:64, 0:512])
    em.emit("vector", "tensor_copy", KVU[:, 512:1024], pb[1][0:64, 0:512])
    em.emit("vector", "tensor_copy", KVU[:, 1024:1056], pb[2][0:64, 0:32],
            inc=("VE", 1))
    em.wait("vector", "VE")
    em.emit("vector", "tensor_copy", KVUB[:], KVU[:], inc=("VE", 1))
    if dbg:
        em.wait("sync", "VE")
        em.emit("sync", "dma_start", out=kvdbg_d[:, 0:1056], in_=KVU[:],
                inc=("ST", 16))
        em.emit("sync", "dma_start", out=kvdbg_d[:, 1056:2112], in_=PACK[:],
                inc=("ST", 16))

    corr_mk = [0, 0]
    nts_mk = [0, 0]
    wo_mk = {1: 0, 7: 0}

    def _load_corr(g_):
        p = g_ % 2
        g0_ = g_ * GS
        em.emit("sync", "dma_start", out=ARb(p)[0:64, :],
                in_=pq_d[:, g_ * NH * GS:(g_ + 1) * NH * GS],
                inc=("SP%d" % p, 16))
        em.emit("sync", "dma_start", out=ARb(p)[64:128, :],
                in_=nu_d[:, g_ * NH * GS:(g_ + 1) * NH * GS],
                inc=("SP%d" % p, 16))
        for f in range(FC):
            em.emit("sync", "dma_start", out=XTb(p)[:, f * GS:(f + 1) * GS],
                    in_=x2_d[f * 128:(f + 1) * 128, g0_:g0_ + GS],
                    inc=("LX%d" % p, 16))

    em.wait("sync", "PE")
    em.wait("sync", "VE")
    em.wait("sync", "SE")
    em.wait("sync", "SPQ")
    em.wait("sync", "LNU")
    _load_corr(0)
    for g in range(NG):
        par = g % 2
        lx = "LX%d" % par
        g0 = g * GS
        if g + 1 < NG:
            em.wait("sync", "PE")
            em.wait("sync", "VE")
            _load_corr(g + 1)
        for ci in range(4):
            c = g * 4 + ci
            em.wait("tensor", "SP%d" % par)
            em.wait("tensor", "VE")
            for h in range(NH):
                em.emit("tensor", "matmul", pb[6][0:128, h:h + 1],
                        ARb(par)[0:64, h * GS + ci * CH: h * GS + (ci + 1) * CH],
                        KVUB[:, h * 66 + 64:h * 66 + 65],
                        start=(h == 0), stop=(h == NH - 1),
                        inc=("PE", 1) if h == NH - 1 else None)
            em.wait("vector", "PE")
            em.emit("vector", "scalar_tensor_tensor", RECC[:], pb[6][0:128, 0:16],
                    CST[:, 1:2], DNC[:, c * NH:(c + 1) * NH], ALU.add, ALU.add,
                    inc=("VE", 1))
            em.wait("vector", "VE")
            em.emit("vector", "reciprocal", RECC[:], RECC[:], inc=("VE", 1))
            em.wait("tensor", "VE")
            em.emit("tensor", "transpose", pb[7][0:16, 0:128], RECC[:], IDF[:],
                    inc=("PE", 1))
            em.wait("vector", "PE")
            em.emit("vector", "tensor_copy", RECR[:, ci * CH:(ci + 1) * CH],
                    pb[7][0:16, 0:128], inc=("VE", 1))
        rec_ready = em.cnt["VE"]
        for h in range(NH):
            hp = h % 2
            bc = 4 + hp
            br = 2 + hp
            em.wait("tensor", "SP%d" % par)
            em.wait("tensor", "VE", max(rec_ready, corr_mk[hp]))
            em.emit("tensor", "matmul", pb[bc][0:64, 0:512],
                    KVUB[:, h * 66:h * 66 + 64],
                    ARb(par)[0:64, h * GS:(h + 1) * GS],
                    start=True, stop=True)
            em.emit("tensor", "matmul", pb[br][0:64, 0:512], SEL16[:, h * 64:(h + 1) * 64],
                    RECR[:], start=True, stop=True, inc=("PE", 1))
            em.wait("vector", "PE")
            em.wait("vector", "VE", nts_mk[hp])
            em.emit("vector", "tensor_add", NTS[:, hp * GS:(hp + 1) * GS],
                    ARb(par)[64:128, h * GS:(h + 1) * GS],
                    pb[bc][0:64, 0:512], inc=("VE", 1))
            em.wait("vector", "VE")
            em.emit("vector", "tensor_mul",
                    AFT[hp * 64:hp * 64 + 64, (h // 2) * GS:(h // 2 + 1) * GS],
                    NTS[:, hp * GS:(hp + 1) * GS], pb[br][0:64, 0:512],
                    inc=("VE", 1))
            corr_mk[hp] = em.cnt["VE"]
            nts_mk[hp] = em.cnt["VE"]
        aft_ve = em.cnt["VE"]
        em.wait("tensor", "LWO")
        for oc in range(FC):
            bo = 1 if oc % 2 == 0 else 7
            em.wait("tensor", "VE", aft_ve)
            em.wait("tensor", "SE", wo_mk[bo])
            for fc in range(FC):
                em.emit("tensor", "matmul", pb[bo][:],
                        WOUT[:, fc * D + oc * 128: fc * D + (oc + 1) * 128],
                        AFT[:, fc * GS:(fc + 1) * GS],
                        start=(fc == 0), stop=(fc == FC - 1),
                        inc=("PE", 1) if fc == FC - 1 else None)
            em.wait("scalar", "PE")
            em.wait("scalar", lx)
            em.wait("scalar", "VE")
            em.wait("scalar", "ST")
            em.emit("scalar", "activation", TTS[:, GS:2 * GS], pb[bo][:], AF.Copy,
                    inc=("SE", 1))
            wo_mk[bo] = em.cnt["SE"]
            em.wait("vector", "SE")
            em.emit("vector", "tensor_add", TTS[:, GS:2 * GS], TTS[:, GS:2 * GS],
                    XTb(par)[:, oc * GS:(oc + 1) * GS], inc=("VE", 1))
            em.wait("sync", "VE")
            em.emit("sync", "dma_start",
                    out=x3_d[oc * 128:(oc + 1) * 128, g0:g0 + GS],
                    in_=TTS[:, GS:2 * GS], inc=("ST", 16))
            if rows_next is not None:
                sl = (oc % 2) * GS
                em.wait("vector", "VE")
                em.emit("vector", "tensor_copy", XLN[:, sl:sl + GS],
                        TTS[:, GS:2 * GS], inc=("VE", 1))
                em.wait("scalar", "VE")
                em.emit("scalar", "activation", XLN[:, 2 * GS + sl:2 * GS + sl + GS],
                        TTS[:, GS:2 * GS], AF.Square, inc=("SE", 1))
                if oc == 0:
                    em.wait("tensor", "VE")
                em.wait("tensor", "SE")
                em.emit("tensor", "matmul", pb[0][0:1, 0:GS], ONEB[:, 0:1],
                        XLN[:, sl:sl + GS], start=(oc == 0), stop=(oc == FC - 1))
                em.emit("tensor", "matmul", pb[2][0:1, 0:GS], ONEB[:, 0:1],
                        XLN[:, 2 * GS + sl:2 * GS + sl + GS],
                        start=(oc == 0), stop=(oc == FC - 1),
                        inc=("PE", 1) if oc == FC - 1 else None)
        if rows_next is not None:
            em.wait("vector", "PE")
            em.wait("vector", "VE")
            em.emit("vector", "tensor_scalar", SROW[0:1, 0:GS], pb[0][0:1, 0:GS],
                    1.0 / D, None, ALU.mult)
            em.emit("vector", "tensor_scalar", SROW[0:1, GS:2 * GS],
                    pb[2][0:1, 0:GS], 1.0 / D, None, ALU.mult, inc=("VE", 1))
            em.wait("scalar", "VE")
            em.emit("scalar", "activation", T1[0:1, :], SROW[0:1, 0:GS], AF.Square,
                    inc=("SE", 1))
            em.wait("vector", "SE")
            em.emit("vector", "tensor_sub", T2[0:1, :], SROW[0:1, GS:2 * GS],
                    T1[0:1, :], inc=("VE", 1))
            em.wait("scalar", "VE")
            em.emit("scalar", "activation", T1[0:1, :], T2[0:1, :], AF.Sqrt,
                    bias=CST[0:1, 0:1], inc=("SE", 1))
            em.wait("vector", "SE")
            em.wait("vector", "ST")
            em.emit("vector", "reciprocal", RR[0:1, :], T1[0:1, :], inc=("VE", 1))
            em.wait("vector", "VE")
            em.emit("vector", "tensor_mul", QQ[0:1, :], SROW[0:1, 0:GS], RR[0:1, :],
                    inc=("VE", 1))
            em.wait("sync", "VE")
            em.emit("sync", "dma_start", out=rows_next[0:1, g0:g0 + GS],
                    in_=RR[0:1, :], inc=("ST", 16))
            em.emit("sync", "dma_start", out=rows_next[1:2, g0:g0 + GS],
                    in_=QQ[0:1, :], inc=("ST", 16))

    em.wait("sync", "ST")


def build_C(n_cores=8, dbg=False):
    spec = C_DRAMS(n_cores)
    if dbg:
        spec["kvdbg"] = ([64, 2 * 1056], F32, "out")
    nc, em, ar, dio, pb = _standalone(spec)
    emit_C(nc, em, ar, dio, pb, n_cores=n_cores, dbg=dbg)
    em.finalize(nc)
    return nc


def _kv_sel_mats(n_cores):
    out = []
    for c in range(n_cores):
        m = np.zeros((128, 64), np.float32)
        if c % 2 == 1:
            for i in range(64):
                m[i, i] = 1.0
        out.append(m)
    return out


def _prep_C(inputs, n_cores=8):
    f32 = np.float32
    wqkv = np.ascontiguousarray(np.asarray(inputs["L1_attn_W_qkv"], f32))
    wout = np.ascontiguousarray(np.asarray(inputs["L1_attn_W_out"], f32))
    utm = np.triu(np.ones((128, 128), f32))
    idn = np.eye(128, dtype=f32)
    idlo = np.zeros((128, 64), f32)
    for i in range(64):
        idlo[64 + i, i] = 1.0
    one = np.ones((128, 128), f32)
    sel16 = np.zeros((16, 16 * 64), f32)
    for h in range(16):
        sel16[h, h * 64:(h + 1) * 64] = 1.0
    mkv = _kv_sel_mats(n_cores)
    shared = dict(wqkv=wqkv, wout=wout, utm=utm, idn=idn, idlo=idlo, one=one,
                  sel16=sel16)
    return [dict(mkv=mkv[c], **shared) for c in range(n_cores)]





def emit_D(nc, em, ar, dio, pb, skip_prepass=False):
    x3_d = dio["x3"]; rwt_d = dio["rwt"]; rb_d = dio["rb"]
    w1e_d = dio["w1e"]; b1t_d = dio["b1t"]; w2e_d = dio["w2e"]
    b2_d = dio["b2"]; sel32_d = dio["sel32"]; idn_d = dio["idn"]
    one_d = dio["one"]; x4_d = dio["x4"]; rows_d = dio["rows_d4"]

    sb = ar.alloc
    XT = sb("XT", [128, FC * GS], F32)
    XT2 = sb("XT2", [128, FC * GS], F32)
    XLF = sb("XLF", [128, FC * GS], F32)
    XLN = sb("XLN", [128, FC * GS], BF16)
    HM = sb("HM", [128, 32 * 1024], BF16)
    TTS = sb("TTS", [128, 5 * GS], F32)
    W1E = sb("W1E", [128, 3 * 2048], BF16)
    W2E = sb("W2E", [128, 3 * 2048], BF16)
    B2R = sb("B2R", [32, 1024], BF16)
    B1T = sb("B1T", [128, 64], F32)
    RWT = sb("RWT", [128, FC * 32], F32)
    RB = sb("RB", [32, 1], F32)
    LG = sb("LG", [32, GS], F32)
    LGT = sb("LGT", [128, 4 * 32], F32)
    MX = sb("MX", [128, 4], F32)
    OHT = sb("OHT", [128, 4 * 32], F32)
    OHE = sb("OHE", [32, GS], BF16)
    SEL32 = sb("SEL32", [32, 32 * 128], BF16)
    SCR = sb("SCR", [128, 4 * GS], F32)
    OT = sb("OT", [128, 1024], F32)
    X4B = sb("X4B", [128, 1024], F32)
    IDF = sb("IDF", [128, 128], F32)
    ONEF = sb("ONEF", [128, 128], F32)
    ONEB = sb("ONEB", [128, 128], BF16)
    RR = sb("RR", [1, GS], F32)
    QQ = sb("QQ", [1, GS], F32)
    RR2 = sb("RR2", [1, GS], F32)
    QQ2 = sb("QQ2", [1, GS], F32)
    T1 = sb("T1", [1, GS], F32)
    T2 = sb("T2", [1, GS], F32)
    SROW = sb("SROW", [1, 2 * GS], F32)
    CST = sb("CST", [128, 4], F32)
    XBC = HM[:].bitcast(BF16)[:, 0:FC * GS]
    SQS = HM[:].bitcast(BF16)[:, FC * GS:2 * FC * GS]

    def XTb(p):
        return XT if p == 0 else XT2

    def RRb(p):
        return RR if p == 0 else RR2

    def QQb(p):
        return QQ if p == 0 else QQ2

    def _load_tile(t_):
        p = t_ % 2
        g0_ = t_ * GS
        em.emit("sync", "dma_start", out=RRb(p)[0:1, :],
                in_=rows_d[0:1, g0_:g0_ + GS], inc=("LR%d" % p, 16))
        em.emit("sync", "dma_start", out=QQb(p)[0:1, :],
                in_=rows_d[1:2, g0_:g0_ + GS], inc=("LR%d" % p, 16))
        for f in range(FC):
            em.emit("sync", "dma_start", out=XTb(p)[:, f * GS:(f + 1) * GS],
                    in_=x3_d[f * 128:(f + 1) * 128, g0_:g0_ + GS],
                    inc=("LX%d" % p, 16))

    for dst, src_ in ((B1T, b1t_d), (RWT, None), (RB, rb_d), (IDF, idn_d),
                      (ONEF, one_d)):
        if src_ is not None:
            em.emit("sync", "dma_start", out=dst[:], in_=src_[:], inc=("LDI", 16))
    for ic in range(FC):
        em.emit("sync", "dma_start", out=RWT[:, ic * 32:(ic + 1) * 32],
                in_=rwt_d[ic * 128:(ic + 1) * 128, :], inc=("LDI", 16))
    em.emit("gpsimd", "dma_start", out=ONEB[:], in_=one_d[:], inc=("LWI", 16))
    em.emit("gpsimd", "dma_start", out=SEL32[:], in_=sel32_d[:], inc=("LWI", 16))
    em.emit("gpsimd", "dma_start", out=B2R[:], in_=b2_d[:], inc=("LWI", 16))
    em.emit("vector", "memset", CST[:, 0:1], EPS_LN, inc=("VE", 1))
    for e_ in ("tensor", "vector", "scalar"):
        em.wait(e_, "LDI")
        em.wait(e_, "LWI")

    for g in (() if skip_prepass else range(NG)):
        em.wait("sync", "PE")
        em.wait("sync", "SE")
        for f in range(FC):
            em.emit("sync", "dma_start", out=XT[:, f * GS:(f + 1) * GS],
                    in_=x3_d[f * 128:(f + 1) * 128, g * GS:(g + 1) * GS],
                    inc=("LXP", 16))
        tiles = [XT[:, f * GS:(f + 1) * GS] for f in range(FC)]
        _ln_stats(em, pb, tiles, ONEB, XBC, SQS, SROW, T1, T2,
                  RR[0:1, :], QQ[0:1, :], CST, ld_waits=("LXP",))
        em.wait("sync", "VE")
        em.emit("sync", "dma_start", out=rows_d[0:1, g * GS:(g + 1) * GS],
                in_=RR[0:1, :], inc=("ST", 16))
        em.emit("sync", "dma_start", out=rows_d[1:2, g * GS:(g + 1) * GS],
                in_=QQ[0:1, :], inc=("ST", 16))

    pe_w1 = [0, 0, 0]
    pe_w2 = [0, 0, 0]
    mb_ve = [0, 0]
    dbank_se = {(p, j): 0 for p in range(2) for j in range(2)}
    def _ln_waves(t_):
        p = t_ % 2
        em.wait("tensor", "LR%d" % p)
        em.wait("tensor", "VE")
        em.wait("tensor", "SE")
        em.emit("tensor", "matmul", pb[4][:], ONEF[0:1, :], RRb(p)[0:1, :],
                start=True, stop=True)
        em.emit("tensor", "matmul", pb[5][:], ONEF[0:1, :], QQb(p)[0:1, :],
                start=True, stop=True, inc=("PE", 1))
        em.wait("vector", "PE")
        em.wait("vector", "LX%d" % p)
        for w_ in range(2):
            em.wait("vector", "VE")
            for f4 in range(4):
                f = w_ * 4 + f4
                em.emit("vector", "tensor_mul",
                        TTS[:, (1 + f4) * GS:(2 + f4) * GS],
                        XTb(p)[:, f * GS:(f + 1) * GS], pb[4][:],
                        inc=("VE", 1) if f4 == 3 else None)
            em.wait("vector", "VE")
            for f4 in range(4):
                f = w_ * 4 + f4
                em.emit("vector", "tensor_sub", XLF[:, f * GS:(f + 1) * GS],
                        TTS[:, (1 + f4) * GS:(2 + f4) * GS], pb[5][:],
                        inc=("VE", 1) if f4 == 3 else None)
        em.wait("vector", "VE")
        for f in range(FC):
            em.emit("vector", "tensor_copy", XLN[:, f * GS:(f + 1) * GS],
                    XLF[:, f * GS:(f + 1) * GS],
                    inc=("VE", 1) if f == FC - 1 else None)

    em.wait("sync", "PE")
    em.wait("sync", "VE")
    em.wait("sync", "ST")
    _load_tile(0)
    _ln_waves(0)
    for t in range(NG):
        par = t % 2
        lx = "LX%d" % par
        g0 = t * GS
        if t + 1 < NG:
            em.wait("sync", "PE")
            em.wait("sync", "VE")
            em.wait("sync", "ST")
            _load_tile(t + 1)
        em.wait("tensor", "VE")
        em.wait("tensor", "LDI")
        for ic in range(FC):
            em.emit("tensor", "matmul", pb[0][0:32, 0:GS],
                    RWT[:, ic * 32:(ic + 1) * 32], XLF[:, ic * GS:(ic + 1) * GS],
                    start=(ic == 0), stop=(ic == FC - 1),
                    inc=("PE", 1) if ic == FC - 1 else None)
        em.wait("vector", "PE")
        em.emit("vector", "tensor_scalar", LG[:], pb[0][0:32, 0:GS], RB[:],
                None, ALU.add, inc=("VE", 1))
        for q in range(4):
            em.wait("tensor", "VE")
            em.emit("tensor", "transpose", pb[1][0:128, 0:32],
                    LG[:, q * 128:(q + 1) * 128], IDF[0:32, 0:32], inc=("PE", 1))
            em.wait("vector", "PE")
            em.emit("vector", "tensor_copy", LGT[:, q * 32:(q + 1) * 32],
                    pb[1][0:128, 0:32], inc=("VE", 1))
        em.wait("vector", "VE")
        em.emit("vector", "tensor_reduce", MX[:],
                LGT[:].rearrange("p (q e) -> p q e", e=32), AX, ALU.max,
                inc=("VE", 1))
        em.wait("vector", "VE")
        for q in range(4):
            em.emit("vector", "tensor_scalar", OHT[:, q * 32:(q + 1) * 32],
                    LGT[:, q * 32:(q + 1) * 32], MX[:, q:q + 1], None, ALU.is_ge,
                    inc=("VE", 1) if q == 3 else None)
        for q in range(4):
            em.wait("tensor", "VE")
            em.emit("tensor", "transpose", pb[1][0:32, 0:128],
                    OHT[:, q * 32:(q + 1) * 32], IDF[:], inc=("PE", 1))
            em.wait("vector", "PE")
            em.emit("vector", "tensor_copy", OHE[:, q * 128:(q + 1) * 128],
                    pb[1][0:32, 0:128], inc=("VE", 1))
        ve_ohe = em.cnt["VE"]
        for k_ in dbank_se:
            dbank_se[k_] = max(dbank_se[k_], em.cnt["SE"])
        mb_ve = [max(mb_ve[0], em.cnt["VE"]), max(mb_ve[1], em.cnt["VE"])]
        for e in range(32):
            ep = e % 2
            wp = e % 3
            lw = "LW%d" % wp
            em.wait("gpsimd", "PE", pe_w1[wp])
            em.emit("gpsimd", "dma_start",
                    out=W1E[:, wp * 2048:(wp + 1) * 2048],
                    in_=w1e_d[e * 128:(e + 1) * 128, :], inc=(lw, 16))
            em.wait("tensor", lw)
            em.wait("tensor", "LWI")
            em.wait("tensor", "VE", max(ve_ohe, mb_ve[ep]))
            em.emit("tensor", "matmul", pb[4 + ep][:],
                    SEL32[:, e * 128:(e + 1) * 128], OHE[:],
                    start=True, stop=True)
            for jc in range(2):
                bh = 2 * ep + jc
                em.wait("tensor", "SE", dbank_se[(ep, jc)])
                for ic in range(FC):
                    em.emit("tensor", "matmul", pb[bh][:],
                            W1E[:, wp * 2048 + ic * 256 + jc * 128:
                                wp * 2048 + ic * 256 + (jc + 1) * 128],
                            XLN[:, ic * GS:(ic + 1) * GS],
                            start=(ic == 0), stop=(ic == FC - 1),
                            inc=("PE", 1) if ic == FC - 1 else None)
                sc_sl = (2 * ep + jc) * GS
                em.wait("scalar", "PE")
                em.emit("scalar", "activation", SCR[:, sc_sl:sc_sl + GS],
                        pb[bh][:], AF.Relu,
                        bias=B1T[:, jc * 32 + e:jc * 32 + e + 1], inc=("SE", 1))
                em.wait("vector", "SE")
                em.emit("vector", "tensor_mul",
                        HM[:, e * 1024 + jc * GS: e * 1024 + (jc + 1) * GS],
                        SCR[:, sc_sl:sc_sl + GS], pb[4 + ep][:], inc=("VE", 1))
                dbank_se[(ep, jc)] = em.cnt["SE"]
            mb_ve[ep] = em.cnt["VE"]
            pe_w1[wp] = em.cnt["PE"]
        for e in range(32):
            ep = e % 2
            wp = e % 3
            lv = "LV%d" % wp
            em.wait("gpsimd", "PE", pe_w2[wp])
            em.emit("gpsimd", "dma_start",
                    out=W2E[:, wp * 2048:(wp + 1) * 2048],
                    in_=w2e_d[e * 128:(e + 1) * 128, :], inc=(lv, 16))
            em.wait("tensor", lv)
            if e == 0:
                em.wait("tensor", "VE")
                em.wait("tensor", "SE")
            for q in range(4):
                for jc in range(2):
                    for half in range(2):
                        em.emit("tensor", "matmul", pb[2 * q + half][:],
                                HM[:, e * 1024 + jc * GS + q * 128:
                                   e * 1024 + jc * GS + (q + 1) * 128],
                                W2E[:, wp * 2048 + jc * 1024 + half * GS:
                                    wp * 2048 + jc * 1024 + (half + 1) * GS],
                                start=(e == 0 and jc == 0), stop=False,
                                inc=("PE", 1) if (q == 3 and half == 1
                                                  and jc == 1) else None)
            pe_w2[wp] = em.cnt["PE"]
        em.wait("tensor", "LWI")
        for q in range(4):
            for half in range(2):
                bo = 2 * q + half
                em.emit("tensor", "matmul", pb[bo][:],
                        OHE[:, q * 128:(q + 1) * 128],
                        B2R[:, half * GS:(half + 1) * GS],
                        start=False, stop=True,
                        inc=("PE", 1) if (q == 3 and half == 1) else None)
        for q in range(4):
            em.wait("scalar", "PE")
            em.wait("scalar", "ST")
            em.emit("scalar", "activation", OT[:, 0:GS], pb[2 * q][:], AF.Relu)
            em.emit("scalar", "activation", OT[:, GS:2 * GS], pb[2 * q + 1][:],
                    AF.Relu, inc=("SE", 1))
            tr_mk = [0, 0]
            for f in range(FC):
                fb = 2 * q + f % 2
                em.wait("tensor", "SE")
                em.wait("tensor", "VE", tr_mk[f % 2])
                em.emit("tensor", "transpose", pb[fb][0:128, 0:128],
                        XTb(par)[:, f * GS + q * 128: f * GS + (q + 1) * 128],
                        IDF[:], inc=("PE", 1))
                em.wait("vector", "PE")
                em.wait("vector", "ST")
                em.emit("vector", "tensor_add", X4B[:, f * 128:(f + 1) * 128],
                        OT[:, f * 128:(f + 1) * 128], pb[fb][0:128, 0:128],
                        inc=("VE", 1))
                tr_mk[f % 2] = em.cnt["VE"]
            em.wait("sync", "VE")
            em.emit("sync", "dma_start",
                    out=x4_d[g0 + q * 128: g0 + (q + 1) * 128, :], in_=X4B[:],
                    inc=("ST", 16))
            if q == 2 and t + 1 < NG:
                _ln_waves(t + 1)

    em.wait("sync", "ST")


def build_D():
    nc, em, ar, dio, pb = _standalone(D_DRAMS())
    emit_D(nc, em, ar, dio, pb)
    em.finalize(nc)
    return nc


def _prep_D(inputs):
    f32 = np.float32
    rwt = np.ascontiguousarray(np.asarray(inputs["L1_ffn_router_w"], f32).T)
    rb = np.ascontiguousarray(np.asarray(inputs["L1_ffn_router_b"], f32).reshape(32, 1))
    w1e = np.ascontiguousarray(
        np.asarray(inputs["L1_ffn_weights1"], f32).reshape(32, 8, 128, 256)
        .transpose(0, 2, 1, 3).reshape(32 * 128, 2048))
    b1t = np.ascontiguousarray(
        np.asarray(inputs["L1_ffn_bias1"], f32).T.reshape(2, 128, 32)
        .transpose(1, 0, 2).reshape(128, 64))
    w2e = np.ascontiguousarray(
        np.asarray(inputs["L1_ffn_weights2"], f32).reshape(32, 2, 128, 1024)
        .transpose(0, 2, 1, 3).reshape(32 * 128, 2048))
    b2 = np.ascontiguousarray(np.asarray(inputs["L1_ffn_bias2"], f32))
    sel32 = np.zeros((32, 32 * 128), f32)
    for e in range(32):
        sel32[e, e * 128:(e + 1) * 128] = 1.0
    idn = np.eye(128, dtype=f32)
    one = np.ones((128, 128), f32)
    return dict(rwt=rwt, rb=rb, w1e=w1e, b1t=b1t, w2e=w2e, b2=b2,
                sel32=sel32, idn=idn, one=one)


def _phase_barrier(em):
    for eng in ENGINES:
        for s in em.cnt:
            em.wait(eng, s)


def build_fused(n_cores=8):
    nc = bass.Bass("TRN2", target_bir_lowering=False, debug=False)
    spec = {}
    spec.update(A_DRAMS(n_cores))
    spec.update(B_DRAMS())
    spec.update(C_DRAMS(n_cores))
    spec.update(D_DRAMS())
    for k in ("x1", "x2", "x3"):
        shape, dt_, _ = spec[k]
        spec[k] = (shape, dt_, "int")
    dio = _mk_drams(nc, spec)
    ar = Ar(nc)
    pb = [nc.alloc_psum_tensor(f"pb{i}", [128, 512], F32) for i in range(8)]
    em = Em()
    for s in ALL_SEMS:
        em.sem(s)
    emit_A(nc, em, ar, dio, pb, n_cores=n_cores, rows_next=dio["rows_b"])
    _phase_barrier(em)
    ar.reset()
    emit_B(nc, em, ar, dio, pb, skip_prepass=True, rows_next=dio["rows_c"])
    _phase_barrier(em)
    ar.reset()
    emit_C(nc, em, ar, dio, pb, n_cores=n_cores, skip_prepass=True,
           rows_next=dio["rows_d4"])
    _phase_barrier(em)
    ar.reset()
    emit_D(nc, em, ar, dio, pb, skip_prepass=True)
    em.finalize(nc)
    return nc


def _prep_fused(inputs, n_cores=8):
    maps_a = _prep_A(inputs, n_cores=n_cores)
    shared_b = _prep_B(inputs)
    maps_c = _prep_C(inputs, n_cores=n_cores)
    shared_d = _prep_D(inputs)
    maps = []
    for c in range(n_cores):
        m = {}
        m.update(maps_a[c])
        m.update(shared_b)
        m.update(maps_c[c])
        m.update(shared_d)
        m.pop("x2", None)
        m.pop("x3", None)
        maps.append(m)
    return maps


_NCORES = 8
_FUSED = True


def kernel_fused(**inputs):
    n = _NCORES
    maps = _prep_fused(inputs, n_cores=n)
    nc = build_fused(n_cores=n)
    res = run_bass_kernel_spmd(nc, maps, core_ids=list(range(n)))
    out = np.empty((4, 8192, 1024), np.float32)
    for c in range(n):
        b, hh = c // 2, c % 2
        out[b, hh * TOK:(hh + 1) * TOK, :] = res.results[c]["x4"]
    return out


def kernel(**inputs):
    if _FUSED:
        return kernel_fused(**inputs)
    return kernel_unfused(**inputs)


def kernel_unfused(**inputs):
    n = _NCORES
    maps_a = _prep_A(inputs, n_cores=n)
    shared_b = _prep_B(inputs)
    maps_c = _prep_C(inputs, n_cores=n)
    shared_d = _prep_D(inputs)

    nc_a = build_A(n_cores=n)
    res_a = run_bass_kernel_spmd(nc_a, maps_a, core_ids=list(range(n)))
    x1s = [r["x1"] for r in res_a.results]

    nc_b = build_B()
    res_b = run_bass_kernel_spmd(
        nc_b, [dict(x1=x1s[c], **shared_b) for c in range(n)],
        core_ids=list(range(n)))
    x2s = [r["x2"] for r in res_b.results]

    nc_c = build_C(n_cores=n)
    for c in range(n):
        maps_c[c]["x2"] = x2s[c]
    res_c = run_bass_kernel_spmd(nc_c, maps_c, core_ids=list(range(n)))
    x3s = [r["x3"] for r in res_c.results]

    nc_d = build_D()
    res_d = run_bass_kernel_spmd(
        nc_d, [dict(x3=x3s[c], **shared_d) for c in range(n)],
        core_ids=list(range(n)))

    out = np.empty((4, 8192, 1024), np.float32)
    for c in range(n):
        b, hh = c // 2, c % 2
        out[b, hh * TOK:(hh + 1) * TOK, :] = res_d.results[c]["x4"]
    return out
